# version 16
# baseline (speedup 1.0000x reference)
import contextlib

import numpy as np
import ml_dtypes

import concourse.bass as bass
import concourse.mybir as mybir
from concourse.bass_utils import run_bass_kernel_spmd

F32 = mybir.dt.float32
BF16 = mybir.dt.bfloat16
AF = mybir.ActivationFunctionType
ALU = mybir.AluOpType
AX = mybir.AxisListType
BFNP = ml_dtypes.bfloat16

D = 2048
T = 2048
HYP = 512
NHYP = T // HYP
H = 16
KD = 64
HD = 128
C = 128
TC = HYP // C
PC = D // 128
QPC = (H * KD) // 128
FF = 5632
FC = FF // 128
FH = FF // 2
KSCALE = float(HD) ** -0.5
NFF = FF * D
EPS = 1e-5
ENGS = ("pe", "act", "dve", "gp", "sync")


class Prog:

    QR = {"qs": 10, "qg": 6, "qt": 8}

    def __init__(self):
        self.steps = []
        self.seq = dict.fromkeys(ENGS, 0)
        self.dseq = {"qs": 0, "qg": 0, "qt": 0}
        self.res = {}

    def _r(self, r):
        return self.res.setdefault(r, {"w": None, "rs": {}})

    def step(self, eng, emit, reads=(), writes=(), dma=None, tag=""):
        deps = []
        for r in reads:
            e = self._r(r)["w"]
            if e is not None:
                deps.append(e)
        for r in writes:
            d = self._r(r)
            if d["w"] is not None:
                deps.append(d["w"])
            deps.extend(d["rs"].items())
        if dma is not None:
            i = self.dseq[dma]
            self.dseq[dma] += 1
            R = self.QR[dma]
            sem = f"{dma}{i % R}"
            ev = (sem, 16 * (i // R + 1))
            if i // R > 0:
                deps.append((sem, 16 * (i // R)))
        else:
            self.seq[eng] += 1
            ev = (eng, self.seq[eng])
        self.steps.append(dict(eng=eng, emit=emit, deps=deps, dma=dma, ev=ev, tag=tag,
                               preads=tuple(reads)))
        for r in reads:
            rs = self._r(r)["rs"]
            if rs.get(ev[0], 0) < ev[1]:
                rs[ev[0]] = ev[1]
        for r in writes:
            d = self._r(r)
            d["w"] = ev
            d["rs"] = {ev[0]: ev[1]}
        return ev

    STRICT = ("dve", "act", "gp")

    def resolve(self, serial=False):
        if serial:
            prev = None
            for st in self.steps:
                if prev is not None:
                    st["deps"] = list(st["deps"]) + [prev]
                prev = st["ev"]
        loose = {}
        for st in self.steps:
            if st["dma"] is None and st["eng"] in self.STRICT:
                loose[st["ev"]] = True
        post_vc = {}
        vc = {e: {} for e in ENGS}
        for st in self.steps:
            eng = st["eng"]
            strict = eng in self.STRICT
            mv = vc[eng]
            waits = []
            for (s, v) in st["deps"]:
                if s == eng and st["dma"] is None and not loose.get((s, v), False):
                    continue
                if mv.get(s, 0) >= v:
                    continue
                waits.append((s, v))
                src = post_vc.get((s, v))
                if src:
                    for k2, v2 in src.items():
                        if mv.get(k2, 0) < v2:
                            mv[k2] = v2
                if mv.get(s, 0) < v:
                    mv[s] = v
            st["waits"] = waits
            s0, v0 = st["ev"]
            if st["dma"] is not None:
                pv = dict(mv)
                pv[s0] = max(pv.get(s0, 0), v0)
                post_vc[st["ev"]] = pv
            else:
                if mv.get(s0, 0) < v0 and not loose.get(st["ev"], False):
                    mv[s0] = v0
                pv = dict(mv)
                pv[s0] = max(pv.get(s0, 0), v0)
                post_vc[st["ev"]] = pv

    def stream(self, name, sems):
        mysteps = [st for st in self.steps if st["eng"] == name]
        dq = dict(self.dseq)

        def f(eng):
            for st in mysteps:
                for (s, v) in st["waits"]:
                    eng.wait_ge(sems[s], v)
                try:
                    inst = st["emit"](eng)
                except Exception as ex:
                    raise RuntimeError(f"step {st['tag']} failed: {ex}") from ex
                s0, _ = st["ev"]
                inst.then_inc(sems[s0], 16 if st["dma"] is not None else 1)
            if name == "sync":
                for q in ("qs", "qt"):
                    R = self.QR[q]
                    for j in range(min(R, dq[q])):
                        n = dq[q] // R + (1 if j < dq[q] % R else 0)
                        eng.wait_ge(sems[f"{q}{j}"], 16 * n)
            if name == "gp":
                q, R = "qg", self.QR["qg"]
                for j in range(min(R, dq[q])):
                    n = dq[q] // R + (1 if j < dq[q] % R else 0)
                    eng.wait_ge(sems[f"{q}{j}"], 16 * n)
        return f


def colap(t, start, step, count, pcount=128, pbase=0):
    fsz = int(np.prod(t.shape[1:]))
    return bass.AP(t, pbase * fsz + start, [[fsz, pcount], [step, count]])


def ap3(t, f0, dims, pcount=128, pbase=0):
    fsz = int(np.prod(t.shape[1:]))
    return bass.AP(t, pbase * fsz + f0, [[fsz, pcount]] + [list(d) for d in dims])


def build(debug_taps=(), n_pre=NHYP, n_main=NHYP, do_p2=True, do_gamma=True, stage="all", serial=False):
    taps = set(debug_taps)
    nc = bass.Bass("TRN2", target_bir_lowering=False, debug=False)

    xm = nc.dram_tensor("xm", [D, T], F32, kind="ExternalInput")
    xp = nc.dram_tensor("xp", [D, T], F32, kind="ExternalInput")
    Wq = nc.dram_tensor("Wq", [8, 128, D], F32, kind="ExternalInput")
    Wk = nc.dram_tensor("Wk", [8, 128, D], F32, kind="ExternalInput")
    Wkg1 = nc.dram_tensor("Wkg1", [D, 16], F32, kind="ExternalInput")
    bkg2b_d = nc.dram_tensor("bkg2b", [128, 1024], F32, kind="ExternalInput")
    wkg2p_d = nc.dram_tensor("Wkg2p", [128, 1024], F32, kind="ExternalInput")
    Wv = nc.dram_tensor("Wv", [4, 128, 8192], F32, kind="ExternalInput")
    Wg = nc.dram_tensor("Wg", [16, 128, D], F32, kind="ExternalInput")
    bgc_d = nc.dram_tensor("bgc", [128, PC], F32, kind="ExternalInput")
    Wo = nc.dram_tensor("Wo", [16, 128, D], F32, kind="ExternalInput")
    ln1_d = nc.dram_tensor("ln1c", [128, PC], F32, kind="ExternalInput")
    ln2_d = nc.dram_tensor("ln2c", [128, PC], F32, kind="ExternalInput")
    W1 = nc.dram_tensor("W1", [FC, 128, D], F32, kind="ExternalInput")
    W3 = nc.dram_tensor("W3", [FC, 128, D], F32, kind="ExternalInput")
    W2 = nc.dram_tensor("W2", [16, 128, FF], F32, kind="ExternalInput")
    fw1T = nc.dram_tensor("fw1T", [FC, 128, D], F32, kind="ExternalInput")
    fw2T = nc.dram_tensor("fw2T", [FC, 128, D], F32, kind="ExternalInput")
    fw3T = nc.dram_tensor("fw3T", [16, 128, FF], F32, kind="ExternalInput")
    out = nc.dram_tensor("out", [D, T], F32, kind="ExternalOutput")

    wqc = nc.dram_tensor("wqc", [8, 128, 2048], BF16, kind="Internal")
    wkc = nc.dram_tensor("wkc", [8, 128, 2048], BF16, kind="Internal")
    wvc = nc.dram_tensor("wvc", [4, 128, 8192], BF16, kind="Internal")
    wgc = nc.dram_tensor("wgc", [16, 128, 2048], BF16, kind="Internal")
    woc = nc.dram_tensor("woc", [16, 128, 2048], BF16, kind="Internal")
    w1c = nc.dram_tensor("w1c", [FC, 128, 2048], BF16, kind="Internal")
    w3c = nc.dram_tensor("w3c", [FC, 128, 2048], BF16, kind="Internal")
    w2c = nc.dram_tensor("w2c", [PC, 128, FF], BF16, kind="Internal")
    qw1s = nc.dram_tensor("qw1s", [FC, 128, 2048], BF16, kind="Internal")
    qw2s = nc.dram_tensor("qw2s", [FC, 128, 2048], BF16, kind="Internal")
    qw3s = nc.dram_tensor("qw3s", [PC, 128, FF], BF16, kind="Internal")

    rows4 = np.zeros((128, 128), np.float32)
    for p in (0, 32, 64, 96):
        rows4[p, :] = 1.0
    colsc = np.concatenate([np.ones((128, 1), np.float32),
                            np.full((128, 1), 1 / 128, np.float32),
                            np.full((128, 1), 1 / 2048, np.float32)], 1)
    identc = nc.inline_tensor(np.eye(128, dtype=BFNP), "identc")
    maskutc = nc.inline_tensor(np.triu(np.ones((128, 128), np.float32)).astype(BFNP), "maskutc")
    triu16c = nc.inline_tensor((np.triu(np.ones((128, 128), np.float32)) / 16.0).astype(np.float32), "triu16c")
    colsfc = nc.inline_tensor(colsc, "colsfc")
    colsbc = nc.inline_tensor(colsc.astype(BFNP), "colsbc")
    rows4fc = nc.inline_tensor(rows4, "rows4fc")
    actcc = nc.inline_tensor(np.concatenate([np.full((128, 1), EPS, np.float32),
                                             np.ones((128, 1), np.float32)], 1), "actcc")
    rows4bc = nc.inline_tensor(rows4.astype(BFNP), "rows4bc")

    AT = nc.alloc_sbuf_tensor
    arA = AT("arA", [128, PC * HYP], BF16)
    arB = AT("arB", [128, 2 * QPC * HYP], BF16)
    arC = AT("arC", [128, TC * D], BF16)
    arD = AT("arD", [128, PC * HYP], BF16)
    arE = AT("arE", [128, 10240], BF16)
    arF = AT("arF", [128, 6144], F32)
    gbA = AT("gbA", [128, FF], F32)
    s2b = AT("s2b", [128, PC * HYP], BF16)
    wsm = [AT(f"wsm{i}", [128, 2048], BF16) for i in range(4)]

    id_sb = AT("id_sb", [128, 128], BF16)
    mut_sb = AT("mut_sb", [128, 128], BF16)
    tri_sb = AT("tri_sb", [128, 128], F32)
    colb = AT("colb", [128, 3], BF16)
    colf = AT("colf", [128, 3], F32)
    r4f = AT("r4f", [128, 128], F32)
    r4b = AT("r4b", [128, 128], BF16)
    ln1_sb = AT("ln1_sb", [128, PC], F32)
    ln2_sb = AT("ln2_sb", [128, PC], F32)
    bg_sb = AT("bg_sb", [128, PC], F32)
    bk2b = AT("bk2b", [128, 1024], BF16)
    wkg2p_sb = AT("wkg2p_sb", [128, 1024], BF16)
    wkg1_sb = AT("wkg1_sb", [128, 256], BF16)
    thr_sb = AT("thr_sb", [128, 8], F32)
    gacc = AT("gacc", [128, 4], F32)
    gtmp = AT("gtmp", [128, 1], F32)
    acc_sb = AT("acc_sb", [128, 2], F32)
    z1b = AT("z1b", [16, HYP], BF16)
    S_sb = AT("S_sb", [128, 1024], F32)
    S_bf = AT("S_bf", [128, 1024], BF16)
    xsq = [AT(f"xsq{i}", [128, HYP], BF16) for i in range(2)]
    rrow = AT("rrow", [1, 512], F32)
    rstdb = AT("rstdb", [128, HYP], BF16)
    mubf = AT("mubf", [128, HYP], BF16)
    P_sb = AT("P_sb", [128, 2048], BF16)
    o_sb = AT("o_sb", [128, 2048], BF16)
    amid = AT("amid", [128, QPC], F32)
    namid = AT("namid", [128, QPC], F32)
    eacm = AT("eacm", [128, QPC], F32)
    eamid = AT("eamid", [128, QPC], F32)
    dcol = AT("dcol", [128, QPC], F32)
    scrb = AT("scrb", [128, 2048], BF16)
    scrf = AT("scrf", [128, 1024], F32)
    s1c = [AT(f"s1c{i}", [128, HYP], BF16) for i in range(2)]
    sgt = [AT(f"sgt{i}", [128, HYP], BF16) for i in range(2)]

    PS = nc.alloc_psum_tensor
    pb = [PS(f"pb{i}", [128, 512], F32) for i in range(7)]
    pT = PS("pT", [128, 1024], BF16)

    ctx = contextlib.ExitStack()
    semnames = ["pe", "act", "dve", "gp"]
    for q, R in Prog.QR.items():
        semnames += [f"{q}{j}" for j in range(R)]
    sems = {s: ctx.enter_context(nc.semaphore(f"sem_{s}")) for s in semnames}
    P = Prog()

    def dma(q, outap, inap, reads, writes, tag=""):
        eng = "sync" if q == "qs" else "gp"
        P.step(eng, lambda e, o=outap, i=inap: e.dma_start(out=o, in_=i),
               reads=reads, writes=writes, dma=q, tag=tag)

    def dmaT(outap, inap, reads, writes, tag=""):
        P.step("sync", lambda e, o=outap, i=inap: e.dma_start_transpose(o, i),
               reads=reads, writes=writes, dma="qt", tag=tag)

    def mm(outap, lhs, rhs, start, stop, reads, writes, tag=""):
        P.step("pe", lambda e, o=outap, a=lhs, b=rhs, s1=start, s2=stop:
               e.matmul(o, a, b, start=s1, stop=s2),
               reads=reads, writes=writes, tag=tag)

    def tr(outap, inap, idap, reads, writes, tag=""):
        P.step("pe", lambda e, o=outap, i=inap, d=idap: e.transpose(o, i, d),
               reads=reads, writes=writes, tag=tag)

    def act(fn, outap, inap, reads, writes, bias=0.0, scale=1.0, tag=""):
        P.step("act", lambda e, o=outap, i=inap, b=bias, s=scale, f=fn:
               e.activation(o, i, f, bias=b, scale=s),
               reads=reads, writes=writes, tag=tag)

    def ew(eng, name, args, reads, writes, tag=""):
        P.step(eng, lambda e, n=name, a=args: getattr(e, n)(*a),
               reads=reads, writes=writes, tag=tag)

    def tap(name, srcap, shape, reads, dtype=BF16):
        if name not in taps:
            return
        dt_ = nc.dram_tensor(f"tap_{name}", list(shape), dtype, kind="ExternalOutput")
        dma("qs", dt_[tuple(slice(0, s) for s in shape)], srcap, reads=reads,
            writes=[f"tap_{name}"], tag=f"tap_{name}")

    def d3(ap, n):
        return ap.rearrange("p (dk n) -> p dk n", n=n)

    def stream_w(first, srcf32, cache_ap, dest, destres, tag, extra_writes=(), cres=None):
        c = cres or f"c{tag}"
        if first:
            dma("qg", dest, srcf32, [], [destres, *extra_writes], f"{tag}ld")
            dma("qs", cache_ap, dest, [destres], [c], f"{tag}cw")
        else:
            dma("qs", dest, cache_ap, [c], [destres, *extra_writes], f"{tag}cr")

    gamma_units = []

    def make_gamma_units():
        ui = [0]
        for (wt, nblk, cw, wi, nm) in ((fw1T, FC, D, 0, "g1"), (fw2T, FC, D, 1, "g2"),
                                       (fw3T, 16, FF, 2, "g3")):
            for r in range(nblk):
                def unit(r=r, wt=wt, cw=cw, wi=wi, nm=nm):
                    sl = ui[0] % 2
                    ui[0] += 1
                    b = gbA[:, sl * FH:sl * FH + cw] if cw <= FH else gbA[:, 0:cw]
                    bn = f"gbA{sl}" if cw <= FH else "gbA01"
                    xw = [] if cw <= FH else ["gbA0", "gbA1"]
                    dma("qs", b, wt[r, :, 0:cw], [], [bn, *xw], f"{nm}ld")
                    ew("dve", "tensor_reduce", (gtmp[:, 0:1], b, AX.X, ALU.add, True),
                       [bn], ["gtmp"], f"{nm}red")
                    ew("dve", "tensor_add", (gacc[:, wi:wi + 1], gacc[:, wi:wi + 1], gtmp[:, 0:1]),
                       ["gtmp", "gacc"], ["gacc"], f"{nm}acc")
                gamma_units.append(unit)

    def drip_gamma(n=1):
        for _ in range(n):
            if gamma_units:
                gamma_units.pop(0)()

    def gamma_finish():
        while gamma_units:
            drip_gamma()
        tap("gacc", gacc[:, 0:4], (128, 4), ["gacc"], F32)
        mm(pb[6][0:1, 0:3], colf[:, 0:1], gacc[:, 0:3], True, True,
           ["colf", "gacc"], ["pb6"], "gsum")
        ew("dve", "tensor_copy", (scrf[0:1, 0:3], pb[6][0:1, 0:3]), ["pb6"], ["scrf"])
        ew("dve", "tensor_scalar",
           (scrf[0:1, 4:7], scrf[0:1, 0:3], 0.5 / NFF, 0.5e-8, ALU.mult, ALU.add),
           ["scrf"], ["scrf"], "gthr")
        tap("scrf", scrf[0:1, 0:8], (1, 8), ["scrf"], F32)
        for wi in range(3):
            mm(pb[6][:, 0:1], r4f[0:1, :], scrf[0:1, 4 + wi:5 + wi], True, True,
               ["r4f", "scrf"], ["pb6"])
            ew("dve", "tensor_copy", (thr_sb[:, wi:wi + 1], pb[6][:, 0:1]),
               ["pb6"], ["thr_sb"])
        ew("dve", "tensor_scalar", (thr_sb[:, 4:7], thr_sb[:, 0:3], -1.0, None, ALU.mult),
           ["thr_sb"], ["thr_sb"])
        tap("thr", thr_sb[:, 0:8], (128, 8), ["thr_sb"], F32)

    qspecs = []
    for hc in range(FC):
        for (wt, cache, wi, cp) in ((fw1T, qw1s, 0, "c1"), (fw2T, qw2s, 1, "c2")):
            for pc2 in range(2):
                qspecs.append((wt[hc, :, pc2 * 1024:(pc2 + 1) * 1024],
                               cache[hc, :, pc2 * 1024:(pc2 + 1) * 1024],
                               wi, 1024, f"{cp}_{hc}_{pc2}"))
    for oc in range(PC):
        for q4 in range(4):
            qspecs.append((fw3T[oc, :, q4 * 1408:(q4 + 1) * 1408],
                           qw3s[oc, :, q4 * 1408:(q4 + 1) * 1408],
                           2, 1408, f"c3q_{oc}_{q4}"))
    NQ = len(qspecs)
    qpt = [NQ + 2]

    def s2names(a, b):
        return [f"s2_{k}" for k in range(a // HYP, -(-b // HYP))]

    def qslot(i, w):
        j = i % 4
        return s2b[:, j * 1408:j * 1408 + w], \
            [f"s2q{j}"] + s2names(j * 1408, j * 1408 + w)

    def q_load(i):
        src, dst, wi, w, cres = qspecs[i]
        j = i % 4
        xw = ["gbA0", "gbA1", "gbA01"] if i < 4 else []
        dma("qt", gbA[:, j * 1408:j * 1408 + w], src, [], [f"gq{j}", *xw], f"dqld{i}")

    def q_comp(i):
        src, dst, wi, w, cres = qspecs[i]
        j = i % 4
        g = gbA[:, j * 1408:j * 1408 + w]
        sg, snm = qslot(i + 1, w)
        oap, onm = qslot(i, w)
        act(AF.Sign, sg, g, [f"gq{j}"], snm, tag=f"dqsg{i}")
        act(AF.Abs, g, g, [f"gq{j}"], [f"gq{j}"], tag=f"dqab{i}")
        ew("dve", "scalar_tensor_tensor", (oap, g, thr_sb[:, wi:wi + 1], sg,
                                           ALU.is_ge, ALU.mult),
           [f"gq{j}", "thr_sb", snm[0]], onm, f"dqge{i}")

    def q_write(i):
        src, dst, wi, w, cres = qspecs[i]
        oap, onm = qslot(i, w)
        dma("qs", dst, oap, onm, [cres], f"dqcw{i}")

    def qstep(i):
        if 0 <= i - 2 < NQ:
            q_write(i - 2)
        if i + 2 < NQ:
            q_load(i + 2)
        if i < NQ:
            q_comp(i)

    def make_qunits():
        qpt[0] = 0
        q_load(0)
        q_load(1)

    def drip_q(n=1):
        for _ in range(n):
            if qpt[0] <= NQ + 1:
                i = qpt[0]
                qpt[0] += 1
                qstep(i)

    def flush_q():
        while qpt[0] <= NQ + 1:
            drip_q()

    cspecs = []
    for oc in range(8):
        cspecs.append((Wq[oc, :, :], wqc[oc, :, :], f"cq{oc}"))
    for oc in range(16):
        cspecs.append((Wg[oc, :, :], wgc[oc, :, :], f"cg{oc}"))
    for oc in range(16):
        cspecs.append((Wo[oc, :, :], woc[oc, :, :], f"co{oc}"))
    NC_ = len(cspecs)
    cpt = [NC_]

    def make_cunits():
        cpt[0] = 0

    def drip_c(n=1):
        for _ in range(n):
            if cpt[0] < NC_:
                u = cpt[0]
                cpt[0] += 1
                src, cache, cres = cspecs[u]
                j = u % 4
                slot = s2b[:, j * 2048:(j + 1) * 2048]
                nm = [f"cs{j}"] + s2names(j * 2048, (j + 1) * 2048)
                dma("qg", slot, src, [], nm, f"dc{u}ld")
                dma("qs", cache, slot, nm, [cres], f"dc{u}cw")

    dma("qs", id_sb[:, :], identc[:, :], [], ["id_sb"])
    dma("qs", mut_sb[:, :], maskutc[:, :], [], ["mut_sb"])
    dma("qs", tri_sb[:, :], triu16c[:, :], [], ["tri_sb"])
    dma("qs", colf[:, :], colsfc[:, :], [], ["colf"])
    dma("qs", colb[:, :], colsbc[:, :], [], ["colb"])
    dma("qs", r4f[:, :], rows4fc[:, :], [], ["r4f"])
    dma("qs", acc_sb[:, :], actcc[:, :], [], ["actc"])
    dma("qs", r4b[:, :], rows4bc[:, :], [], ["r4b"])
    dma("qs", ln1_sb[:, :], ln1_d[:, :], [], ["ln1_sb"])
    dma("qs", ln2_sb[:, :], ln2_d[:, :], [], ["ln2_sb"])
    dma("qs", bg_sb[:, :], bgc_d[:, :], [], ["bg_sb"])
    dma("qg", bk2b[:, :], bkg2b_d[:, :], [], ["bk2b"])
    dma("qg", wkg2p_sb[:, :], wkg2p_d[:, :], [], ["wkg2_sb"])
    dma("qg", wkg1_sb[:, :].rearrange("p (dk n) -> p dk n", n=16),
        Wkg1.ap().rearrange("(dk p) n -> p dk n", p=128), [], ["wkg1_sb"])
    ew("dve", "memset", (S_sb[:, :], 0.0), [], ["S_sb"] + [f"S{h}" for h in range(16)])
    ew("dve", "memset", (S_bf[:, :], 0.0), [], [f"Sb{h}" for h in range(16)])
    ew("dve", "memset", (gacc[:, :], 0.0), [], ["gacc"])
    make_gamma_units()

    ZT0, LG0, GK0, A20 = 0, 1024, 2048, 3072
    XFS = 4096
    EV0 = 0
    SQT, SQG, SKT, SKH = 0, 1024, 2048, 3072
    SKHT = 8192
    ETRANS = ["Ekt0", "Ekh0", "Eqt0", "Eqg0", "Ekt1", "Ekh1", "Eqt1", "Eqg1",
              "EkhT0", "EkhT1", "Evw"]

    def rms_xn(xdram, hyp, tag):
        t0 = hyp * HYP
        for pc in range(PC):
            xfo = XFS + (pc % 4) * HYP
            xfn = f"xf{pc % 4}"
            dma("qs", arF[:, xfo:xfo + HYP], xdram[pc * 128:(pc + 1) * 128, t0:t0 + HYP],
                [], [xfn], f"{tag}xld")
            ew("dve", "tensor_copy", (arA[:, pc * HYP:(pc + 1) * HYP], arF[:, xfo:xfo + HYP]),
               [xfn, "x1f0", "x1f1"], [f"A{pc}"], f"{tag}xc")
            act(AF.Square, xsq[pc % 2][:, :], arA[:, pc * HYP:(pc + 1) * HYP],
                [f"A{pc}"], [f"xsq{pc % 2}"], tag=f"{tag}sq")
            mm(pb[4][0:1, 0:512], colb[:, 2:3], xsq[pc % 2][:, :],
               pc == 0, pc == PC - 1, [f"xsq{pc % 2}", "colb"], ["pb4"], f"{tag}s2")
            drip_gamma(1)
            drip_q(1)
            drip_c(1)
        ew("dve", "tensor_copy", (rrow[0:1, :], pb[4][0:1, 0:512]), ["pb4"], ["rrow"], f"{tag}rc")
        act(AF.Ln, rrow[0:1, :], rrow[0:1, :], ["rrow", "actc"], ["rrow"], bias=acc_sb[0:1, 0:1], tag=f"{tag}ln")
        act(AF.Exp, rrow[0:1, :], rrow[0:1, :], ["rrow"], ["rrow"], scale=-0.5, tag=f"{tag}ex")
        mm(pb[6][:, 0:512], r4f[0:1, :], rrow[0:1, :], True, True,
           ["r4f", "rrow"], ["pb6"], f"{tag}bc")
        ew("dve", "tensor_copy", (rstdb[:, :], pb[6][:, 0:512]), ["pb6"], ["rstdb"], f"{tag}bcc")
        for pc in range(PC):
            sl = arA[:, pc * HYP:(pc + 1) * HYP]
            ew("dve", "scalar_tensor_tensor",
               (sl, sl, ln1_sb[:, pc:pc + 1], rstdb[:, :], ALU.mult, ALU.mult),
               [f"A{pc}", "ln1_sb", "rstdb"], [f"A{pc}"], f"{tag}xn")

    def proj_fm(wdram, cache, dout, first, evac, tag, cpfx=None):
        for oc in range(dout // 128):
            ws = wsm[oc % 4]
            wn = f"wsm{oc % 4}"
            stream_w(first, wdram[oc, :, :], cache[oc, :, :], ws[:, :], wn, f"{tag}{oc}",
                     cres=f"{cpfx}{oc}" if cpfx else None)
            for dk in range(PC):
                mm(pb[oc % 4][:, 0:512], ws[:, dk * 128:(dk + 1) * 128],
                   arA[:, dk * HYP:(dk + 1) * HYP],
                   dk == 0, dk == PC - 1, [wn, f"A{dk}"], [f"pb{oc % 4}"], f"{tag}mm")
            evac(pb[oc % 4][:, 0:512], f"pb{oc % 4}", oc)
            drip_gamma(1)
            drip_q(1)
            drip_c(1)

    def proj_vtm(first, tag):
        for js in range(4):
            for hf in range(2):
                dk0 = hf * 8
                stream_w(first, Wv[js, :, hf * 4096:(hf + 1) * 4096],
                         wvc[js, :, hf * 4096:(hf + 1) * 4096],
                         arE[:, EV0 + hf * 4096:EV0 + (hf + 1) * 4096], f"Evw{hf}",
                         f"{tag}{js}_{hf}", extra_writes=ETRANS[:8],
                         cres=f"cv{js}_{hf}")
            for tc in range(TC):
                if tc % 2 == 0:
                    drip_q(1)
                    drip_c(1)
                bk = tc % 4
                for dk in range(PC):
                    mm(pb[bk][:, 0:512],
                       arA[:, dk * HYP + tc * 128: dk * HYP + (tc + 1) * 128],
                       arE[:, EV0 + dk * 512:EV0 + (dk + 1) * 512],
                       dk == 0, dk == PC - 1, [f"Evw{dk // 8}", f"A{dk}"], [f"pb{bk}"], f"{tag}mm")
                ew("act" if tc % 2 else "dve", "copy" if tc % 2 else "tensor_copy",
                   (arC[:, tc * D + js * 512: tc * D + (js + 1) * 512], pb[bk][:, 0:512]),
                   [f"pb{bk}"], [f"V{tc}"], f"{tag}ev")

    def proj_z1(tag):
        for dk in range(PC):
            mm(pb[2][0:16, 0:512], wkg1_sb[:, dk * 16:(dk + 1) * 16],
               arA[:, dk * HYP:(dk + 1) * HYP],
               dk == 0, dk == PC - 1, ["wkg1_sb", f"A{dk}"], ["pb2"], f"{tag}mm")
        ew("dve", "tensor_copy", (z1b[0:16, :], pb[2][0:16, 0:512]), ["pb2"], ["z1b"], f"{tag}ev")

    def scan_prep(tc, full, tag):
        sl = tc % 2
        zt = arF[:, ZT0:ZT0 + 1024]
        lg = arF[:, LG0:LG0 + 1024]
        gk = arF[:, GK0:GK0 + 1024]
        for j in range(2):
            mm(pb[3 + j][:, 0:512], z1b[0:16, tc * 128:(tc + 1) * 128],
               wkg2p_sb[0:16, j * 512:(j + 1) * 512], True, True,
               ["z1b", "wkg2_sb"], [f"pb{3 + j}"], f"{tag}zm")
            ew("dve", "tensor_add", (arF[:, ZT0 + j * 512:ZT0 + (j + 1) * 512],
                                     pb[3 + j][:, 0:512], bk2b[:, j * 512:(j + 1) * 512]),
               [f"pb{3 + j}", "bk2b"], ["Fzt"], f"{tag}ze")
        act(AF.Abs, lg, zt, ["Fzt"], ["Flg"], tag=f"{tag}la")
        act(AF.Exp, lg, lg, ["Flg"], ["Flg"], scale=-1.0, tag=f"{tag}le")
        act(AF.Ln, lg, lg, ["Flg", "actc"], ["Flg"], bias=acc_sb[:, 1:2], tag=f"{tag}ll")
        ew("dve", "tensor_scalar", (gk, zt, 0.0, None, ALU.min), ["Fzt"], ["Fgk"], f"{tag}lm")
        ew("dve", "tensor_sub", (gk, gk, lg), ["Fgk", "Flg"], ["Fgk"], f"{tag}ls")
        for p8 in range(QPC):
            bk = 3 + (p8 % 2)
            mm(pb[bk][:, 0:128], arF[:, GK0 + p8 * 128:GK0 + (p8 + 1) * 128], tri_sb[:, :],
               True, True, ["Fgk", "tri_sb"], [f"pb{bk}"], f"{tag}Am")
            ew("dve", "tensor_copy", (arF[:, A20 + p8 * 128:A20 + (p8 + 1) * 128],
                                      pb[bk][:, 0:128]),
               [f"pb{bk}"], ["FA"], f"{tag}Ae")
        a63 = colap(arF, A20 + 63, 128, QPC)
        a127 = colap(arF, A20 + 127, 128, QPC)
        ew("dve", "tensor_copy", (amid[:, :], a63), ["FA"], ["amid"], f"{tag}am")
        act(AF.Exp, dcol[:, :], a127, ["FA"], ["dcol"], tag=f"{tag}de")
        if full:
            ew("dve", "tensor_scalar", (namid[:, :], amid[:, :], -1.0, None, ALU.mult),
               ["amid"], ["namid"], f"{tag}nm")
        kt0, kh0 = SKT + sl * 4096, SKH + sl * 4096
        qt0, qg0 = SQT + sl * 4096, SQG + sl * 4096
        ktn, khn, qtn, qgn = f"Ekt{sl}", f"Ekh{sl}", f"Eqt{sl}", f"Eqg{sl}"
        for p8 in range(QPC):
            asl = arF[:, A20 + p8 * 128:A20 + (p8 + 1) * 128]
            act(AF.Exp, scrb[:, p8 * 128:(p8 + 1) * 128], asl, ["FA", "amid"], ["scrb"],
                bias=amid[:, p8:p8 + 1], scale=-1.0, tag=f"{tag}Dk")
            act(AF.Exp, P_sb[:, p8 * 128:(p8 + 1) * 128], asl,
                ["FA"], ["Dk2", f"P{p8}"],
                bias=colap(arF, A20 + p8 * 128 + 127, 128, 1), scale=-1.0, tag=f"{tag}D2")
        kap = ap3(arB, 4096 + tc * 128, [(HYP, QPC), (1, 128)])
        dkap = ap3(scrb, 0, [(128, QPC), (1, 128)])
        dk2ap = ap3(P_sb, 0, [(128, QPC), (1, 128)])
        ktap = ap3(arE, kt0, [(128, QPC), (1, 128)])
        khap = ap3(arE, kh0, [(128, QPC), (1, 128)])
        ew("dve", "tensor_mul", (ktap, kap, dkap),
           [f"Bk{p8}" for p8 in range(QPC)] + ["scrb"], [ktn, "Evw0", "Evw1"], f"{tag}kt")
        ew("dve", "tensor_mul", (khap, kap, dk2ap),
           [f"Bk{p8}" for p8 in range(QPC)] + ["Dk2"], [khn, "Evw0", "Evw1"], f"{tag}kh")
        if full:
            for p8 in range(QPC):
                asl = arF[:, A20 + p8 * 128:A20 + (p8 + 1) * 128]
                act(AF.Exp, scrb[:, 1024 + p8 * 128:1024 + (p8 + 1) * 128], asl,
                    ["FA", "namid"], ["sDq"],
                    bias=namid[:, p8:p8 + 1], scale=1.0, tag=f"{tag}Dq")
                act(AF.Exp, o_sb[:, p8 * 128:(p8 + 1) * 128], asl,
                    ["FA"], ["Dg", f"ob{p8 // 4}"], scale=1.0, tag=f"{tag}Dg")
            qap = ap3(arB, tc * 128, [(HYP, QPC), (1, 128)])
            dqap = ap3(scrb, 1024, [(128, QPC), (1, 128)])
            dgap = ap3(o_sb, 0, [(128, QPC), (1, 128)])
            qtap = ap3(arE, qt0, [(128, QPC), (1, 128)])
            qgap = ap3(arE, qg0, [(128, QPC), (1, 128)])
            ew("dve", "tensor_mul", (qtap, qap, dqap),
               [f"Bq{p8}" for p8 in range(QPC)] + ["sDq"], [qtn, "Evw0", "Evw1"], f"{tag}qt")
            ew("dve", "tensor_mul", (qgap, qap, dgap),
               [f"Bq{p8}" for p8 in range(QPC)] + ["Dg"], [qgn, "Evw0", "Evw1"], f"{tag}qg")

    def scan_chunk(tc, full, tag):
        sl = tc % 2
        kt0, kh0 = SKT + sl * 4096, SKH + sl * 4096
        qt0, qg0 = SQT + sl * 4096, SQG + sl * 4096
        kht0 = SKHT + sl * 1024
        ktn, khn, qtn, qgn = f"Ekt{sl}", f"Ekh{sl}", f"Eqt{sl}", f"Eqg{sl}"
        khtn = f"EkhT{sl}"
        for p8 in range(QPC):
            dmaT(arE[:, kht0 + p8 * 128:kht0 + (p8 + 1) * 128],
                 arE[:, kh0 + p8 * 128:kh0 + (p8 + 1) * 128],
                 [khn], [khtn], f"{tag}T")
        if stage == "T1":
            return
        for g in range(4):
            ob = g % 2
            for h in range(4 * g, 4 * g + 4):
                p8, pbs = h // 2, (h % 2) * 64
                if full:
                    mm(pb[5][:, (h % 4) * 128:(h % 4) * 128 + 128],
                       arE[pbs:pbs + 64, kt0 + p8 * 128:kt0 + (p8 + 1) * 128],
                       arE[pbs:pbs + 64, qt0 + p8 * 128:qt0 + (p8 + 1) * 128],
                       True, True, [ktn, qtn], ["pb5"], f"{tag}sc")
                    ew("dve", "tensor_mul", (P_sb[:, h * 128:(h + 1) * 128],
                                             pb[5][:, (h % 4) * 128:(h % 4) * 128 + 128],
                                             mut_sb[:, :]),
                       ["pb5", "mut_sb"], [f"P{h}"], f"{tag}mk")
                    mm(pb[ob][:, (h % 4) * 128:(h % 4) * 128 + 128],
                       arC[:, tc * D + h * HD: tc * D + (h + 1) * HD],
                       P_sb[:, h * 128:(h + 1) * 128],
                       True, False, [f"V{tc}", f"P{h}"], [f"pb{ob}"], f"{tag}ov")
                    mm(pb[ob][:, (h % 4) * 128:(h % 4) * 128 + 128],
                       S_bf[pbs:pbs + 64, (h // 2) * 128:(h // 2 + 1) * 128],
                       arE[pbs:pbs + 64, qg0 + p8 * 128:qg0 + (p8 + 1) * 128],
                       False, True, [f"Sb{h}", qgn], [f"pb{ob}"], f"{tag}oi")
                upb = pb[2][pbs:pbs + 64, ((h // 2) % 4) * 128:((h // 2) % 4) * 128 + 128]
                mm(upb, arE[:, kht0 + h * 64:kht0 + (h + 1) * 64],
                   arC[:, tc * D + h * HD: tc * D + (h + 1) * HD],
                   True, True, [khtn, f"V{tc}"], ["pb2"], f"{tag}U")
                if stage == "U1":
                    continue
                Ssl = S_sb[pbs:pbs + 64, (h // 2) * 128:(h // 2 + 1) * 128]
                ew("dve", "scalar_tensor_tensor",
                   (Ssl, Ssl, dcol[pbs:pbs + 64, p8:p8 + 1], upb, ALU.mult, ALU.add),
                   [f"S{h}", "dcol", "pb2"], [f"S{h}"], f"{tag}Su")
                if stage != "Sup1":
                    ew("dve", "tensor_copy",
                       (S_bf[pbs:pbs + 64, (h // 2) * 128:(h // 2 + 1) * 128], Ssl),
                       [f"S{h}"], [f"Sb{h}"], f"{tag}Sb")
            if full:
                ew("dve", "tensor_copy", (o_sb[:, g * 512:(g + 1) * 512], pb[ob][:, 0:512]),
                   [f"pb{ob}"], [f"ob{g}"], f"{tag}oe")
        if full:
            head_ln(tc, tag)

    def head_ln(tc, tag):
        for j in range(2):
            act(AF.Square, P_sb[:, j * 1024:(j + 1) * 1024], o_sb[:, j * 1024:(j + 1) * 1024],
                [f"ob{2 * j}", f"ob{2 * j + 1}"],
                [f"Q{j}"] + [f"P{h}" for h in range(8 * j, 8 * j + 8)],
                tag=f"{tag}q2")
        for j4 in range(4):
            bk, pp = 3 + j4 // 2, (j4 % 2) * 32
            mm(pb[bk][pp:pp + 1, 0:512], colb[:, 1:2], o_sb[:, j4 * 512:(j4 + 1) * 512],
               True, True, [f"ob{j4}", "colb"], [f"pb{bk}"], f"{tag}mu")
            bk2 = 5 + j4 // 2
            mm(pb[bk2][pp:pp + 1, 0:512], colb[:, 1:2], P_sb[:, j4 * 512:(j4 + 1) * 512],
               True, True, [f"Q{j4 // 2}", "colb"], [f"pb{bk2}"], f"{tag}m2")
        for i, bk in enumerate((3, 4)):
            ew("dve", "tensor_copy", (scrb[0:64, i * 512:(i + 1) * 512], pb[bk][0:64, 0:512]),
               [f"pb{bk}"], ["scrb"], f"{tag}e1")
        for i, bk in enumerate((5, 6)):
            ew("dve", "tensor_copy", (scrb[0:64, 1024 + i * 512:1024 + (i + 1) * 512],
                                      pb[bk][0:64, 0:512]),
               [f"pb{bk}"], ["scrb"], f"{tag}e2")
        for j4 in range(4):
            i, pp = j4 // 2, (j4 % 2) * 32
            lhs = r4b[pp:pp + 1, :]
            mm(pb[3][:, 0:512], lhs, scrb[pp:pp + 1, i * 512:(i + 1) * 512], True, True,
               ["scrb", "r4b"], ["pb3"], f"{tag}bm")
            mm(pb[4][:, 0:512], lhs, scrb[pp:pp + 1, 1024 + i * 512:1024 + (i + 1) * 512],
               True, True, ["scrb", "r4b"], ["pb4"], f"{tag}b2")
            act(AF.Square, sgt[0][:, :], pb[3][:, 0:512], ["pb3"], ["sgt0"], tag=f"{tag}s")
            ew("dve", "tensor_sub", (scrf[:, 0:512], pb[4][:, 0:512], sgt[0][:, :]),
               ["pb4", "sgt0"], ["scrf"], f"{tag}v")
            act(AF.Ln, scrf[:, 0:512], scrf[:, 0:512], ["scrf", "actc"], ["scrf"], bias=acc_sb[:, 0:1], tag=f"{tag}vl")
            act(AF.Exp, sgt[1][:, :], scrf[:, 0:512], ["scrf"], ["sgt1"], scale=-0.5, tag=f"{tag}ve")
            ew("dve", "tensor_sub", (o_sb[:, j4 * 512:(j4 + 1) * 512],
                                     o_sb[:, j4 * 512:(j4 + 1) * 512], pb[3][:, 0:512]),
               [f"ob{j4}", "pb3"], [f"ob{j4}"], f"{tag}t1")
            dsl = ap3(arD, 4 * j4 * HYP + tc * 128, [(HYP, 4), (1, 128)])
            i0 = ap3(o_sb, j4 * 512, [(128, 4), (1, 128)])
            i1 = ap3(sgt[1], 0, [(128, 4), (1, 128)])
            ew("dve", "tensor_mul", (dsl, i0, i1),
               [f"ob{j4}", "sgt1"],
               [f"on{4 * j4}", f"on{4 * j4 + 1}", f"on{4 * j4 + 2}", f"on{4 * j4 + 3}"],
               f"{tag}on")

    def p1_hyper(xdram, hyp, full, first, tag):
        rms_xn(xdram, hyp, f"{tag}r")
        if stage == "rms":
            return
        tap(f"xn_{tag}", arA[:, 0:2048], (128, 2048), ["A0", "A1", "A2", "A3"])
        if full:
            def evq(ps, pn, oc):
                if oc % 2:
                    ew("act", "copy", (arB[:, oc * HYP:(oc + 1) * HYP], ps),
                       [pn], [f"Bq{oc}"], f"{tag}qe")
                else:
                    ew("dve", "tensor_copy", (arB[:, oc * HYP:(oc + 1) * HYP], ps),
                       [pn], [f"Bq{oc}"], f"{tag}qe")
            proj_fm(Wq, wqc, 1024, False, evq, f"{tag}q", cpfx="cq")
        proj_vtm(first and not full, f"{tag}v")

        def evk(ps, pn, oc):
            if oc % 2:
                ew("act", "mul", (arB[:, 4096 + oc * HYP:4096 + (oc + 1) * HYP], ps, KSCALE),
                   [pn], [f"Bk{oc}"], f"{tag}ke")
            else:
                ew("dve", "tensor_scalar",
                   (arB[:, 4096 + oc * HYP:4096 + (oc + 1) * HYP], ps, KSCALE, None, ALU.mult),
                   [pn], [f"Bk{oc}"], f"{tag}ke")
        proj_fm(Wk, wkc, 1024, first and not full, evk, f"{tag}k", cpfx="ck")
        proj_z1(f"{tag}z")
        if stage == "proj":
            return
        ntc = 1 if stage in ("prep1", "chunk1") else TC
        for tc in range(ntc):
            scan_prep(tc, full, f"{tag}p{tc}_")
            if stage == "prep1":
                continue
            scan_chunk(tc, full, f"{tag}s{tc}_")
        if full:
            def evg(ps, pn, oc):
                slg = oc % 2
                act(AF.Silu, sgt[slg][:, :], ps, [pn, "bg_sb"], [f"sgt{slg}"],
                    bias=bg_sb[:, oc:oc + 1], tag=f"{tag}gs")
                dsl = arD[:, oc * HYP:(oc + 1) * HYP]
                ew("dve", "tensor_mul", (dsl, dsl, sgt[slg][:, :]),
                   [f"on{oc}", f"sgt{slg}"], [f"on{oc}"], f"{tag}gm")
            proj_fm(Wg, wgc, 2048, False, evg, f"{tag}g", cpfx="cg")
            tap(f"og_{tag}", arD[:, 0:PC * HYP], (128, PC * HYP), [f"on{i}" for i in range(PC)])

    def layer_norm(srcfn, dstfn, tag, store=None):
        for oc in range(PC):
            sap, sname = srcfn(oc)
            act(AF.Square, xsq[oc % 2][:, :], sap, [sname], [f"xsq{oc % 2}"], tag=f"{tag}q")
            mm(pb[4][0:1, 0:512], colb[:, 2:3], sap, oc == 0, oc == PC - 1,
               [sname, "colb"], ["pb4"], f"{tag}m")
            mm(pb[5][0:1, 0:512], colb[:, 2:3], xsq[oc % 2][:, :], oc == 0, oc == PC - 1,
               [f"xsq{oc % 2}", "colb"], ["pb5"], f"{tag}2")
        ew("dve", "tensor_copy", (scrf[0:1, 0:512], pb[4][0:1, 0:512]), ["pb4"], ["scrf"], f"{tag}c")
        ew("dve", "tensor_copy", (scrf[0:1, 512:1024], pb[5][0:1, 0:512]), ["pb5"], ["scrf"], f"{tag}c2")
        ew("dve", "tensor_mul", (scrb[0:1, 0:512], scrf[0:1, 0:512], scrf[0:1, 0:512]),
           ["scrf"], ["scrb"], f"{tag}s")
        ew("dve", "tensor_sub", (scrf[0:1, 512:1024], scrf[0:1, 512:1024], scrb[0:1, 0:512]),
           ["scrf", "scrb"], ["scrf"], f"{tag}v")
        act(AF.Ln, scrf[0:1, 512:1024], scrf[0:1, 512:1024], ["scrf", "actc"], ["scrf"], bias=acc_sb[0:1, 0:1], tag=f"{tag}l")
        act(AF.Exp, scrf[0:1, 512:1024], scrf[0:1, 512:1024], ["scrf"], ["scrf"], scale=-0.5, tag=f"{tag}e")
        mm(pb[6][:, 0:512], r4f[0:1, :], scrf[0:1, 0:512], True, True,
           ["scrf", "r4f"], ["pb6"], f"{tag}bm")
        ew("dve", "tensor_copy", (mubf[:, :], pb[6][:, 0:512]), ["pb6"], ["mubf"], f"{tag}bmc")
        mm(pb[6][:, 0:512], r4f[0:1, :], scrf[0:1, 512:1024], True, True,
           ["scrf", "r4f"], ["pb6"], f"{tag}br")
        ew("dve", "tensor_copy", (rstdb[:, :], pb[6][:, 0:512]), ["pb6"], ["rstdb"], f"{tag}brc")
        for oc in range(PC):
            sap, sname = srcfn(oc)
            dap, dname, xw = dstfn(oc)
            ew("dve", "tensor_sub", (dap, sap, mubf[:, :]), [sname, "mubf"], [dname, *xw], f"{tag}a1")
            ew("dve", "tensor_mul", (dap, dap, rstdb[:, :]), [dname, "rstdb"], [dname], f"{tag}a2")
            if store is not None:
                (oap,) = store(oc)
                dma("qs", oap, dap, [dname, *xw], [], f"{tag}st")

    def hh_ap(hc, name):
        g0 = hc * HYP
        other = "hh%d" % hc if name == "fh" else "fh%d" % hc
        if hc < 16:
            return arC[:, g0:g0 + HYP], f"{name}{hc}", [f"V{hc // 4}", other]
        if hc < 32:
            return (arD[:, g0 - 16 * HYP:g0 - 16 * HYP + HYP], f"{name}{hc}",
                    [f"on{hc - 16}", other])
        return (arE[:, g0 - 32 * HYP:g0 - 32 * HYP + HYP], f"{name}{hc}",
                ETRANS[:8] + [other])

    def p2_hyper(hyp, first, tag):
        t0 = hyp * HYP
        for oc in range(PC):
            ws = wsm[oc % 4]
            wn = f"wsm{oc % 4}"
            stream_w(False, Wo[oc, :, :], woc[oc, :, :], ws[:, :], wn, f"{tag}o{oc}", cres=f"co{oc}")
            xfo = XFS + (oc % 2) * HYP
            xfn = f"xf{oc % 2}"
            dma("qs", arF[:, xfo:xfo + HYP], xm[oc * 128:(oc + 1) * 128, t0:t0 + HYP],
                [], [xfn], f"{tag}ox")
            for dk in range(PC):
                mm(pb[oc % 4][:, 0:512], ws[:, dk * 128:(dk + 1) * 128],
                   arD[:, dk * HYP:(dk + 1) * HYP],
                   dk == 0, dk == PC - 1, [wn, f"on{dk}"], [f"pb{oc % 4}"], f"{tag}om")
            x1o = XFS + 1024 + (oc % 2) * HYP
            x1n = f"x1f{oc % 2}"
            ew("dve", "tensor_add", (arF[:, x1o:x1o + HYP], arF[:, xfo:xfo + HYP],
                                     pb[oc % 4][:, 0:512]),
               [xfn, f"pb{oc % 4}"], [x1n], f"{tag}x1")
            bxw = [f"Bq{oc}"] if oc < 8 else [f"Bk{oc - 8}"]
            ew("dve", "tensor_add", (arB[:, oc * HYP:(oc + 1) * HYP],
                                     arF[:, x1o:x1o + HYP], arF[:, xfo:xfo + HYP]),
               [x1n, xfn], [f"Bx{oc}", *bxw], f"{tag}xx")
            ew("dve", "tensor_copy", (arA[:, oc * HYP:(oc + 1) * HYP], arF[:, x1o:x1o + HYP]),
               [x1n], [f"A{oc}"], f"{tag}xb")
            act(AF.Square, xsq[oc % 2][:, :], arA[:, oc * HYP:(oc + 1) * HYP],
                [f"A{oc}"], [f"xsq{oc % 2}"], tag=f"{tag}rq")
            mm(pb[4][0:1, 0:512], colb[:, 2:3], xsq[oc % 2][:, :], oc == 0, oc == PC - 1,
               [f"xsq{oc % 2}", "colb"], ["pb4"], f"{tag}rs")
            drip_q(1)
        ew("dve", "tensor_copy", (rrow[0:1, :], pb[4][0:1, 0:512]), ["pb4"], ["rrow"], f"{tag}rc")
        act(AF.Ln, rrow[0:1, :], rrow[0:1, :], ["rrow", "actc"], ["rrow"], bias=acc_sb[0:1, 0:1], tag=f"{tag}rl")
        act(AF.Exp, rrow[0:1, :], rrow[0:1, :], ["rrow"], ["rrow"], scale=-0.5, tag=f"{tag}re")
        mm(pb[6][:, 0:512], r4f[0:1, :], rrow[0:1, :], True, True, ["r4f", "rrow"], ["pb6"], f"{tag}rb")
        ew("dve", "tensor_copy", (rstdb[:, :], pb[6][:, 0:512]), ["pb6"], ["rstdb"], f"{tag}rbc")
        for oc in range(PC):
            sl = arA[:, oc * HYP:(oc + 1) * HYP]
            ew("dve", "scalar_tensor_tensor",
               (sl, sl, ln2_sb[:, oc:oc + 1], rstdb[:, :], ALU.mult, ALU.mult),
               [f"A{oc}", "ln2_sb", "rstdb"], [f"A{oc}"], f"{tag}xn2")
        tap(f"xn2_{tag}", arA[:, 0:PC * HYP], (128, PC * HYP), [f"A{i}" for i in range(PC)])

        for hc in range(FC):
            ws = wsm[hc % 2]
            wn = f"wsm{hc % 2}"
            stream_w(first, W1[hc, :, :], w1c[hc, :, :], ws[:, :], wn, f"{tag}w1_{hc}", cres=f"cw1_{hc}")
            for dk in range(PC):
                mm(pb[hc % 2][:, 0:512], ws[:, dk * 128:(dk + 1) * 128],
                   arA[:, dk * HYP:(dk + 1) * HYP],
                   dk == 0, dk == PC - 1, [wn, f"A{dk}"], [f"pb{hc % 2}"], f"{tag}1m")
            act(AF.Silu, s1c[hc % 2][:, :], pb[hc % 2][:, 0:512], [f"pb{hc % 2}"],
                [f"s1c{hc % 2}"], tag=f"{tag}si")
            w3s = 2 + hc % 2
            stream_w(first, W3[hc, :, :], w3c[hc, :, :], wsm[w3s][:, :], f"wsm{w3s}", f"{tag}w3_{hc}", cres=f"cw3_{hc}")
            for dk in range(PC):
                mm(pb[2 + hc % 2][:, 0:512], wsm[w3s][:, dk * 128:(dk + 1) * 128],
                   arA[:, dk * HYP:(dk + 1) * HYP],
                   dk == 0, dk == PC - 1, [f"wsm{w3s}", f"A{dk}"], [f"pb{2 + hc % 2}"], f"{tag}3m")
            hap, hname, xw = hh_ap(hc, "hh")
            ew("dve", "tensor_mul", (hap, pb[2 + hc % 2][:, 0:512], s1c[hc % 2][:, :]),
               [f"pb{2 + hc % 2}", f"s1c{hc % 2}"], [hname, *xw], f"{tag}hh")
            drip_q(1)

        for oc in range(PC):
            for half in range(2):
                hsl = (oc * 2 + half) % 2
                w2b = arA[:, hsl * FH:(hsl + 1) * FH]
                w2n = f"Aw2{hsl}"
                affected = [f"A{k}" for k in range(hsl * 5, hsl * 5 + 6)]
                stream_w(first, W2[oc, :, half * FH:(half + 1) * FH],
                         w2c[oc, :, half * FH:(half + 1) * FH],
                         w2b, w2n, f"{tag}w2_{oc}_{half}", extra_writes=affected,
                         cres=f"cw2_{oc}_{half}")
                for dkh in range(22):
                    hap, hname, _ = hh_ap(half * 22 + dkh, "hh")
                    mm(pb[oc % 4][:, 0:512], w2b[:, dkh * 128:(dkh + 1) * 128],
                       hap, half == 0 and dkh == 0, half == 1 and dkh == 21,
                       [w2n, hname], [f"pb{oc % 4}"], f"{tag}2m")
            sl = arB[:, oc * HYP:(oc + 1) * HYP]
            ew("dve", "tensor_add", (sl, sl, pb[oc % 4][:, 0:512]),
               [f"Bx{oc}", f"pb{oc % 4}"], [f"Bx{oc}"], f"{tag}s")
            drip_q(2)

        layer_norm(lambda oc: (arB[:, oc * HYP:(oc + 1) * HYP], f"Bx{oc}"),
                   lambda oc: (arA[:, oc * HYP:(oc + 1) * HYP], f"A{oc}",
                               ["Aw20", "Aw21"] if oc < 12 else []),
                   f"{tag}L1")
        tap(f"out1_{tag}", arA[:, 0:PC * HYP], (128, PC * HYP), [f"A{i}" for i in range(PC)])

        if first:
            flush_q()
        for hc in range(FC):
            hs = hc % 2
            qsl = arB[:, hs * 2048:hs * 2048 + 2048]
            qn = f"Bqw{hs}"
            bxq = [f"Bx{4 * hs}", f"Bx{4 * hs + 1}", f"Bx{4 * hs + 2}", f"Bx{4 * hs + 3}"]
            dma("qs", qsl, qw1s[hc, :, :], [f"c1_{hc}_0", f"c1_{hc}_1"],
                [qn, *bxq], f"{tag}c1r")
            if hc == 0:
                tap(f"qw0_{tag}", arB[:, 0:2048], (128, 2048), [qn])
            for dk in range(PC):
                mm(pb[hs][:, 0:512], qsl[:, dk * 128:(dk + 1) * 128],
                   arA[:, dk * HYP:(dk + 1) * HYP],
                   dk == 0, dk == PC - 1, [qn, f"A{dk}"], [f"pb{hs}"], f"{tag}f1m")
            act(AF.Sigmoid, s1c[hs][:, :], pb[hs][:, 0:512], [f"pb{hs}"], [f"s1c{hs}"], tag=f"{tag}sg")
            dma("qs", qsl, qw2s[hc, :, :], [f"c2_{hc}_0", f"c2_{hc}_1"],
                [qn], f"{tag}c2r")
            for dk in range(PC):
                mm(pb[2 + hs][:, 0:512], qsl[:, dk * 128:(dk + 1) * 128],
                   arA[:, dk * HYP:(dk + 1) * HYP],
                   dk == 0, dk == PC - 1, [qn, f"A{dk}"], [f"pb{2 + hs}"], f"{tag}f2m")
            hap, hname, xw = hh_ap(hc, "fh")
            ew("dve", "tensor_mul", (hap, pb[2 + hs][:, 0:512], s1c[hs][:, :]),
               [f"pb{2 + hs}", f"s1c{hs}"], [hname, *xw], f"{tag}fh")
        tap(f"fh_{tag}", arC[:, 0:2048], (128, 2048), ["fh0", "fh1", "fh2", "fh3"])
        tap(f"qw1_{tag}", arB[:, 0:2048], (128, 2048), ["Bqw0", "Bqw1"])
        for oc in range(PC):
            for half in range(2):
                hsl = (oc * 2 + half) % 2
                q3 = arB[:, hsl * FH:(hsl + 1) * FH]
                qn3 = f"Bq3{hsl}"
                bx3 = [f"Bx{k}" for k in range(hsl * 5, hsl * 5 + 6)] + \
                      [f"Bqw{k}" for k in range(2)] + ["Btmp"]
                dma("qs", q3, qw3s[oc, :, half * FH:(half + 1) * FH],
                    [f"c3q_{oc}_{2 * half}", f"c3q_{oc}_{2 * half + 1}"],
                    [qn3, *bx3], f"{tag}c3r")
                for dkh in range(22):
                    hap, hname, _ = hh_ap(half * 22 + dkh, "fh")
                    mm(pb[oc % 4][:, 0:512], q3[:, dkh * 128:(dkh + 1) * 128], hap,
                       half == 0 and dkh == 0, half == 1 and dkh == 21,
                       [qn3, hname], [f"pb{oc % 4}"], f"{tag}f3m")
            ew("dve", "tensor_add", (s2b[:, oc * HYP:(oc + 1) * HYP],
                                     arA[:, oc * HYP:(oc + 1) * HYP], pb[oc % 4][:, 0:512]),
               [f"A{oc}", f"pb{oc % 4}"], [f"s2_{oc}"], f"{tag}s2")
            if oc == 0:
                tap(f"q3a_{tag}", arB[:, 0:FH], (128, FH), ["Bq30"])
                tap(f"q3b_{tag}", arB[:, FH:2 * FH], (128, FH), ["Bq31"])
        tap(f"s2_{tag}", s2b[:, 0:2048], (128, 2048), ["s2_0", "s2_1", "s2_2", "s2_3"])
        tap(f"fhC_{tag}", arC[:, 0:8192], (128, 8192), [f"fh{i}" for i in range(16)])
        tap(f"fhD_{tag}", arD[:, 0:8192], (128, 8192), [f"fh{i}" for i in range(16, 32)])
        tap(f"fhE_{tag}", arE[:, 0:6144], (128, 6144), [f"fh{i}" for i in range(32, 44)])
        layer_norm(lambda oc: (s2b[:, oc * HYP:(oc + 1) * HYP], f"s2_{oc}"),
                   lambda oc: (arF[:, (oc % 2) * HYP:(oc % 2) * HYP + HYP],
                               f"os{oc % 2}", ["Fzt", "Flg"]),
                   f"{tag}L2",
                   store=lambda oc: (out[oc * 128:(oc + 1) * 128, t0:t0 + HYP],))

    if not do_gamma:
        gamma_units.clear()
    for hyp in range(n_pre):
        p1_hyper(xp, hyp, False, hyp == 0, f"P{hyp}")
        if hyp == 0:
            make_cunits()
    if do_gamma:
        gamma_finish()
        make_qunits()
    else:
        gamma_units.clear()
        qpt[0] = NQ + 2
    tap("Spre", S_sb[:, 0:1024], (128, 1024), [f"S{h}" for h in range(16)], F32)
    for hyp in range(n_main):
        p1_hyper(xm, hyp, True, hyp == 0, f"M{hyp}")
        if do_p2:
            p2_hyper(hyp, hyp == 0, f"W{hyp}")
    if n_main == 0:
        dma("qs", out[0:128, 0:512], arF[:, 0:512], ["Fzt"], [], "dummyout")

    P.resolve(serial=serial)
    with nc.Block() as block:
        block.tensor(P.stream("pe", sems))
        block.scalar(P.stream("act", sems))
        block.vector(P.stream("dve", sems))
        block.gpsimd(P.stream("gp", sems))
        block.sync(P.stream("sync", sems))
    ctx.close()
    return nc


def _chunkW(W, nb):
    K, N = W.shape
    H = W.reshape(K // 128, 128, N // 128, 128).transpose(2, 1, 0, 3).reshape(N // 128, 128, K)
    assert H.shape[0] == nb
    return np.ascontiguousarray(H)


def _chunkV(W):
    H = W.reshape(16, 128, 4, 512).transpose(2, 1, 0, 3).reshape(4, 128, 8192)
    return np.ascontiguousarray(H)


def _prep_inputs(inputs):
    i = inputs
    f32 = np.float32
    common = {
        "Wq": _chunkW(np.asarray(i["L0_attn_Wq"], f32), 8),
        "Wk": _chunkW(np.asarray(i["L0_attn_Wk"], f32), 8),
        "Wkg1": np.ascontiguousarray(np.asarray(i["L0_attn_Wkg1"], f32)),
        "bkg2b": np.ascontiguousarray(np.repeat(np.asarray(i["L0_attn_bkg2"], f32).reshape(1, 1024), 128, 0)),
        "Wkg2p": np.ascontiguousarray(np.pad(np.asarray(i["L0_attn_Wkg2"], f32), ((0, 112), (0, 0)))),
        "Wv": _chunkV(np.asarray(i["L0_attn_Wv"], f32)),
        "Wg": _chunkW(np.asarray(i["L0_attn_Wg"], f32), 16),
        "bgc": np.ascontiguousarray(np.asarray(i["L0_attn_bg"], f32).reshape(PC, 128).T),
        "Wo": _chunkW(np.asarray(i["L0_attn_Wo"], f32), 16),
        "ln1c": np.ascontiguousarray(np.asarray(i["L0_attn_ln1_w"], f32).reshape(PC, 128).T),
        "ln2c": np.ascontiguousarray(np.asarray(i["L0_attn_ln2_w"], f32).reshape(PC, 128).T),
        "W1": _chunkW(np.asarray(i["L0_attn_W1"], f32), 44),
        "W3": _chunkW(np.asarray(i["L0_attn_W3"], f32), 44),
        "W2": _chunkW(np.asarray(i["L0_attn_W2"], f32), 16),
        "fw1T": _chunkW(np.ascontiguousarray(np.asarray(i["L0_ffn_w1"], f32).T), 44),
        "fw2T": _chunkW(np.ascontiguousarray(np.asarray(i["L0_ffn_w2"], f32).T), 44),
        "fw3T": _chunkW(np.ascontiguousarray(np.asarray(i["L0_ffn_w3"], f32).T), 16),
    }
    x = np.asarray(i["L0_attn_x"], f32)
    zeros = np.zeros((D, T), f32)
    in_maps = []
    for c in range(8):
        b, h = c // 2, c % 2
        m = dict(common)
        m["xm"] = np.ascontiguousarray(x[b, h * T:(h + 1) * T, :].T)
        m["xp"] = np.ascontiguousarray(x[b, 0:T, :].T) if h == 1 else zeros
        in_maps.append(m)
    return in_maps


def kernel(**inputs):
    in_maps = _prep_inputs(inputs)
    nc = build()
    res = run_bass_kernel_spmd(nc, in_maps, core_ids=list(range(8)))
    outp = np.empty((4, 4096, D), np.float32)
    for c in range(8):
        b, h = c // 2, c % 2
        outp[b, h * T:(h + 1) * T, :] = res.results[c]["out"].T
    return outp


# version 17
# speedup vs baseline: 1.1111x; 1.1111x over previous
import contextlib

import numpy as np
import ml_dtypes

import concourse.bass as bass
import concourse.mybir as mybir
from concourse.bass_utils import run_bass_kernel_spmd

F32 = mybir.dt.float32
BF16 = mybir.dt.bfloat16
AF = mybir.ActivationFunctionType
ALU = mybir.AluOpType
AX = mybir.AxisListType
BFNP = ml_dtypes.bfloat16

D = 2048
T = 2048
HYP = 512
NHYP = T // HYP
H = 16
KD = 64
HD = 128
C = 128
TC = HYP // C
PC = D // 128
QPC = (H * KD) // 128
FF = 5632
FC = FF // 128
FH = FF // 2
KSCALE = float(HD) ** -0.5
NFF = FF * D
EPS = 1e-5
ENGS = ("pe", "act", "dve", "gp", "sync")


class Prog:

    QR = {"qs": 10, "qg": 6, "qt": 8}

    def __init__(self):
        self.steps = []
        self.seq = dict.fromkeys(ENGS, 0)
        self.dseq = {"qs": 0, "qg": 0, "qt": 0}
        self.res = {}

    def _r(self, r):
        return self.res.setdefault(r, {"w": None, "rs": {}})

    def step(self, eng, emit, reads=(), writes=(), dma=None, tag=""):
        deps = []
        for r in reads:
            e = self._r(r)["w"]
            if e is not None:
                deps.append(e)
        for r in writes:
            d = self._r(r)
            if d["w"] is not None:
                deps.append(d["w"])
            deps.extend(d["rs"].items())
        if dma is not None:
            i = self.dseq[dma]
            self.dseq[dma] += 1
            R = self.QR[dma]
            sem = f"{dma}{i % R}"
            ev = (sem, 16 * (i // R + 1))
            if i // R > 0:
                deps.append((sem, 16 * (i // R)))
        else:
            self.seq[eng] += 1
            ev = (eng, self.seq[eng])
        self.steps.append(dict(eng=eng, emit=emit, deps=deps, dma=dma, ev=ev, tag=tag,
                               preads=tuple(reads)))
        for r in reads:
            rs = self._r(r)["rs"]
            if rs.get(ev[0], 0) < ev[1]:
                rs[ev[0]] = ev[1]
        for r in writes:
            d = self._r(r)
            d["w"] = ev
            d["rs"] = {ev[0]: ev[1]}
        return ev

    STRICT = ("dve", "act", "gp")

    def resolve(self, serial=False):
        if serial:
            prev = None
            for st in self.steps:
                if prev is not None:
                    st["deps"] = list(st["deps"]) + [prev]
                prev = st["ev"]
        loose = {}
        for st in self.steps:
            if st["dma"] is None and st["eng"] in self.STRICT:
                loose[st["ev"]] = True
        post_vc = {}
        vc = {e: {} for e in ENGS}
        for st in self.steps:
            eng = st["eng"]
            strict = eng in self.STRICT
            mv = vc[eng]
            waits = []
            for (s, v) in st["deps"]:
                if s == eng and st["dma"] is None and not loose.get((s, v), False):
                    continue
                if mv.get(s, 0) >= v:
                    continue
                waits.append((s, v))
                src = post_vc.get((s, v))
                if src:
                    for k2, v2 in src.items():
                        if mv.get(k2, 0) < v2:
                            mv[k2] = v2
                if mv.get(s, 0) < v:
                    mv[s] = v
            st["waits"] = waits
            s0, v0 = st["ev"]
            if st["dma"] is not None:
                pv = dict(mv)
                pv[s0] = max(pv.get(s0, 0), v0)
                post_vc[st["ev"]] = pv
            else:
                if mv.get(s0, 0) < v0 and not loose.get(st["ev"], False):
                    mv[s0] = v0
                pv = dict(mv)
                pv[s0] = max(pv.get(s0, 0), v0)
                post_vc[st["ev"]] = pv

    def stream(self, name, sems):
        mysteps = [st for st in self.steps if st["eng"] == name]
        dq = dict(self.dseq)

        def f(eng):
            for st in mysteps:
                for (s, v) in st["waits"]:
                    eng.wait_ge(sems[s], v)
                try:
                    inst = st["emit"](eng)
                except Exception as ex:
                    raise RuntimeError(f"step {st['tag']} failed: {ex}") from ex
                s0, _ = st["ev"]
                inst.then_inc(sems[s0], 16 if st["dma"] is not None else 1)
            if name == "sync":
                for q in ("qs", "qt"):
                    R = self.QR[q]
                    for j in range(min(R, dq[q])):
                        n = dq[q] // R + (1 if j < dq[q] % R else 0)
                        eng.wait_ge(sems[f"{q}{j}"], 16 * n)
            if name == "gp":
                q, R = "qg", self.QR["qg"]
                for j in range(min(R, dq[q])):
                    n = dq[q] // R + (1 if j < dq[q] % R else 0)
                    eng.wait_ge(sems[f"{q}{j}"], 16 * n)
        return f


def colap(t, start, step, count, pcount=128, pbase=0):
    fsz = int(np.prod(t.shape[1:]))
    return bass.AP(t, pbase * fsz + start, [[fsz, pcount], [step, count]])


def ap3(t, f0, dims, pcount=128, pbase=0):
    fsz = int(np.prod(t.shape[1:]))
    return bass.AP(t, pbase * fsz + f0, [[fsz, pcount]] + [list(d) for d in dims])


def build(debug_taps=(), n_pre=NHYP, n_main=NHYP, do_p2=True, do_gamma=True, stage="all", serial=False):
    taps = set(debug_taps)
    nc = bass.Bass("TRN2", target_bir_lowering=False, debug=False)

    xm = nc.dram_tensor("xm", [D, T], F32, kind="ExternalInput")
    xp = nc.dram_tensor("xp", [D, T], F32, kind="ExternalInput")
    Wq = nc.dram_tensor("Wq", [8, 128, D], F32, kind="ExternalInput")
    Wk = nc.dram_tensor("Wk", [8, 128, D], F32, kind="ExternalInput")
    Wkg1 = nc.dram_tensor("Wkg1", [D, 16], F32, kind="ExternalInput")
    bkg2b_d = nc.dram_tensor("bkg2b", [128, 1024], F32, kind="ExternalInput")
    wkg2p_d = nc.dram_tensor("Wkg2p", [128, 1024], F32, kind="ExternalInput")
    Wv = nc.dram_tensor("Wv", [4, 128, 8192], F32, kind="ExternalInput")
    Wg = nc.dram_tensor("Wg", [16, 128, D], F32, kind="ExternalInput")
    bgc_d = nc.dram_tensor("bgc", [128, PC], F32, kind="ExternalInput")
    Wo = nc.dram_tensor("Wo", [16, 128, D], F32, kind="ExternalInput")
    ln1_d = nc.dram_tensor("ln1c", [128, PC], F32, kind="ExternalInput")
    ln2_d = nc.dram_tensor("ln2c", [128, PC], F32, kind="ExternalInput")
    W1 = nc.dram_tensor("W1", [FC, 128, D], F32, kind="ExternalInput")
    W3 = nc.dram_tensor("W3", [FC, 128, D], F32, kind="ExternalInput")
    W2 = nc.dram_tensor("W2", [16, 128, FF], F32, kind="ExternalInput")
    fw1T = nc.dram_tensor("fw1T", [FC, 128, D], F32, kind="ExternalInput")
    fw2T = nc.dram_tensor("fw2T", [FC, 128, D], F32, kind="ExternalInput")
    fw3T = nc.dram_tensor("fw3T", [16, 128, FF], F32, kind="ExternalInput")
    out = nc.dram_tensor("out", [D, T], F32, kind="ExternalOutput")

    wqc = nc.dram_tensor("wqc", [8, 128, 2048], BF16, kind="Internal")
    wkc = nc.dram_tensor("wkc", [8, 128, 2048], BF16, kind="Internal")
    wvc = nc.dram_tensor("wvc", [4, 128, 8192], BF16, kind="Internal")
    wgc = nc.dram_tensor("wgc", [16, 128, 2048], BF16, kind="Internal")
    woc = nc.dram_tensor("woc", [16, 128, 2048], BF16, kind="Internal")
    w1c = nc.dram_tensor("w1c", [FC, 128, 2048], BF16, kind="Internal")
    w3c = nc.dram_tensor("w3c", [FC, 128, 2048], BF16, kind="Internal")
    w2c = nc.dram_tensor("w2c", [PC, 128, FF], BF16, kind="Internal")
    qw1s = nc.dram_tensor("qw1s", [FC, 128, 2048], BF16, kind="Internal")
    qw2s = nc.dram_tensor("qw2s", [FC, 128, 2048], BF16, kind="Internal")
    qw3s = nc.dram_tensor("qw3s", [PC, 128, FF], BF16, kind="Internal")

    rows4 = np.zeros((128, 128), np.float32)
    for p in (0, 32, 64, 96):
        rows4[p, :] = 1.0
    colsc = np.concatenate([np.ones((128, 1), np.float32),
                            np.full((128, 1), 1 / 128, np.float32),
                            np.full((128, 1), 1 / 2048, np.float32)], 1)
    identc = nc.inline_tensor(np.eye(128, dtype=BFNP), "identc")
    maskutc = nc.inline_tensor(np.triu(np.ones((128, 128), np.float32)).astype(BFNP), "maskutc")
    triu16c = nc.inline_tensor((np.triu(np.ones((128, 128), np.float32)) / 16.0).astype(np.float32), "triu16c")
    colsfc = nc.inline_tensor(colsc, "colsfc")
    colsbc = nc.inline_tensor(colsc.astype(BFNP), "colsbc")
    rows4fc = nc.inline_tensor(rows4, "rows4fc")
    actcc = nc.inline_tensor(np.concatenate([np.full((128, 1), EPS, np.float32),
                                             np.ones((128, 1), np.float32)], 1), "actcc")
    rows4bc = nc.inline_tensor(rows4.astype(BFNP), "rows4bc")

    AT = nc.alloc_sbuf_tensor
    arA = AT("arA", [128, PC * HYP], BF16)
    arB = AT("arB", [128, 2 * QPC * HYP], BF16)
    arC = AT("arC", [128, TC * D], BF16)
    arD = AT("arD", [128, PC * HYP], BF16)
    arE = AT("arE", [128, 10240], BF16)
    arF = AT("arF", [128, 6144], F32)
    gbA = AT("gbA", [128, FF], F32)
    s2b = AT("s2b", [128, PC * HYP], BF16)
    wsm = [AT(f"wsm{i}", [128, 2048], BF16) for i in range(4)]

    id_sb = AT("id_sb", [128, 128], BF16)
    mut_sb = AT("mut_sb", [128, 128], BF16)
    tri_sb = AT("tri_sb", [128, 128], F32)
    colb = AT("colb", [128, 3], BF16)
    colf = AT("colf", [128, 3], F32)
    r4f = AT("r4f", [128, 128], F32)
    r4b = AT("r4b", [128, 128], BF16)
    ln1_sb = AT("ln1_sb", [128, PC], F32)
    ln2_sb = AT("ln2_sb", [128, PC], F32)
    bg_sb = AT("bg_sb", [128, PC], F32)
    bk2b = AT("bk2b", [128, 1024], BF16)
    wkg2p_sb = AT("wkg2p_sb", [128, 1024], BF16)
    wkg1_sb = AT("wkg1_sb", [128, 256], BF16)
    thr_sb = AT("thr_sb", [128, 8], F32)
    gacc = AT("gacc", [128, 4], F32)
    gtmp = AT("gtmp", [128, 1], F32)
    acc_sb = AT("acc_sb", [128, 2], F32)
    z1b = AT("z1b", [16, HYP], BF16)
    S_sb = AT("S_sb", [128, 1024], F32)
    S_bf = AT("S_bf", [128, 1024], BF16)
    xsq = [AT(f"xsq{i}", [128, HYP], BF16) for i in range(2)]
    rrow = AT("rrow", [1, 512], F32)
    rstdb = AT("rstdb", [128, HYP], BF16)
    mubf = AT("mubf", [128, HYP], BF16)
    P_sb = AT("P_sb", [128, 2048], BF16)
    o_sb = AT("o_sb", [128, 2048], BF16)
    amid = AT("amid", [128, QPC], F32)
    namid = AT("namid", [128, QPC], F32)
    eacm = AT("eacm", [128, QPC], F32)
    eamid = AT("eamid", [128, QPC], F32)
    dcol = AT("dcol", [128, QPC], F32)
    scrb = AT("scrb", [128, 2048], BF16)
    scrf = AT("scrf", [128, 1024], F32)
    s1c = [AT(f"s1c{i}", [128, HYP], BF16) for i in range(2)]
    sgt = [AT(f"sgt{i}", [128, HYP], BF16) for i in range(2)]

    PS = nc.alloc_psum_tensor
    pb = [PS(f"pb{i}", [128, 512], F32) for i in range(7)]
    pT = PS("pT", [128, 1024], BF16)

    ctx = contextlib.ExitStack()
    semnames = ["pe", "act", "dve", "gp"]
    for q, R in Prog.QR.items():
        semnames += [f"{q}{j}" for j in range(R)]
    sems = {s: ctx.enter_context(nc.semaphore(f"sem_{s}")) for s in semnames}
    P = Prog()

    def dma(q, outap, inap, reads, writes, tag=""):
        eng = "sync" if q == "qs" else "gp"
        P.step(eng, lambda e, o=outap, i=inap: e.dma_start(out=o, in_=i),
               reads=reads, writes=writes, dma=q, tag=tag)

    def dmaT(outap, inap, reads, writes, tag=""):
        P.step("sync", lambda e, o=outap, i=inap: e.dma_start_transpose(o, i),
               reads=reads, writes=writes, dma="qt", tag=tag)

    def mm(outap, lhs, rhs, start, stop, reads, writes, tag=""):
        P.step("pe", lambda e, o=outap, a=lhs, b=rhs, s1=start, s2=stop:
               e.matmul(o, a, b, start=s1, stop=s2),
               reads=reads, writes=writes, tag=tag)

    def tr(outap, inap, idap, reads, writes, tag=""):
        P.step("pe", lambda e, o=outap, i=inap, d=idap: e.transpose(o, i, d),
               reads=reads, writes=writes, tag=tag)

    def act(fn, outap, inap, reads, writes, bias=0.0, scale=1.0, tag=""):
        P.step("act", lambda e, o=outap, i=inap, b=bias, s=scale, f=fn:
               e.activation(o, i, f, bias=b, scale=s),
               reads=reads, writes=writes, tag=tag)

    def ew(eng, name, args, reads, writes, tag=""):
        P.step(eng, lambda e, n=name, a=args: getattr(e, n)(*a),
               reads=reads, writes=writes, tag=tag)

    def tap(name, srcap, shape, reads, dtype=BF16):
        if name not in taps:
            return
        dt_ = nc.dram_tensor(f"tap_{name}", list(shape), dtype, kind="ExternalOutput")
        dma("qs", dt_[tuple(slice(0, s) for s in shape)], srcap, reads=reads,
            writes=[f"tap_{name}"], tag=f"tap_{name}")

    def d3(ap, n):
        return ap.rearrange("p (dk n) -> p dk n", n=n)

    def stream_w(first, srcf32, cache_ap, dest, destres, tag, extra_writes=(), cres=None):
        c = cres or f"c{tag}"
        if first:
            dma("qg", dest, srcf32, [], [destres, *extra_writes], f"{tag}ld")
            dma("qs", cache_ap, dest, [destres], [c], f"{tag}cw")
        else:
            dma("qs", dest, cache_ap, [c], [destres, *extra_writes], f"{tag}cr")

    gamma_units = []

    def make_gamma_units():
        ui = [0]
        for (wt, nblk, cw, wi, nm) in ((fw1T, FC, D, 0, "g1"), (fw2T, FC, D, 1, "g2"),
                                       (fw3T, 16, FF, 2, "g3")):
            for r in range(nblk):
                def unit(r=r, wt=wt, cw=cw, wi=wi, nm=nm):
                    sl = ui[0] % 2
                    ui[0] += 1
                    b = gbA[:, sl * FH:sl * FH + cw] if cw <= FH else gbA[:, 0:cw]
                    bn = f"gbA{sl}" if cw <= FH else "gbA01"
                    xw = [] if cw <= FH else ["gbA0", "gbA1"]
                    dma("qs", b, wt[r, :, 0:cw], [], [bn, *xw], f"{nm}ld")
                    ew("dve", "tensor_reduce", (gtmp[:, 0:1], b, AX.X, ALU.add, True),
                       [bn], ["gtmp"], f"{nm}red")
                    ew("dve", "tensor_add", (gacc[:, wi:wi + 1], gacc[:, wi:wi + 1], gtmp[:, 0:1]),
                       ["gtmp", "gacc"], ["gacc"], f"{nm}acc")
                gamma_units.append(unit)

    def drip_gamma(n=1):
        for _ in range(n):
            if gamma_units:
                gamma_units.pop(0)()

    def gamma_finish():
        while gamma_units:
            drip_gamma()
        tap("gacc", gacc[:, 0:4], (128, 4), ["gacc"], F32)
        mm(pb[6][0:1, 0:3], colf[:, 0:1], gacc[:, 0:3], True, True,
           ["colf", "gacc"], ["pb6"], "gsum")
        ew("dve", "tensor_copy", (scrf[0:1, 0:3], pb[6][0:1, 0:3]), ["pb6"], ["scrf"])
        ew("dve", "tensor_scalar",
           (scrf[0:1, 4:7], scrf[0:1, 0:3], 0.5 / NFF, 0.5e-8, ALU.mult, ALU.add),
           ["scrf"], ["scrf"], "gthr")
        tap("scrf", scrf[0:1, 0:8], (1, 8), ["scrf"], F32)
        for wi in range(3):
            mm(pb[6][:, 0:1], r4f[0:1, :], scrf[0:1, 4 + wi:5 + wi], True, True,
               ["r4f", "scrf"], ["pb6"])
            ew("dve", "tensor_copy", (thr_sb[:, wi:wi + 1], pb[6][:, 0:1]),
               ["pb6"], ["thr_sb"])
        ew("dve", "tensor_scalar", (thr_sb[:, 4:7], thr_sb[:, 0:3], -1.0, None, ALU.mult),
           ["thr_sb"], ["thr_sb"])
        tap("thr", thr_sb[:, 0:8], (128, 8), ["thr_sb"], F32)

    qspecs = []
    for hc in range(FC):
        for (wt, cache, wi, cp) in ((fw1T, qw1s, 0, "c1"), (fw2T, qw2s, 1, "c2")):
            for pc2 in range(2):
                qspecs.append((wt[hc, :, pc2 * 1024:(pc2 + 1) * 1024],
                               cache[hc, :, pc2 * 1024:(pc2 + 1) * 1024],
                               wi, 1024, f"{cp}_{hc}_{pc2}"))
    for oc in range(PC):
        for q4 in range(4):
            qspecs.append((fw3T[oc, :, q4 * 1408:(q4 + 1) * 1408],
                           qw3s[oc, :, q4 * 1408:(q4 + 1) * 1408],
                           2, 1408, f"c3q_{oc}_{q4}"))
    NQ = len(qspecs)
    qpt = [NQ + 2]

    def s2names(a, b):
        return [f"s2_{k}" for k in range(a // HYP, -(-b // HYP))]

    def qslot(i, w):
        j = i % 4
        return s2b[:, j * 1408:j * 1408 + w], \
            [f"s2q{j}"] + s2names(j * 1408, j * 1408 + w)

    def q_load(i):
        src, dst, wi, w, cres = qspecs[i]
        j = i % 4
        xw = ["gbA0", "gbA1", "gbA01"] if i < 4 else []
        dma("qt", gbA[:, j * 1408:j * 1408 + w], src, [], [f"gq{j}", *xw], f"dqld{i}")

    def q_comp(i):
        src, dst, wi, w, cres = qspecs[i]
        j = i % 4
        g = gbA[:, j * 1408:j * 1408 + w]
        sg, snm = qslot(i + 1, w)
        oap, onm = qslot(i, w)
        act(AF.Sign, sg, g, [f"gq{j}"], snm, tag=f"dqsg{i}")
        act(AF.Abs, g, g, [f"gq{j}"], [f"gq{j}"], tag=f"dqab{i}")
        ew("dve", "scalar_tensor_tensor", (oap, g, thr_sb[:, wi:wi + 1], sg,
                                           ALU.is_ge, ALU.mult),
           [f"gq{j}", "thr_sb", snm[0]], onm, f"dqge{i}")

    def q_write(i):
        src, dst, wi, w, cres = qspecs[i]
        oap, onm = qslot(i, w)
        dma("qs", dst, oap, onm, [cres], f"dqcw{i}")

    def qstep(i):
        if 0 <= i - 2 < NQ:
            q_write(i - 2)
        if i + 2 < NQ:
            q_load(i + 2)
        if i < NQ:
            q_comp(i)

    def make_qunits():
        qpt[0] = 0
        q_load(0)
        q_load(1)

    def drip_q(n=1):
        for _ in range(n):
            if qpt[0] <= NQ + 1:
                i = qpt[0]
                qpt[0] += 1
                qstep(i)

    def flush_q():
        while qpt[0] <= NQ + 1:
            drip_q()

    cspecs = []
    for oc in range(8):
        cspecs.append((Wq[oc, :, :], wqc[oc, :, :], f"cq{oc}"))
    for oc in range(16):
        cspecs.append((Wg[oc, :, :], wgc[oc, :, :], f"cg{oc}"))
    for oc in range(16):
        cspecs.append((Wo[oc, :, :], woc[oc, :, :], f"co{oc}"))
    NC_ = len(cspecs)
    cpt = [NC_]

    def make_cunits():
        cpt[0] = 0

    def drip_c(n=1):
        for _ in range(n):
            if cpt[0] < NC_:
                u = cpt[0]
                cpt[0] += 1
                src, cache, cres = cspecs[u]
                j = u % 4
                slot = s2b[:, j * 2048:(j + 1) * 2048]
                nm = [f"cs{j}"] + s2names(j * 2048, (j + 1) * 2048)
                dma("qg", slot, src, [], nm, f"dc{u}ld")
                dma("qs", cache, slot, nm, [cres], f"dc{u}cw")

    dma("qs", id_sb[:, :], identc[:, :], [], ["id_sb"])
    dma("qs", mut_sb[:, :], maskutc[:, :], [], ["mut_sb"])
    dma("qs", tri_sb[:, :], triu16c[:, :], [], ["tri_sb"])
    dma("qs", colf[:, :], colsfc[:, :], [], ["colf"])
    dma("qs", colb[:, :], colsbc[:, :], [], ["colb"])
    dma("qs", r4f[:, :], rows4fc[:, :], [], ["r4f"])
    dma("qs", acc_sb[:, :], actcc[:, :], [], ["actc"])
    dma("qs", r4b[:, :], rows4bc[:, :], [], ["r4b"])
    dma("qs", ln1_sb[:, :], ln1_d[:, :], [], ["ln1_sb"])
    dma("qs", ln2_sb[:, :], ln2_d[:, :], [], ["ln2_sb"])
    dma("qs", bg_sb[:, :], bgc_d[:, :], [], ["bg_sb"])
    dma("qg", bk2b[:, :], bkg2b_d[:, :], [], ["bk2b"])
    dma("qg", wkg2p_sb[:, :], wkg2p_d[:, :], [], ["wkg2_sb"])
    dma("qg", wkg1_sb[:, :].rearrange("p (dk n) -> p dk n", n=16),
        Wkg1.ap().rearrange("(dk p) n -> p dk n", p=128), [], ["wkg1_sb"])
    ew("dve", "memset", (S_sb[:, :], 0.0), [], ["S_sb"] + [f"S{h}" for h in range(16)])
    ew("dve", "memset", (S_bf[:, :], 0.0), [], [f"Sb{h}" for h in range(16)])
    ew("dve", "memset", (gacc[:, :], 0.0), [], ["gacc"])
    make_gamma_units()

    ZT0, LG0, GK0, A20 = 0, 1024, 2048, 3072
    XFS = 4096
    EV0 = 0
    SQT, SQG, SKT, SKH = 0, 1024, 2048, 3072
    SKHT = 8192
    ETRANS = ["Ekt0", "Ekh0", "Eqt0", "Eqg0", "Ekt1", "Ekh1", "Eqt1", "Eqg1",
              "EkhT0", "EkhT1", "Evw"]

    def rms_xn(xdram, hyp, tag):
        t0 = hyp * HYP
        for pc in range(PC):
            xfo = XFS + (pc % 4) * HYP
            xfn = f"xf{pc % 4}"
            dma("qs", arF[:, xfo:xfo + HYP], xdram[pc * 128:(pc + 1) * 128, t0:t0 + HYP],
                [], [xfn], f"{tag}xld")
            ew("dve", "tensor_copy", (arA[:, pc * HYP:(pc + 1) * HYP], arF[:, xfo:xfo + HYP]),
               [xfn, "x1f0", "x1f1"], [f"A{pc}"], f"{tag}xc")
            act(AF.Square, xsq[pc % 2][:, :], arA[:, pc * HYP:(pc + 1) * HYP],
                [f"A{pc}"], [f"xsq{pc % 2}"], tag=f"{tag}sq")
            mm(pb[4][0:1, 0:512], colb[:, 2:3], xsq[pc % 2][:, :],
               pc == 0, pc == PC - 1, [f"xsq{pc % 2}", "colb"], ["pb4"], f"{tag}s2")
            drip_gamma(1)
            drip_q(1)
            drip_c(1)
        ew("dve", "tensor_copy", (rrow[0:1, :], pb[4][0:1, 0:512]), ["pb4"], ["rrow"], f"{tag}rc")
        act(AF.Ln, rrow[0:1, :], rrow[0:1, :], ["rrow", "actc"], ["rrow"], bias=acc_sb[0:1, 0:1], tag=f"{tag}ln")
        act(AF.Exp, rrow[0:1, :], rrow[0:1, :], ["rrow"], ["rrow"], scale=-0.5, tag=f"{tag}ex")
        mm(pb[6][:, 0:512], r4f[0:1, :], rrow[0:1, :], True, True,
           ["r4f", "rrow"], ["pb6"], f"{tag}bc")
        ew("dve", "tensor_copy", (rstdb[:, :], pb[6][:, 0:512]), ["pb6"], ["rstdb"], f"{tag}bcc")
        for pc in range(PC):
            sl = arA[:, pc * HYP:(pc + 1) * HYP]
            ew("dve", "scalar_tensor_tensor",
               (sl, sl, ln1_sb[:, pc:pc + 1], rstdb[:, :], ALU.mult, ALU.mult),
               [f"A{pc}", "ln1_sb", "rstdb"], [f"A{pc}"], f"{tag}xn")

    def proj_fm(wdram, cache, dout, first, evac, tag, cpfx=None):
        for oc in range(dout // 128):
            ws = wsm[oc % 4]
            wn = f"wsm{oc % 4}"
            stream_w(first, wdram[oc, :, :], cache[oc, :, :], ws[:, :], wn, f"{tag}{oc}",
                     cres=f"{cpfx}{oc}" if cpfx else None)
            for dk in range(PC):
                mm(pb[oc % 4][:, 0:512], ws[:, dk * 128:(dk + 1) * 128],
                   arA[:, dk * HYP:(dk + 1) * HYP],
                   dk == 0, dk == PC - 1, [wn, f"A{dk}"], [f"pb{oc % 4}"], f"{tag}mm")
            evac(pb[oc % 4][:, 0:512], f"pb{oc % 4}", oc)
            drip_gamma(1)
            drip_q(1)
            drip_c(1)

    def proj_vtm(first, tag):
        for js in range(4):
            for hf in range(2):
                dk0 = hf * 8
                stream_w(first, Wv[js, :, hf * 4096:(hf + 1) * 4096],
                         wvc[js, :, hf * 4096:(hf + 1) * 4096],
                         arE[:, EV0 + hf * 4096:EV0 + (hf + 1) * 4096], f"Evw{hf}",
                         f"{tag}{js}_{hf}", extra_writes=ETRANS[:8],
                         cres=f"cv{js}_{hf}")
            for tc in range(TC):
                if tc % 2 == 0:
                    drip_q(1)
                    drip_c(1)
                bk = tc % 4
                for dk in range(PC):
                    mm(pb[bk][:, 0:512],
                       arA[:, dk * HYP + tc * 128: dk * HYP + (tc + 1) * 128],
                       arE[:, EV0 + dk * 512:EV0 + (dk + 1) * 512],
                       dk == 0, dk == PC - 1, [f"Evw{dk // 8}", f"A{dk}"], [f"pb{bk}"], f"{tag}mm")
                ew("act" if tc % 2 else "dve", "copy" if tc % 2 else "tensor_copy",
                   (arC[:, tc * D + js * 512: tc * D + (js + 1) * 512], pb[bk][:, 0:512]),
                   [f"pb{bk}"], [f"V{tc}"], f"{tag}ev")

    def proj_z1(tag):
        for dk in range(PC):
            mm(pb[2][0:16, 0:512], wkg1_sb[:, dk * 16:(dk + 1) * 16],
               arA[:, dk * HYP:(dk + 1) * HYP],
               dk == 0, dk == PC - 1, ["wkg1_sb", f"A{dk}"], ["pb2"], f"{tag}mm")
        ew("dve", "tensor_copy", (z1b[0:16, :], pb[2][0:16, 0:512]), ["pb2"], ["z1b"], f"{tag}ev")

    def scan_prep(tc, full, tag):
        sl = tc % 2
        zt = arF[:, ZT0:ZT0 + 1024]
        lg = arF[:, LG0:LG0 + 1024]
        gk = arF[:, GK0:GK0 + 1024]
        for j in range(2):
            mm(pb[3 + j][:, 0:512], z1b[0:16, tc * 128:(tc + 1) * 128],
               wkg2p_sb[0:16, j * 512:(j + 1) * 512], True, True,
               ["z1b", "wkg2_sb"], [f"pb{3 + j}"], f"{tag}zm")
            ew("dve", "tensor_add", (arF[:, ZT0 + j * 512:ZT0 + (j + 1) * 512],
                                     pb[3 + j][:, 0:512], bk2b[:, j * 512:(j + 1) * 512]),
               [f"pb{3 + j}", "bk2b"], ["Fzt"], f"{tag}ze")
        act(AF.Abs, lg, zt, ["Fzt"], ["Flg"], tag=f"{tag}la")
        act(AF.Exp, lg, lg, ["Flg"], ["Flg"], scale=-1.0, tag=f"{tag}le")
        act(AF.Ln, lg, lg, ["Flg", "actc"], ["Flg"], bias=acc_sb[:, 1:2], tag=f"{tag}ll")
        ew("dve", "tensor_scalar", (gk, zt, 0.0, None, ALU.min), ["Fzt"], ["Fgk"], f"{tag}lm")
        ew("dve", "tensor_sub", (gk, gk, lg), ["Fgk", "Flg"], ["Fgk"], f"{tag}ls")
        for p8 in range(QPC):
            bk = 3 + (p8 % 2)
            mm(pb[bk][:, 0:128], arF[:, GK0 + p8 * 128:GK0 + (p8 + 1) * 128], tri_sb[:, :],
               True, True, ["Fgk", "tri_sb"], [f"pb{bk}"], f"{tag}Am")
            ew("dve", "tensor_copy", (arF[:, A20 + p8 * 128:A20 + (p8 + 1) * 128],
                                      pb[bk][:, 0:128]),
               [f"pb{bk}"], ["FA"], f"{tag}Ae")
        a63 = colap(arF, A20 + 63, 128, QPC)
        a127 = colap(arF, A20 + 127, 128, QPC)
        ew("dve", "tensor_copy", (amid[:, :], a63), ["FA"], ["amid"], f"{tag}am")
        act(AF.Exp, dcol[:, :], a127, ["FA"], ["dcol"], tag=f"{tag}de")
        if full:
            ew("dve", "tensor_scalar", (namid[:, :], amid[:, :], -1.0, None, ALU.mult),
               ["amid"], ["namid"], f"{tag}nm")
        kt0, kh0 = SKT + sl * 4096, SKH + sl * 4096
        qt0, qg0 = SQT + sl * 4096, SQG + sl * 4096
        ktn, khn, qtn, qgn = f"Ekt{sl}", f"Ekh{sl}", f"Eqt{sl}", f"Eqg{sl}"
        for p8 in range(QPC):
            asl = arF[:, A20 + p8 * 128:A20 + (p8 + 1) * 128]
            act(AF.Exp, scrb[:, p8 * 128:(p8 + 1) * 128], asl, ["FA", "amid"], ["scrb"],
                bias=amid[:, p8:p8 + 1], scale=-1.0, tag=f"{tag}Dk")
            act(AF.Exp, P_sb[:, p8 * 128:(p8 + 1) * 128], asl,
                ["FA"], ["Dk2", f"P{p8}"],
                bias=colap(arF, A20 + p8 * 128 + 127, 128, 1), scale=-1.0, tag=f"{tag}D2")
        kap = ap3(arB, 4096 + tc * 128, [(HYP, QPC), (1, 128)])
        dkap = ap3(scrb, 0, [(128, QPC), (1, 128)])
        dk2ap = ap3(P_sb, 0, [(128, QPC), (1, 128)])
        ktap = ap3(arE, kt0, [(128, QPC), (1, 128)])
        khap = ap3(arE, kh0, [(128, QPC), (1, 128)])
        ew("dve", "tensor_mul", (ktap, kap, dkap),
           [f"Bk{p8}" for p8 in range(QPC)] + ["scrb"], [ktn, "Evw0", "Evw1"], f"{tag}kt")
        ew("dve", "tensor_mul", (khap, kap, dk2ap),
           [f"Bk{p8}" for p8 in range(QPC)] + ["Dk2"], [khn, "Evw0", "Evw1"], f"{tag}kh")
        if full:
            for p8 in range(QPC):
                asl = arF[:, A20 + p8 * 128:A20 + (p8 + 1) * 128]
                act(AF.Exp, scrb[:, 1024 + p8 * 128:1024 + (p8 + 1) * 128], asl,
                    ["FA", "namid"], ["sDq"],
                    bias=namid[:, p8:p8 + 1], scale=1.0, tag=f"{tag}Dq")
                act(AF.Exp, o_sb[:, p8 * 128:(p8 + 1) * 128], asl,
                    ["FA"], ["Dg", f"ob{p8 // 4}"], scale=1.0, tag=f"{tag}Dg")
            qap = ap3(arB, tc * 128, [(HYP, QPC), (1, 128)])
            dqap = ap3(scrb, 1024, [(128, QPC), (1, 128)])
            dgap = ap3(o_sb, 0, [(128, QPC), (1, 128)])
            qtap = ap3(arE, qt0, [(128, QPC), (1, 128)])
            qgap = ap3(arE, qg0, [(128, QPC), (1, 128)])
            ew("dve", "tensor_mul", (qtap, qap, dqap),
               [f"Bq{p8}" for p8 in range(QPC)] + ["sDq"], [qtn, "Evw0", "Evw1"], f"{tag}qt")
            ew("dve", "tensor_mul", (qgap, qap, dgap),
               [f"Bq{p8}" for p8 in range(QPC)] + ["Dg"], [qgn, "Evw0", "Evw1"], f"{tag}qg")

    def scan_chunk(tc, full, tag):
        sl = tc % 2
        kt0, kh0 = SKT + sl * 4096, SKH + sl * 4096
        qt0, qg0 = SQT + sl * 4096, SQG + sl * 4096
        kht0 = SKHT + sl * 1024
        ktn, khn, qtn, qgn = f"Ekt{sl}", f"Ekh{sl}", f"Eqt{sl}", f"Eqg{sl}"
        khtn = f"EkhT{sl}"
        for p8 in range(QPC):
            dmaT(arE[:, kht0 + p8 * 128:kht0 + (p8 + 1) * 128],
                 arE[:, kh0 + p8 * 128:kh0 + (p8 + 1) * 128],
                 [khn], [khtn], f"{tag}T")
        if stage == "T1":
            return
        for g in range(4):
            ob = g % 2
            for h in range(4 * g, 4 * g + 4):
                p8, pbs = h // 2, (h % 2) * 64
                if full:
                    mm(pb[5][:, (h % 4) * 128:(h % 4) * 128 + 128],
                       arE[pbs:pbs + 64, kt0 + p8 * 128:kt0 + (p8 + 1) * 128],
                       arE[pbs:pbs + 64, qt0 + p8 * 128:qt0 + (p8 + 1) * 128],
                       True, True, [ktn, qtn], ["pb5"], f"{tag}sc")
                    ew("dve", "tensor_mul", (P_sb[:, h * 128:(h + 1) * 128],
                                             pb[5][:, (h % 4) * 128:(h % 4) * 128 + 128],
                                             mut_sb[:, :]),
                       ["pb5", "mut_sb"], [f"P{h}"], f"{tag}mk")
                    mm(pb[ob][:, (h % 4) * 128:(h % 4) * 128 + 128],
                       arC[:, tc * D + h * HD: tc * D + (h + 1) * HD],
                       P_sb[:, h * 128:(h + 1) * 128],
                       True, False, [f"V{tc}", f"P{h}"], [f"pb{ob}"], f"{tag}ov")
                    mm(pb[ob][:, (h % 4) * 128:(h % 4) * 128 + 128],
                       S_bf[pbs:pbs + 64, (h // 2) * 128:(h // 2 + 1) * 128],
                       arE[pbs:pbs + 64, qg0 + p8 * 128:qg0 + (p8 + 1) * 128],
                       False, True, [f"Sb{h}", qgn], [f"pb{ob}"], f"{tag}oi")
                upb = pb[2][pbs:pbs + 64, ((h // 2) % 4) * 128:((h // 2) % 4) * 128 + 128]
                mm(upb, arE[:, kht0 + h * 64:kht0 + (h + 1) * 64],
                   arC[:, tc * D + h * HD: tc * D + (h + 1) * HD],
                   True, True, [khtn, f"V{tc}"], ["pb2"], f"{tag}U")
                if stage == "U1":
                    continue
                Ssl = S_sb[pbs:pbs + 64, (h // 2) * 128:(h // 2 + 1) * 128]
                ew("dve", "scalar_tensor_tensor",
                   (Ssl, Ssl, dcol[pbs:pbs + 64, p8:p8 + 1], upb, ALU.mult, ALU.add),
                   [f"S{h}", "dcol", "pb2"], [f"S{h}"], f"{tag}Su")
                if stage != "Sup1":
                    ew("dve", "tensor_copy",
                       (S_bf[pbs:pbs + 64, (h // 2) * 128:(h // 2 + 1) * 128], Ssl),
                       [f"S{h}"], [f"Sb{h}"], f"{tag}Sb")
            if full:
                ew("dve", "tensor_copy", (o_sb[:, g * 512:(g + 1) * 512], pb[ob][:, 0:512]),
                   [f"pb{ob}"], [f"ob{g}"], f"{tag}oe")
        if full:
            head_ln(tc, tag)

    def head_ln(tc, tag):
        for j in range(2):
            act(AF.Square, P_sb[:, j * 1024:(j + 1) * 1024], o_sb[:, j * 1024:(j + 1) * 1024],
                [f"ob{2 * j}", f"ob{2 * j + 1}"],
                [f"Q{j}"] + [f"P{h}" for h in range(8 * j, 8 * j + 8)],
                tag=f"{tag}q2")
        for j4 in range(4):
            bk, pp = 3 + j4 // 2, (j4 % 2) * 32
            mm(pb[bk][pp:pp + 1, 0:512], colb[:, 1:2], o_sb[:, j4 * 512:(j4 + 1) * 512],
               True, True, [f"ob{j4}", "colb"], [f"pb{bk}"], f"{tag}mu")
            bk2 = 5 + j4 // 2
            mm(pb[bk2][pp:pp + 1, 0:512], colb[:, 1:2], P_sb[:, j4 * 512:(j4 + 1) * 512],
               True, True, [f"Q{j4 // 2}", "colb"], [f"pb{bk2}"], f"{tag}m2")
        for i, bk in enumerate((3, 4)):
            ew("dve", "tensor_copy", (scrb[0:64, i * 512:(i + 1) * 512], pb[bk][0:64, 0:512]),
               [f"pb{bk}"], ["scrb"], f"{tag}e1")
        for i, bk in enumerate((5, 6)):
            ew("dve", "tensor_copy", (scrb[0:64, 1024 + i * 512:1024 + (i + 1) * 512],
                                      pb[bk][0:64, 0:512]),
               [f"pb{bk}"], ["scrb"], f"{tag}e2")
        for j4 in range(4):
            i, pp = j4 // 2, (j4 % 2) * 32
            lhs = r4b[pp:pp + 1, :]
            mm(pb[3][:, 0:512], lhs, scrb[pp:pp + 1, i * 512:(i + 1) * 512], True, True,
               ["scrb", "r4b"], ["pb3"], f"{tag}bm")
            mm(pb[4][:, 0:512], lhs, scrb[pp:pp + 1, 1024 + i * 512:1024 + (i + 1) * 512],
               True, True, ["scrb", "r4b"], ["pb4"], f"{tag}b2")
            act(AF.Square, sgt[0][:, :], pb[3][:, 0:512], ["pb3"], ["sgt0"], tag=f"{tag}s")
            ew("dve", "tensor_sub", (scrf[:, 0:512], pb[4][:, 0:512], sgt[0][:, :]),
               ["pb4", "sgt0"], ["scrf"], f"{tag}v")
            act(AF.Ln, scrf[:, 0:512], scrf[:, 0:512], ["scrf", "actc"], ["scrf"], bias=acc_sb[:, 0:1], tag=f"{tag}vl")
            act(AF.Exp, sgt[1][:, :], scrf[:, 0:512], ["scrf"], ["sgt1"], scale=-0.5, tag=f"{tag}ve")
            ew("dve", "tensor_sub", (o_sb[:, j4 * 512:(j4 + 1) * 512],
                                     o_sb[:, j4 * 512:(j4 + 1) * 512], pb[3][:, 0:512]),
               [f"ob{j4}", "pb3"], [f"ob{j4}"], f"{tag}t1")
            dsl = ap3(arD, 4 * j4 * HYP + tc * 128, [(HYP, 4), (1, 128)])
            i0 = ap3(o_sb, j4 * 512, [(128, 4), (1, 128)])
            i1 = ap3(sgt[1], 0, [(128, 4), (1, 128)])
            ew("dve", "tensor_mul", (dsl, i0, i1),
               [f"ob{j4}", "sgt1"],
               [f"on{4 * j4}", f"on{4 * j4 + 1}", f"on{4 * j4 + 2}", f"on{4 * j4 + 3}"],
               f"{tag}on")

    def p1_hyper(xdram, hyp, full, first, tag):
        rms_xn(xdram, hyp, f"{tag}r")
        if stage == "rms":
            return
        tap(f"xn_{tag}", arA[:, 0:2048], (128, 2048), ["A0", "A1", "A2", "A3"])
        if full:
            def evq(ps, pn, oc):
                if oc % 2:
                    ew("act", "copy", (arB[:, oc * HYP:(oc + 1) * HYP], ps),
                       [pn], [f"Bq{oc}"], f"{tag}qe")
                else:
                    ew("dve", "tensor_copy", (arB[:, oc * HYP:(oc + 1) * HYP], ps),
                       [pn], [f"Bq{oc}"], f"{tag}qe")
            proj_fm(Wq, wqc, 1024, first, evq, f"{tag}q", cpfx="cq")
        proj_vtm(first and not full, f"{tag}v")

        def evk(ps, pn, oc):
            if oc % 2:
                ew("act", "mul", (arB[:, 4096 + oc * HYP:4096 + (oc + 1) * HYP], ps, KSCALE),
                   [pn], [f"Bk{oc}"], f"{tag}ke")
            else:
                ew("dve", "tensor_scalar",
                   (arB[:, 4096 + oc * HYP:4096 + (oc + 1) * HYP], ps, KSCALE, None, ALU.mult),
                   [pn], [f"Bk{oc}"], f"{tag}ke")
        proj_fm(Wk, wkc, 1024, first and not full, evk, f"{tag}k", cpfx="ck")
        proj_z1(f"{tag}z")
        if stage == "proj":
            return
        ntc = 1 if stage in ("prep1", "chunk1") else TC
        for tc in range(ntc):
            scan_prep(tc, full, f"{tag}p{tc}_")
            if stage == "prep1":
                continue
            scan_chunk(tc, full, f"{tag}s{tc}_")
        if full:
            def evg(ps, pn, oc):
                slg = oc % 2
                act(AF.Silu, sgt[slg][:, :], ps, [pn, "bg_sb"], [f"sgt{slg}"],
                    bias=bg_sb[:, oc:oc + 1], tag=f"{tag}gs")
                dsl = arD[:, oc * HYP:(oc + 1) * HYP]
                ew("dve", "tensor_mul", (dsl, dsl, sgt[slg][:, :]),
                   [f"on{oc}", f"sgt{slg}"], [f"on{oc}"], f"{tag}gm")
            proj_fm(Wg, wgc, 2048, first, evg, f"{tag}g", cpfx="cg")
            tap(f"og_{tag}", arD[:, 0:PC * HYP], (128, PC * HYP), [f"on{i}" for i in range(PC)])

    def layer_norm(srcfn, dstfn, tag, store=None):
        for oc in range(PC):
            sap, sname = srcfn(oc)
            act(AF.Square, xsq[oc % 2][:, :], sap, [sname], [f"xsq{oc % 2}"], tag=f"{tag}q")
            mm(pb[4][0:1, 0:512], colb[:, 2:3], sap, oc == 0, oc == PC - 1,
               [sname, "colb"], ["pb4"], f"{tag}m")
            mm(pb[5][0:1, 0:512], colb[:, 2:3], xsq[oc % 2][:, :], oc == 0, oc == PC - 1,
               [f"xsq{oc % 2}", "colb"], ["pb5"], f"{tag}2")
        ew("dve", "tensor_copy", (scrf[0:1, 0:512], pb[4][0:1, 0:512]), ["pb4"], ["scrf"], f"{tag}c")
        ew("dve", "tensor_copy", (scrf[0:1, 512:1024], pb[5][0:1, 0:512]), ["pb5"], ["scrf"], f"{tag}c2")
        ew("dve", "tensor_mul", (scrb[0:1, 0:512], scrf[0:1, 0:512], scrf[0:1, 0:512]),
           ["scrf"], ["scrb"], f"{tag}s")
        ew("dve", "tensor_sub", (scrf[0:1, 512:1024], scrf[0:1, 512:1024], scrb[0:1, 0:512]),
           ["scrf", "scrb"], ["scrf"], f"{tag}v")
        act(AF.Ln, scrf[0:1, 512:1024], scrf[0:1, 512:1024], ["scrf", "actc"], ["scrf"], bias=acc_sb[0:1, 0:1], tag=f"{tag}l")
        act(AF.Exp, scrf[0:1, 512:1024], scrf[0:1, 512:1024], ["scrf"], ["scrf"], scale=-0.5, tag=f"{tag}e")
        mm(pb[6][:, 0:512], r4f[0:1, :], scrf[0:1, 0:512], True, True,
           ["scrf", "r4f"], ["pb6"], f"{tag}bm")
        ew("dve", "tensor_copy", (mubf[:, :], pb[6][:, 0:512]), ["pb6"], ["mubf"], f"{tag}bmc")
        mm(pb[6][:, 0:512], r4f[0:1, :], scrf[0:1, 512:1024], True, True,
           ["scrf", "r4f"], ["pb6"], f"{tag}br")
        ew("dve", "tensor_copy", (rstdb[:, :], pb[6][:, 0:512]), ["pb6"], ["rstdb"], f"{tag}brc")
        for oc in range(PC):
            sap, sname = srcfn(oc)
            dap, dname, xw = dstfn(oc)
            ew("dve", "tensor_sub", (dap, sap, mubf[:, :]), [sname, "mubf"], [dname, *xw], f"{tag}a1")
            ew("dve", "tensor_mul", (dap, dap, rstdb[:, :]), [dname, "rstdb"], [dname], f"{tag}a2")
            if store is not None:
                (oap,) = store(oc)
                dma("qs", oap, dap, [dname, *xw], [], f"{tag}st")

    def hh_ap(hc, name):
        g0 = hc * HYP
        other = "hh%d" % hc if name == "fh" else "fh%d" % hc
        if hc < 16:
            return arC[:, g0:g0 + HYP], f"{name}{hc}", [f"V{hc // 4}", other]
        if hc < 32:
            return (arD[:, g0 - 16 * HYP:g0 - 16 * HYP + HYP], f"{name}{hc}",
                    [f"on{hc - 16}", other])
        return (arE[:, g0 - 32 * HYP:g0 - 32 * HYP + HYP], f"{name}{hc}",
                ETRANS[:8] + [other])

    def p2_hyper(hyp, first, tag):
        t0 = hyp * HYP
        for oc in range(PC):
            ws = wsm[oc % 4]
            wn = f"wsm{oc % 4}"
            stream_w(first, Wo[oc, :, :], woc[oc, :, :], ws[:, :], wn, f"{tag}o{oc}", cres=f"co{oc}")
            xfo = XFS + (oc % 2) * HYP
            xfn = f"xf{oc % 2}"
            dma("qs", arF[:, xfo:xfo + HYP], xm[oc * 128:(oc + 1) * 128, t0:t0 + HYP],
                [], [xfn], f"{tag}ox")
            for dk in range(PC):
                mm(pb[oc % 4][:, 0:512], ws[:, dk * 128:(dk + 1) * 128],
                   arD[:, dk * HYP:(dk + 1) * HYP],
                   dk == 0, dk == PC - 1, [wn, f"on{dk}"], [f"pb{oc % 4}"], f"{tag}om")
            x1o = XFS + 1024 + (oc % 2) * HYP
            x1n = f"x1f{oc % 2}"
            ew("dve", "tensor_add", (arF[:, x1o:x1o + HYP], arF[:, xfo:xfo + HYP],
                                     pb[oc % 4][:, 0:512]),
               [xfn, f"pb{oc % 4}"], [x1n], f"{tag}x1")
            bxw = [f"Bq{oc}"] if oc < 8 else [f"Bk{oc - 8}"]
            ew("dve", "tensor_add", (arB[:, oc * HYP:(oc + 1) * HYP],
                                     arF[:, x1o:x1o + HYP], arF[:, xfo:xfo + HYP]),
               [x1n, xfn], [f"Bx{oc}", *bxw], f"{tag}xx")
            ew("dve", "tensor_copy", (arA[:, oc * HYP:(oc + 1) * HYP], arF[:, x1o:x1o + HYP]),
               [x1n], [f"A{oc}"], f"{tag}xb")
            act(AF.Square, xsq[oc % 2][:, :], arA[:, oc * HYP:(oc + 1) * HYP],
                [f"A{oc}"], [f"xsq{oc % 2}"], tag=f"{tag}rq")
            mm(pb[4][0:1, 0:512], colb[:, 2:3], xsq[oc % 2][:, :], oc == 0, oc == PC - 1,
               [f"xsq{oc % 2}", "colb"], ["pb4"], f"{tag}rs")
            drip_q(1)
        ew("dve", "tensor_copy", (rrow[0:1, :], pb[4][0:1, 0:512]), ["pb4"], ["rrow"], f"{tag}rc")
        act(AF.Ln, rrow[0:1, :], rrow[0:1, :], ["rrow", "actc"], ["rrow"], bias=acc_sb[0:1, 0:1], tag=f"{tag}rl")
        act(AF.Exp, rrow[0:1, :], rrow[0:1, :], ["rrow"], ["rrow"], scale=-0.5, tag=f"{tag}re")
        mm(pb[6][:, 0:512], r4f[0:1, :], rrow[0:1, :], True, True, ["r4f", "rrow"], ["pb6"], f"{tag}rb")
        ew("dve", "tensor_copy", (rstdb[:, :], pb[6][:, 0:512]), ["pb6"], ["rstdb"], f"{tag}rbc")
        for oc in range(PC):
            sl = arA[:, oc * HYP:(oc + 1) * HYP]
            ew("dve", "scalar_tensor_tensor",
               (sl, sl, ln2_sb[:, oc:oc + 1], rstdb[:, :], ALU.mult, ALU.mult),
               [f"A{oc}", "ln2_sb", "rstdb"], [f"A{oc}"], f"{tag}xn2")
        tap(f"xn2_{tag}", arA[:, 0:PC * HYP], (128, PC * HYP), [f"A{i}" for i in range(PC)])

        for hc in range(FC):
            ws = wsm[hc % 2]
            wn = f"wsm{hc % 2}"
            stream_w(first, W1[hc, :, :], w1c[hc, :, :], ws[:, :], wn, f"{tag}w1_{hc}", cres=f"cw1_{hc}")
            for dk in range(PC):
                mm(pb[hc % 2][:, 0:512], ws[:, dk * 128:(dk + 1) * 128],
                   arA[:, dk * HYP:(dk + 1) * HYP],
                   dk == 0, dk == PC - 1, [wn, f"A{dk}"], [f"pb{hc % 2}"], f"{tag}1m")
            act(AF.Silu, s1c[hc % 2][:, :], pb[hc % 2][:, 0:512], [f"pb{hc % 2}"],
                [f"s1c{hc % 2}"], tag=f"{tag}si")
            w3s = 2 + hc % 2
            stream_w(first, W3[hc, :, :], w3c[hc, :, :], wsm[w3s][:, :], f"wsm{w3s}", f"{tag}w3_{hc}", cres=f"cw3_{hc}")
            for dk in range(PC):
                mm(pb[2 + hc % 2][:, 0:512], wsm[w3s][:, dk * 128:(dk + 1) * 128],
                   arA[:, dk * HYP:(dk + 1) * HYP],
                   dk == 0, dk == PC - 1, [f"wsm{w3s}", f"A{dk}"], [f"pb{2 + hc % 2}"], f"{tag}3m")
            hap, hname, xw = hh_ap(hc, "hh")
            ew("dve", "tensor_mul", (hap, pb[2 + hc % 2][:, 0:512], s1c[hc % 2][:, :]),
               [f"pb{2 + hc % 2}", f"s1c{hc % 2}"], [hname, *xw], f"{tag}hh")
            drip_q(1)

        for oc in range(PC):
            for half in range(2):
                hsl = (oc * 2 + half) % 2
                w2b = arA[:, hsl * FH:(hsl + 1) * FH]
                w2n = f"Aw2{hsl}"
                affected = [f"A{k}" for k in range(hsl * 5, hsl * 5 + 6)]
                stream_w(first, W2[oc, :, half * FH:(half + 1) * FH],
                         w2c[oc, :, half * FH:(half + 1) * FH],
                         w2b, w2n, f"{tag}w2_{oc}_{half}", extra_writes=affected,
                         cres=f"cw2_{oc}_{half}")
                for dkh in range(22):
                    hap, hname, _ = hh_ap(half * 22 + dkh, "hh")
                    mm(pb[oc % 4][:, 0:512], w2b[:, dkh * 128:(dkh + 1) * 128],
                       hap, half == 0 and dkh == 0, half == 1 and dkh == 21,
                       [w2n, hname], [f"pb{oc % 4}"], f"{tag}2m")
            sl = arB[:, oc * HYP:(oc + 1) * HYP]
            ew("dve", "tensor_add", (sl, sl, pb[oc % 4][:, 0:512]),
               [f"Bx{oc}", f"pb{oc % 4}"], [f"Bx{oc}"], f"{tag}s")
            drip_q(2)

        layer_norm(lambda oc: (arB[:, oc * HYP:(oc + 1) * HYP], f"Bx{oc}"),
                   lambda oc: (arA[:, oc * HYP:(oc + 1) * HYP], f"A{oc}",
                               ["Aw20", "Aw21"] if oc < 12 else []),
                   f"{tag}L1")
        tap(f"out1_{tag}", arA[:, 0:PC * HYP], (128, PC * HYP), [f"A{i}" for i in range(PC)])

        if first:
            flush_q()
        for hc in range(FC):
            hs = hc % 2
            qsl = arB[:, hs * 2048:hs * 2048 + 2048]
            qn = f"Bqw{hs}"
            bxq = [f"Bx{4 * hs}", f"Bx{4 * hs + 1}", f"Bx{4 * hs + 2}", f"Bx{4 * hs + 3}"]
            dma("qs", qsl, qw1s[hc, :, :], [f"c1_{hc}_0", f"c1_{hc}_1"],
                [qn, *bxq], f"{tag}c1r")
            if hc == 0:
                tap(f"qw0_{tag}", arB[:, 0:2048], (128, 2048), [qn])
            for dk in range(PC):
                mm(pb[hs][:, 0:512], qsl[:, dk * 128:(dk + 1) * 128],
                   arA[:, dk * HYP:(dk + 1) * HYP],
                   dk == 0, dk == PC - 1, [qn, f"A{dk}"], [f"pb{hs}"], f"{tag}f1m")
            act(AF.Sigmoid, s1c[hs][:, :], pb[hs][:, 0:512], [f"pb{hs}"], [f"s1c{hs}"], tag=f"{tag}sg")
            dma("qs", qsl, qw2s[hc, :, :], [f"c2_{hc}_0", f"c2_{hc}_1"],
                [qn], f"{tag}c2r")
            for dk in range(PC):
                mm(pb[2 + hs][:, 0:512], qsl[:, dk * 128:(dk + 1) * 128],
                   arA[:, dk * HYP:(dk + 1) * HYP],
                   dk == 0, dk == PC - 1, [qn, f"A{dk}"], [f"pb{2 + hs}"], f"{tag}f2m")
            hap, hname, xw = hh_ap(hc, "fh")
            ew("dve", "tensor_mul", (hap, pb[2 + hs][:, 0:512], s1c[hs][:, :]),
               [f"pb{2 + hs}", f"s1c{hs}"], [hname, *xw], f"{tag}fh")
        tap(f"fh_{tag}", arC[:, 0:2048], (128, 2048), ["fh0", "fh1", "fh2", "fh3"])
        tap(f"qw1_{tag}", arB[:, 0:2048], (128, 2048), ["Bqw0", "Bqw1"])
        for oc in range(PC):
            for half in range(2):
                hsl = (oc * 2 + half) % 2
                q3 = arB[:, hsl * FH:(hsl + 1) * FH]
                qn3 = f"Bq3{hsl}"
                bx3 = [f"Bx{k}" for k in range(hsl * 5, hsl * 5 + 6)] + \
                      [f"Bqw{k}" for k in range(2)] + ["Btmp"]
                dma("qs", q3, qw3s[oc, :, half * FH:(half + 1) * FH],
                    [f"c3q_{oc}_{2 * half}", f"c3q_{oc}_{2 * half + 1}"],
                    [qn3, *bx3], f"{tag}c3r")
                for dkh in range(22):
                    hap, hname, _ = hh_ap(half * 22 + dkh, "fh")
                    mm(pb[oc % 4][:, 0:512], q3[:, dkh * 128:(dkh + 1) * 128], hap,
                       half == 0 and dkh == 0, half == 1 and dkh == 21,
                       [qn3, hname], [f"pb{oc % 4}"], f"{tag}f3m")
            ew("dve", "tensor_add", (s2b[:, oc * HYP:(oc + 1) * HYP],
                                     arA[:, oc * HYP:(oc + 1) * HYP], pb[oc % 4][:, 0:512]),
               [f"A{oc}", f"pb{oc % 4}"], [f"s2_{oc}"], f"{tag}s2")
            if oc == 0:
                tap(f"q3a_{tag}", arB[:, 0:FH], (128, FH), ["Bq30"])
                tap(f"q3b_{tag}", arB[:, FH:2 * FH], (128, FH), ["Bq31"])
        tap(f"s2_{tag}", s2b[:, 0:2048], (128, 2048), ["s2_0", "s2_1", "s2_2", "s2_3"])
        tap(f"fhC_{tag}", arC[:, 0:8192], (128, 8192), [f"fh{i}" for i in range(16)])
        tap(f"fhD_{tag}", arD[:, 0:8192], (128, 8192), [f"fh{i}" for i in range(16, 32)])
        tap(f"fhE_{tag}", arE[:, 0:6144], (128, 6144), [f"fh{i}" for i in range(32, 44)])
        layer_norm(lambda oc: (s2b[:, oc * HYP:(oc + 1) * HYP], f"s2_{oc}"),
                   lambda oc: (arF[:, (oc % 2) * HYP:(oc % 2) * HYP + HYP],
                               f"os{oc % 2}", ["Fzt", "Flg"]),
                   f"{tag}L2",
                   store=lambda oc: (out[oc * 128:(oc + 1) * 128, t0:t0 + HYP],))

    if not do_gamma:
        gamma_units.clear()
    for hyp in range(n_pre):
        p1_hyper(xp, hyp, False, hyp == 0, f"P{hyp}")
    if do_gamma:
        gamma_finish()
        make_qunits()
    else:
        gamma_units.clear()
        qpt[0] = NQ + 2
    tap("Spre", S_sb[:, 0:1024], (128, 1024), [f"S{h}" for h in range(16)], F32)
    for hyp in range(n_main):
        p1_hyper(xm, hyp, True, hyp == 0, f"M{hyp}")
        if do_p2:
            p2_hyper(hyp, hyp == 0, f"W{hyp}")
    if n_main == 0:
        dma("qs", out[0:128, 0:512], arF[:, 0:512], ["Fzt"], [], "dummyout")

    P.resolve(serial=serial)
    with nc.Block() as block:
        block.tensor(P.stream("pe", sems))
        block.scalar(P.stream("act", sems))
        block.vector(P.stream("dve", sems))
        block.gpsimd(P.stream("gp", sems))
        block.sync(P.stream("sync", sems))
    ctx.close()
    return nc


def _chunkW(W, nb):
    K, N = W.shape
    H = W.reshape(K // 128, 128, N // 128, 128).transpose(2, 1, 0, 3).reshape(N // 128, 128, K)
    assert H.shape[0] == nb
    return np.ascontiguousarray(H)


def _chunkV(W):
    H = W.reshape(16, 128, 4, 512).transpose(2, 1, 0, 3).reshape(4, 128, 8192)
    return np.ascontiguousarray(H)


def _prep_inputs(inputs):
    i = inputs
    f32 = np.float32
    common = {
        "Wq": _chunkW(np.asarray(i["L0_attn_Wq"], f32), 8),
        "Wk": _chunkW(np.asarray(i["L0_attn_Wk"], f32), 8),
        "Wkg1": np.ascontiguousarray(np.asarray(i["L0_attn_Wkg1"], f32)),
        "bkg2b": np.ascontiguousarray(np.repeat(np.asarray(i["L0_attn_bkg2"], f32).reshape(1, 1024), 128, 0)),
        "Wkg2p": np.ascontiguousarray(np.pad(np.asarray(i["L0_attn_Wkg2"], f32), ((0, 112), (0, 0)))),
        "Wv": _chunkV(np.asarray(i["L0_attn_Wv"], f32)),
        "Wg": _chunkW(np.asarray(i["L0_attn_Wg"], f32), 16),
        "bgc": np.ascontiguousarray(np.asarray(i["L0_attn_bg"], f32).reshape(PC, 128).T),
        "Wo": _chunkW(np.asarray(i["L0_attn_Wo"], f32), 16),
        "ln1c": np.ascontiguousarray(np.asarray(i["L0_attn_ln1_w"], f32).reshape(PC, 128).T),
        "ln2c": np.ascontiguousarray(np.asarray(i["L0_attn_ln2_w"], f32).reshape(PC, 128).T),
        "W1": _chunkW(np.asarray(i["L0_attn_W1"], f32), 44),
        "W3": _chunkW(np.asarray(i["L0_attn_W3"], f32), 44),
        "W2": _chunkW(np.asarray(i["L0_attn_W2"], f32), 16),
        "fw1T": _chunkW(np.ascontiguousarray(np.asarray(i["L0_ffn_w1"], f32).T), 44),
        "fw2T": _chunkW(np.ascontiguousarray(np.asarray(i["L0_ffn_w2"], f32).T), 44),
        "fw3T": _chunkW(np.ascontiguousarray(np.asarray(i["L0_ffn_w3"], f32).T), 16),
    }
    x = np.asarray(i["L0_attn_x"], f32)
    zeros = np.zeros((D, T), f32)
    in_maps = []
    for c in range(8):
        b, h = c // 2, c % 2
        m = dict(common)
        m["xm"] = np.ascontiguousarray(x[b, h * T:(h + 1) * T, :].T)
        m["xp"] = np.ascontiguousarray(x[b, 0:T, :].T) if h == 1 else zeros
        in_maps.append(m)
    return in_maps


def kernel(**inputs):
    in_maps = _prep_inputs(inputs)
    nc = build()
    res = run_bass_kernel_spmd(nc, in_maps, core_ids=list(range(8)))
    outp = np.empty((4, 4096, D), np.float32)
    for c in range(8):
        b, h = c // 2, c % 2
        outp[b, h * T:(h + 1) * T, :] = res.results[c]["out"].T
    return outp


# version 18
# speedup vs baseline: 1.1147x; 1.0033x over previous
import contextlib

import numpy as np
import ml_dtypes

import concourse.bass as bass
import concourse.mybir as mybir
from concourse.bass_utils import run_bass_kernel_spmd

F32 = mybir.dt.float32
BF16 = mybir.dt.bfloat16
AF = mybir.ActivationFunctionType
ALU = mybir.AluOpType
AX = mybir.AxisListType
BFNP = ml_dtypes.bfloat16

D = 2048
T = 2048
HYP = 512
NHYP = T // HYP
H = 16
KD = 64
HD = 128
C = 128
TC = HYP // C
PC = D // 128
QPC = (H * KD) // 128
FF = 5632
FC = FF // 128
FH = FF // 2
KSCALE = float(HD) ** -0.5
NFF = FF * D
EPS = 1e-5
ENGS = ("pe", "act", "dve", "gp", "sync")


class Prog:

    QR = {"qs": 10, "qg": 6, "qt": 8}

    def __init__(self):
        self.steps = []
        self.seq = dict.fromkeys(ENGS, 0)
        self.dseq = {"qs": 0, "qg": 0, "qt": 0}
        self.res = {}

    def _r(self, r):
        return self.res.setdefault(r, {"w": None, "rs": {}})

    def step(self, eng, emit, reads=(), writes=(), dma=None, tag=""):
        deps = []
        for r in reads:
            e = self._r(r)["w"]
            if e is not None:
                deps.append(e)
        for r in writes:
            d = self._r(r)
            if d["w"] is not None:
                deps.append(d["w"])
            deps.extend(d["rs"].items())
        if dma is not None:
            i = self.dseq[dma]
            self.dseq[dma] += 1
            R = self.QR[dma]
            sem = f"{dma}{i % R}"
            ev = (sem, 16 * (i // R + 1))
            if i // R > 0:
                deps.append((sem, 16 * (i // R)))
        else:
            self.seq[eng] += 1
            ev = (eng, self.seq[eng])
        self.steps.append(dict(eng=eng, emit=emit, deps=deps, dma=dma, ev=ev, tag=tag,
                               preads=tuple(reads)))
        for r in reads:
            rs = self._r(r)["rs"]
            if rs.get(ev[0], 0) < ev[1]:
                rs[ev[0]] = ev[1]
        for r in writes:
            d = self._r(r)
            d["w"] = ev
            d["rs"] = {ev[0]: ev[1]}
        return ev

    STRICT = ("dve", "act", "gp")

    def resolve(self, serial=False):
        if serial:
            prev = None
            for st in self.steps:
                if prev is not None:
                    st["deps"] = list(st["deps"]) + [prev]
                prev = st["ev"]
        loose = {}
        for st in self.steps:
            if st["dma"] is None and st["eng"] in self.STRICT:
                loose[st["ev"]] = True
        post_vc = {}
        vc = {e: {} for e in ENGS}
        for st in self.steps:
            eng = st["eng"]
            strict = eng in self.STRICT
            mv = vc[eng]
            waits = []
            for (s, v) in st["deps"]:
                if s == eng and st["dma"] is None and not loose.get((s, v), False):
                    continue
                if mv.get(s, 0) >= v:
                    continue
                waits.append((s, v))
                src = post_vc.get((s, v))
                if src:
                    for k2, v2 in src.items():
                        if mv.get(k2, 0) < v2:
                            mv[k2] = v2
                if mv.get(s, 0) < v:
                    mv[s] = v
            st["waits"] = waits
            s0, v0 = st["ev"]
            if st["dma"] is not None:
                pv = dict(mv)
                pv[s0] = max(pv.get(s0, 0), v0)
                post_vc[st["ev"]] = pv
            else:
                if mv.get(s0, 0) < v0 and not loose.get(st["ev"], False):
                    mv[s0] = v0
                pv = dict(mv)
                pv[s0] = max(pv.get(s0, 0), v0)
                post_vc[st["ev"]] = pv

    def stream(self, name, sems):
        mysteps = [st for st in self.steps if st["eng"] == name]
        dq = dict(self.dseq)

        def f(eng):
            for st in mysteps:
                for (s, v) in st["waits"]:
                    eng.wait_ge(sems[s], v)
                try:
                    inst = st["emit"](eng)
                except Exception as ex:
                    raise RuntimeError(f"step {st['tag']} failed: {ex}") from ex
                s0, _ = st["ev"]
                inst.then_inc(sems[s0], 16 if st["dma"] is not None else 1)
            if name == "sync":
                for q in ("qs", "qt"):
                    R = self.QR[q]
                    for j in range(min(R, dq[q])):
                        n = dq[q] // R + (1 if j < dq[q] % R else 0)
                        eng.wait_ge(sems[f"{q}{j}"], 16 * n)
            if name == "gp":
                q, R = "qg", self.QR["qg"]
                for j in range(min(R, dq[q])):
                    n = dq[q] // R + (1 if j < dq[q] % R else 0)
                    eng.wait_ge(sems[f"{q}{j}"], 16 * n)
        return f


def colap(t, start, step, count, pcount=128, pbase=0):
    fsz = int(np.prod(t.shape[1:]))
    return bass.AP(t, pbase * fsz + start, [[fsz, pcount], [step, count]])


def ap3(t, f0, dims, pcount=128, pbase=0):
    fsz = int(np.prod(t.shape[1:]))
    return bass.AP(t, pbase * fsz + f0, [[fsz, pcount]] + [list(d) for d in dims])


def build(debug_taps=(), n_pre=NHYP, n_main=NHYP, do_p2=True, do_gamma=True, stage="all", serial=False):
    taps = set(debug_taps)
    nc = bass.Bass("TRN2", target_bir_lowering=False, debug=False)

    xm = nc.dram_tensor("xm", [D, T], F32, kind="ExternalInput")
    xp = nc.dram_tensor("xp", [D, T], F32, kind="ExternalInput")
    Wq = nc.dram_tensor("Wq", [8, 128, D], F32, kind="ExternalInput")
    Wk = nc.dram_tensor("Wk", [8, 128, D], F32, kind="ExternalInput")
    Wkg1 = nc.dram_tensor("Wkg1", [D, 16], F32, kind="ExternalInput")
    bkg2b_d = nc.dram_tensor("bkg2b", [128, 1024], F32, kind="ExternalInput")
    wkg2p_d = nc.dram_tensor("Wkg2p", [128, 1024], F32, kind="ExternalInput")
    Wv = nc.dram_tensor("Wv", [4, 128, 8192], F32, kind="ExternalInput")
    Wg = nc.dram_tensor("Wg", [16, 128, D], F32, kind="ExternalInput")
    bgc_d = nc.dram_tensor("bgc", [128, PC], F32, kind="ExternalInput")
    Wo = nc.dram_tensor("Wo", [16, 128, D], F32, kind="ExternalInput")
    ln1_d = nc.dram_tensor("ln1c", [128, PC], F32, kind="ExternalInput")
    ln2_d = nc.dram_tensor("ln2c", [128, PC], F32, kind="ExternalInput")
    W1 = nc.dram_tensor("W1", [FC, 128, D], F32, kind="ExternalInput")
    W3 = nc.dram_tensor("W3", [FC, 128, D], F32, kind="ExternalInput")
    W2 = nc.dram_tensor("W2", [16, 128, FF], F32, kind="ExternalInput")
    fw1T = nc.dram_tensor("fw1T", [FC, 128, D], F32, kind="ExternalInput")
    fw2T = nc.dram_tensor("fw2T", [FC, 128, D], F32, kind="ExternalInput")
    fw3T = nc.dram_tensor("fw3T", [16, 128, FF], F32, kind="ExternalInput")
    out = nc.dram_tensor("out", [D, T], F32, kind="ExternalOutput")

    wqc = nc.dram_tensor("wqc", [8, 128, 2048], BF16, kind="Internal")
    wkc = nc.dram_tensor("wkc", [8, 128, 2048], BF16, kind="Internal")
    wvc = nc.dram_tensor("wvc", [4, 128, 8192], BF16, kind="Internal")
    wgc = nc.dram_tensor("wgc", [16, 128, 2048], BF16, kind="Internal")
    woc = nc.dram_tensor("woc", [16, 128, 2048], BF16, kind="Internal")
    w1c = nc.dram_tensor("w1c", [FC, 128, 2048], BF16, kind="Internal")
    w3c = nc.dram_tensor("w3c", [FC, 128, 2048], BF16, kind="Internal")
    w2c = nc.dram_tensor("w2c", [PC, 128, FF], BF16, kind="Internal")
    qw1s = nc.dram_tensor("qw1s", [FC, 128, 2048], BF16, kind="Internal")
    qw2s = nc.dram_tensor("qw2s", [FC, 128, 2048], BF16, kind="Internal")
    qw3s = nc.dram_tensor("qw3s", [PC, 128, FF], BF16, kind="Internal")

    rows4 = np.zeros((128, 128), np.float32)
    for p in (0, 32, 64, 96):
        rows4[p, :] = 1.0
    colsc = np.concatenate([np.ones((128, 1), np.float32),
                            np.full((128, 1), 1 / 128, np.float32),
                            np.full((128, 1), 1 / 2048, np.float32)], 1)
    identc = nc.inline_tensor(np.eye(128, dtype=BFNP), "identc")
    maskutc = nc.inline_tensor(np.triu(np.ones((128, 128), np.float32)).astype(BFNP), "maskutc")
    triu16c = nc.inline_tensor((np.triu(np.ones((128, 128), np.float32)) / 16.0).astype(np.float32), "triu16c")
    colsfc = nc.inline_tensor(colsc, "colsfc")
    colsbc = nc.inline_tensor(colsc.astype(BFNP), "colsbc")
    rows4fc = nc.inline_tensor(rows4, "rows4fc")
    actcc = nc.inline_tensor(np.concatenate([np.full((128, 1), EPS, np.float32),
                                             np.ones((128, 1), np.float32)], 1), "actcc")
    rows4bc = nc.inline_tensor(rows4.astype(BFNP), "rows4bc")

    AT = nc.alloc_sbuf_tensor
    arA = AT("arA", [128, PC * HYP], BF16)
    arB = AT("arB", [128, 2 * QPC * HYP], BF16)
    arC = AT("arC", [128, TC * D], BF16)
    arD = AT("arD", [128, PC * HYP], BF16)
    arE = AT("arE", [128, 10240], BF16)
    arF = AT("arF", [128, 6144], F32)
    gbA = AT("gbA", [128, FF], F32)
    s2b = AT("s2b", [128, PC * HYP], BF16)
    wsm = [AT(f"wsm{i}", [128, 2048], BF16) for i in range(4)]

    id_sb = AT("id_sb", [128, 128], BF16)
    mut_sb = AT("mut_sb", [128, 128], BF16)
    tri_sb = AT("tri_sb", [128, 128], F32)
    colb = AT("colb", [128, 3], BF16)
    colf = AT("colf", [128, 3], F32)
    r4f = AT("r4f", [128, 128], F32)
    r4b = AT("r4b", [128, 128], BF16)
    ln1_sb = AT("ln1_sb", [128, PC], F32)
    ln2_sb = AT("ln2_sb", [128, PC], F32)
    bg_sb = AT("bg_sb", [128, PC], F32)
    bk2b = AT("bk2b", [128, 1024], BF16)
    wkg2p_sb = AT("wkg2p_sb", [128, 1024], BF16)
    wkg1_sb = AT("wkg1_sb", [128, 256], BF16)
    thr_sb = AT("thr_sb", [128, 8], F32)
    gacc = AT("gacc", [128, 4], F32)
    gtmp = AT("gtmp", [128, 1], F32)
    acc_sb = AT("acc_sb", [128, 2], F32)
    z1b = AT("z1b", [16, HYP], BF16)
    S_sb = AT("S_sb", [128, 1024], F32)
    S_bf = AT("S_bf", [128, 1024], BF16)
    xsq = [AT(f"xsq{i}", [128, HYP], BF16) for i in range(2)]
    rrow = AT("rrow", [1, 512], F32)
    rstdb = AT("rstdb", [128, HYP], BF16)
    mubf = AT("mubf", [128, HYP], BF16)
    P_sb = AT("P_sb", [128, 2048], BF16)
    o_sb = AT("o_sb", [128, 2048], BF16)
    amid = AT("amid", [128, QPC], F32)
    namid = AT("namid", [128, QPC], F32)
    eacm = AT("eacm", [128, QPC], F32)
    eamid = AT("eamid", [128, QPC], F32)
    dcol = AT("dcol", [128, QPC], F32)
    scrb = AT("scrb", [128, 2048], BF16)
    scrf = AT("scrf", [128, 1024], F32)
    s1c = [AT(f"s1c{i}", [128, HYP], BF16) for i in range(2)]
    sgt = [AT(f"sgt{i}", [128, HYP], BF16) for i in range(2)]

    PS = nc.alloc_psum_tensor
    pb = [PS(f"pb{i}", [128, 512], F32) for i in range(7)]
    pT = PS("pT", [128, 1024], BF16)

    ctx = contextlib.ExitStack()
    semnames = ["pe", "act", "dve", "gp"]
    for q, R in Prog.QR.items():
        semnames += [f"{q}{j}" for j in range(R)]
    sems = {s: ctx.enter_context(nc.semaphore(f"sem_{s}")) for s in semnames}
    P = Prog()

    def dma(q, outap, inap, reads, writes, tag=""):
        eng = "sync" if q == "qs" else "gp"
        P.step(eng, lambda e, o=outap, i=inap: e.dma_start(out=o, in_=i),
               reads=reads, writes=writes, dma=q, tag=tag)

    def dmaT(outap, inap, reads, writes, tag=""):
        P.step("sync", lambda e, o=outap, i=inap: e.dma_start_transpose(o, i),
               reads=reads, writes=writes, dma="qt", tag=tag)

    def mm(outap, lhs, rhs, start, stop, reads, writes, tag=""):
        P.step("pe", lambda e, o=outap, a=lhs, b=rhs, s1=start, s2=stop:
               e.matmul(o, a, b, start=s1, stop=s2),
               reads=reads, writes=writes, tag=tag)

    def tr(outap, inap, idap, reads, writes, tag=""):
        P.step("pe", lambda e, o=outap, i=inap, d=idap: e.transpose(o, i, d),
               reads=reads, writes=writes, tag=tag)

    def act(fn, outap, inap, reads, writes, bias=0.0, scale=1.0, tag=""):
        P.step("act", lambda e, o=outap, i=inap, b=bias, s=scale, f=fn:
               e.activation(o, i, f, bias=b, scale=s),
               reads=reads, writes=writes, tag=tag)

    def ew(eng, name, args, reads, writes, tag=""):
        P.step(eng, lambda e, n=name, a=args: getattr(e, n)(*a),
               reads=reads, writes=writes, tag=tag)

    def tap(name, srcap, shape, reads, dtype=BF16):
        if name not in taps:
            return
        dt_ = nc.dram_tensor(f"tap_{name}", list(shape), dtype, kind="ExternalOutput")
        dma("qs", dt_[tuple(slice(0, s) for s in shape)], srcap, reads=reads,
            writes=[f"tap_{name}"], tag=f"tap_{name}")

    def d3(ap, n):
        return ap.rearrange("p (dk n) -> p dk n", n=n)

    def stream_w(first, srcf32, cache_ap, dest, destres, tag, extra_writes=(), n3=None):
        if first:
            dma("qg", dest, srcf32, [], [destres, *extra_writes], f"{tag}ld")
            dma("qs", cache_ap, dest, [destres], [f"c{tag}"], f"{tag}cw")
        else:
            dma("qs", dest, cache_ap, [f"c{tag}"], [destres, *extra_writes], f"{tag}cr")

    gamma_units = []

    def make_gamma_units():
        ui = [0]
        for (wt, nblk, cw, wi, nm) in ((fw1T, FC, D, 0, "g1"), (fw2T, FC, D, 1, "g2"),
                                       (fw3T, 16, FF, 2, "g3")):
            for r in range(nblk):
                def unit(r=r, wt=wt, cw=cw, wi=wi, nm=nm):
                    sl = ui[0] % 2
                    ui[0] += 1
                    b = gbA[:, sl * FH:sl * FH + cw] if cw <= FH else gbA[:, 0:cw]
                    bn = f"gbA{sl}" if cw <= FH else "gbA01"
                    xw = [] if cw <= FH else ["gbA0", "gbA1"]
                    dma("qs", b, wt[r, :, 0:cw], [], [bn, *xw], f"{nm}ld")
                    ew("dve", "tensor_reduce", (gtmp[:, 0:1], b, AX.X, ALU.add, True),
                       [bn], ["gtmp"], f"{nm}red")
                    ew("dve", "tensor_add", (gacc[:, wi:wi + 1], gacc[:, wi:wi + 1], gtmp[:, 0:1]),
                       ["gtmp", "gacc"], ["gacc"], f"{nm}acc")
                gamma_units.append(unit)

    def drip_gamma(n=1):
        for _ in range(n):
            if gamma_units:
                gamma_units.pop(0)()

    def gamma_finish():
        while gamma_units:
            drip_gamma()
        tap("gacc", gacc[:, 0:4], (128, 4), ["gacc"], F32)
        mm(pb[6][0:1, 0:3], colf[:, 0:1], gacc[:, 0:3], True, True,
           ["colf", "gacc"], ["pb6"], "gsum")
        ew("dve", "tensor_copy", (scrf[0:1, 0:3], pb[6][0:1, 0:3]), ["pb6"], ["scrf"])
        ew("dve", "tensor_scalar",
           (scrf[0:1, 4:7], scrf[0:1, 0:3], 0.5 / NFF, 0.5e-8, ALU.mult, ALU.add),
           ["scrf"], ["scrf"], "gthr")
        tap("scrf", scrf[0:1, 0:8], (1, 8), ["scrf"], F32)
        for wi in range(3):
            mm(pb[6][:, 0:1], r4f[0:1, :], scrf[0:1, 4 + wi:5 + wi], True, True,
               ["r4f", "scrf"], ["pb6"])
            ew("dve", "tensor_copy", (thr_sb[:, wi:wi + 1], pb[6][:, 0:1]),
               ["pb6"], ["thr_sb"])
        ew("dve", "tensor_scalar", (thr_sb[:, 4:7], thr_sb[:, 0:3], -1.0, None, ALU.mult),
           ["thr_sb"], ["thr_sb"])
        tap("thr", thr_sb[:, 0:8], (128, 8), ["thr_sb"], F32)

    dma("qs", id_sb[:, :], identc[:, :], [], ["id_sb"])
    dma("qs", mut_sb[:, :], maskutc[:, :], [], ["mut_sb"])
    dma("qs", tri_sb[:, :], triu16c[:, :], [], ["tri_sb"])
    dma("qs", colf[:, :], colsfc[:, :], [], ["colf"])
    dma("qs", colb[:, :], colsbc[:, :], [], ["colb"])
    dma("qs", r4f[:, :], rows4fc[:, :], [], ["r4f"])
    dma("qs", acc_sb[:, :], actcc[:, :], [], ["actc"])
    dma("qs", r4b[:, :], rows4bc[:, :], [], ["r4b"])
    dma("qs", ln1_sb[:, :], ln1_d[:, :], [], ["ln1_sb"])
    dma("qs", ln2_sb[:, :], ln2_d[:, :], [], ["ln2_sb"])
    dma("qs", bg_sb[:, :], bgc_d[:, :], [], ["bg_sb"])
    dma("qg", bk2b[:, :], bkg2b_d[:, :], [], ["bk2b"])
    dma("qg", wkg2p_sb[:, :], wkg2p_d[:, :], [], ["wkg2_sb"])
    dma("qg", wkg1_sb[:, :].rearrange("p (dk n) -> p dk n", n=16),
        Wkg1.ap().rearrange("(dk p) n -> p dk n", p=128), [], ["wkg1_sb"])
    ew("dve", "memset", (S_sb[:, :], 0.0), [], ["S_sb"] + [f"S{h}" for h in range(16)])
    ew("dve", "memset", (S_bf[:, :], 0.0), [], [f"Sb{h}" for h in range(16)])
    ew("dve", "memset", (gacc[:, :], 0.0), [], ["gacc"])
    make_gamma_units()

    ZT0, LG0, GK0, A20 = 0, 1024, 2048, 3072
    XFS = 4096
    EV0 = 0
    SQT, SQG, SKT, SKH = 0, 1024, 2048, 3072
    SKHT = 8192
    ETRANS = ["Ekt0", "Ekh0", "Eqt0", "Eqg0", "Ekt1", "Ekh1", "Eqt1", "Eqg1",
              "EkhT0", "EkhT1", "Evw"]

    def rms_xn(xdram, hyp, tag):
        t0 = hyp * HYP
        for pc in range(PC):
            xfo = XFS + (pc % 4) * HYP
            xfn = f"xf{pc % 4}"
            dma("qs", arF[:, xfo:xfo + HYP], xdram[pc * 128:(pc + 1) * 128, t0:t0 + HYP],
                [], [xfn], f"{tag}xld")
            if pc % 2:
                ew("act", "copy", (arA[:, pc * HYP:(pc + 1) * HYP], arF[:, xfo:xfo + HYP]),
                   [xfn, "x1f0", "x1f1"], [f"A{pc}"], f"{tag}xc")
            else:
                ew("dve", "tensor_copy", (arA[:, pc * HYP:(pc + 1) * HYP], arF[:, xfo:xfo + HYP]),
                   [xfn, "x1f0", "x1f1"], [f"A{pc}"], f"{tag}xc")
            act(AF.Square, xsq[pc % 2][:, :], arA[:, pc * HYP:(pc + 1) * HYP],
                [f"A{pc}"], [f"xsq{pc % 2}"], tag=f"{tag}sq")
            mm(pb[4][0:1, 0:512], colb[:, 2:3], xsq[pc % 2][:, :],
               pc == 0, pc == PC - 1, [f"xsq{pc % 2}", "colb"], ["pb4"], f"{tag}s2")
            drip_gamma(1)
        ew("dve", "tensor_copy", (rrow[0:1, :], pb[4][0:1, 0:512]), ["pb4"], ["rrow"], f"{tag}rc")
        act(AF.Ln, rrow[0:1, :], rrow[0:1, :], ["rrow", "actc"], ["rrow"], bias=acc_sb[0:1, 0:1], tag=f"{tag}ln")
        act(AF.Exp, rrow[0:1, :], rrow[0:1, :], ["rrow"], ["rrow"], scale=-0.5, tag=f"{tag}ex")
        mm(pb[6][:, 0:512], r4f[0:1, :], rrow[0:1, :], True, True,
           ["r4f", "rrow"], ["pb6"], f"{tag}bc")
        ew("dve", "tensor_copy", (rstdb[:, :], pb[6][:, 0:512]), ["pb6"], ["rstdb"], f"{tag}bcc")
        for pc in range(PC):
            sl = arA[:, pc * HYP:(pc + 1) * HYP]
            ew("dve", "scalar_tensor_tensor",
               (sl, sl, ln1_sb[:, pc:pc + 1], rstdb[:, :], ALU.mult, ALU.mult),
               [f"A{pc}", "ln1_sb", "rstdb"], [f"A{pc}"], f"{tag}xn")

    def proj_fm(wdram, cache, dout, first, evac, tag):
        for oc in range(dout // 128):
            ws = wsm[oc % 4]
            wn = f"wsm{oc % 4}"
            stream_w(first, wdram[oc, :, :], cache[oc, :, :], ws[:, :], wn, f"{tag}{oc}")
            for dk in range(PC):
                mm(pb[oc % 4][:, 0:512], ws[:, dk * 128:(dk + 1) * 128],
                   arA[:, dk * HYP:(dk + 1) * HYP],
                   dk == 0, dk == PC - 1, [wn, f"A{dk}"], [f"pb{oc % 4}"], f"{tag}mm")
            evac(pb[oc % 4][:, 0:512], f"pb{oc % 4}", oc)
            drip_gamma(1)

    def proj_vtm(first, tag):
        for js in range(4):
            for hf in range(2):
                dk0 = hf * 8
                stream_w(first, Wv[js, :, hf * 4096:(hf + 1) * 4096],
                         wvc[js, :, hf * 4096:(hf + 1) * 4096],
                         arE[:, EV0 + hf * 4096:EV0 + (hf + 1) * 4096], f"Evw{hf}",
                         f"{tag}{js}_{hf}", extra_writes=ETRANS[:8])
            for tc in range(TC):
                bk = tc % 4
                for dk in range(PC):
                    mm(pb[bk][:, 0:512],
                       arA[:, dk * HYP + tc * 128: dk * HYP + (tc + 1) * 128],
                       arE[:, EV0 + dk * 512:EV0 + (dk + 1) * 512],
                       dk == 0, dk == PC - 1, [f"Evw{dk // 8}", f"A{dk}"], [f"pb{bk}"], f"{tag}mm")
                ew("act" if tc % 2 else "dve", "copy" if tc % 2 else "tensor_copy",
                   (arC[:, tc * D + js * 512: tc * D + (js + 1) * 512], pb[bk][:, 0:512]),
                   [f"pb{bk}"], [f"V{tc}"], f"{tag}ev")

    def proj_z1(tag):
        for dk in range(PC):
            mm(pb[2][0:16, 0:512], wkg1_sb[:, dk * 16:(dk + 1) * 16],
               arA[:, dk * HYP:(dk + 1) * HYP],
               dk == 0, dk == PC - 1, ["wkg1_sb", f"A{dk}"], ["pb2"], f"{tag}mm")
        ew("dve", "tensor_copy", (z1b[0:16, :], pb[2][0:16, 0:512]), ["pb2"], ["z1b"], f"{tag}ev")

    def scan_prep(tc, full, tag):
        sl = tc % 2
        zt = arF[:, ZT0:ZT0 + 1024]
        lg = arF[:, LG0:LG0 + 1024]
        gk = arF[:, GK0:GK0 + 1024]
        for j in range(2):
            mm(pb[3 + j][:, 0:512], z1b[0:16, tc * 128:(tc + 1) * 128],
               wkg2p_sb[0:16, j * 512:(j + 1) * 512], True, True,
               ["z1b", "wkg2_sb"], [f"pb{3 + j}"], f"{tag}zm")
            ew("dve", "tensor_add", (arF[:, ZT0 + j * 512:ZT0 + (j + 1) * 512],
                                     pb[3 + j][:, 0:512], bk2b[:, j * 512:(j + 1) * 512]),
               [f"pb{3 + j}", "bk2b"], ["Fzt"], f"{tag}ze")
        act(AF.Abs, lg, zt, ["Fzt"], ["Flg"], tag=f"{tag}la")
        act(AF.Exp, lg, lg, ["Flg"], ["Flg"], scale=-1.0, tag=f"{tag}le")
        act(AF.Ln, lg, lg, ["Flg", "actc"], ["Flg"], bias=acc_sb[:, 1:2], tag=f"{tag}ll")
        ew("dve", "tensor_scalar", (gk, zt, 0.0, None, ALU.min), ["Fzt"], ["Fgk"], f"{tag}lm")
        ew("dve", "tensor_sub", (gk, gk, lg), ["Fgk", "Flg"], ["Fgk"], f"{tag}ls")
        for p8 in range(QPC):
            bk = 3 + (p8 % 2)
            mm(pb[bk][:, 0:128], arF[:, GK0 + p8 * 128:GK0 + (p8 + 1) * 128], tri_sb[:, :],
               True, True, ["Fgk", "tri_sb"], [f"pb{bk}"], f"{tag}Am")
            ew("dve", "tensor_copy", (arF[:, A20 + p8 * 128:A20 + (p8 + 1) * 128],
                                      pb[bk][:, 0:128]),
               [f"pb{bk}"], ["FA"], f"{tag}Ae")
        a63 = colap(arF, A20 + 63, 128, QPC)
        a127 = colap(arF, A20 + 127, 128, QPC)
        ew("dve", "tensor_copy", (amid[:, :], a63), ["FA"], ["amid"], f"{tag}am")
        act(AF.Exp, dcol[:, :], a127, ["FA"], ["dcol"], tag=f"{tag}de")
        if full:
            ew("dve", "tensor_scalar", (namid[:, :], amid[:, :], -1.0, None, ALU.mult),
               ["amid"], ["namid"], f"{tag}nm")
        kt0, kh0 = SKT + sl * 4096, SKH + sl * 4096
        qt0, qg0 = SQT + sl * 4096, SQG + sl * 4096
        ktn, khn, qtn, qgn = f"Ekt{sl}", f"Ekh{sl}", f"Eqt{sl}", f"Eqg{sl}"
        for p8 in range(QPC):
            asl = arF[:, A20 + p8 * 128:A20 + (p8 + 1) * 128]
            act(AF.Exp, scrb[:, p8 * 128:(p8 + 1) * 128], asl, ["FA", "amid"], ["scrb"],
                bias=amid[:, p8:p8 + 1], scale=-1.0, tag=f"{tag}Dk")
            act(AF.Exp, P_sb[:, p8 * 128:(p8 + 1) * 128], asl,
                ["FA"], ["Dk2", f"P{p8}"],
                bias=colap(arF, A20 + p8 * 128 + 127, 128, 1), scale=-1.0, tag=f"{tag}D2")
        kap = ap3(arB, 4096 + tc * 128, [(HYP, QPC), (1, 128)])
        dkap = ap3(scrb, 0, [(128, QPC), (1, 128)])
        dk2ap = ap3(P_sb, 0, [(128, QPC), (1, 128)])
        ktap = ap3(arE, kt0, [(128, QPC), (1, 128)])
        khap = ap3(arE, kh0, [(128, QPC), (1, 128)])
        ew("dve", "tensor_mul", (ktap, kap, dkap),
           [f"Bk{p8}" for p8 in range(QPC)] + ["scrb"], [ktn, "Evw0", "Evw1"], f"{tag}kt")
        ew("dve", "tensor_mul", (khap, kap, dk2ap),
           [f"Bk{p8}" for p8 in range(QPC)] + ["Dk2"], [khn, "Evw0", "Evw1"], f"{tag}kh")
        if full:
            for p8 in range(QPC):
                asl = arF[:, A20 + p8 * 128:A20 + (p8 + 1) * 128]
                act(AF.Exp, scrb[:, 1024 + p8 * 128:1024 + (p8 + 1) * 128], asl,
                    ["FA", "namid"], ["sDq"],
                    bias=namid[:, p8:p8 + 1], scale=1.0, tag=f"{tag}Dq")
                act(AF.Exp, o_sb[:, p8 * 128:(p8 + 1) * 128], asl,
                    ["FA"], ["Dg", f"ob{p8 // 4}"], scale=1.0, tag=f"{tag}Dg")
            qap = ap3(arB, tc * 128, [(HYP, QPC), (1, 128)])
            dqap = ap3(scrb, 1024, [(128, QPC), (1, 128)])
            dgap = ap3(o_sb, 0, [(128, QPC), (1, 128)])
            qtap = ap3(arE, qt0, [(128, QPC), (1, 128)])
            qgap = ap3(arE, qg0, [(128, QPC), (1, 128)])
            ew("dve", "tensor_mul", (qtap, qap, dqap),
               [f"Bq{p8}" for p8 in range(QPC)] + ["sDq"], [qtn, "Evw0", "Evw1"], f"{tag}qt")
            ew("dve", "tensor_mul", (qgap, qap, dgap),
               [f"Bq{p8}" for p8 in range(QPC)] + ["Dg"], [qgn, "Evw0", "Evw1"], f"{tag}qg")

    def scan_chunk(tc, full, tag):
        sl = tc % 2
        kt0, kh0 = SKT + sl * 4096, SKH + sl * 4096
        qt0, qg0 = SQT + sl * 4096, SQG + sl * 4096
        kht0 = SKHT + sl * 1024
        ktn, khn, qtn, qgn = f"Ekt{sl}", f"Ekh{sl}", f"Eqt{sl}", f"Eqg{sl}"
        khtn = f"EkhT{sl}"
        for p8 in range(QPC):
            dmaT(arE[:, kht0 + p8 * 128:kht0 + (p8 + 1) * 128],
                 arE[:, kh0 + p8 * 128:kh0 + (p8 + 1) * 128],
                 [khn], [khtn], f"{tag}T")
        if stage == "T1":
            return
        for g in range(4):
            ob = g % 2
            for h in range(4 * g, 4 * g + 4):
                p8, pbs = h // 2, (h % 2) * 64
                if full:
                    mm(pb[5][:, (h % 4) * 128:(h % 4) * 128 + 128],
                       arE[pbs:pbs + 64, kt0 + p8 * 128:kt0 + (p8 + 1) * 128],
                       arE[pbs:pbs + 64, qt0 + p8 * 128:qt0 + (p8 + 1) * 128],
                       True, True, [ktn, qtn], ["pb5"], f"{tag}sc")
                    ew("dve", "tensor_mul", (P_sb[:, h * 128:(h + 1) * 128],
                                             pb[5][:, (h % 4) * 128:(h % 4) * 128 + 128],
                                             mut_sb[:, :]),
                       ["pb5", "mut_sb"], [f"P{h}"], f"{tag}mk")
                    mm(pb[ob][:, (h % 4) * 128:(h % 4) * 128 + 128],
                       arC[:, tc * D + h * HD: tc * D + (h + 1) * HD],
                       P_sb[:, h * 128:(h + 1) * 128],
                       True, False, [f"V{tc}", f"P{h}"], [f"pb{ob}"], f"{tag}ov")
                    mm(pb[ob][:, (h % 4) * 128:(h % 4) * 128 + 128],
                       S_bf[pbs:pbs + 64, (h // 2) * 128:(h // 2 + 1) * 128],
                       arE[pbs:pbs + 64, qg0 + p8 * 128:qg0 + (p8 + 1) * 128],
                       False, True, [f"Sb{h}", qgn], [f"pb{ob}"], f"{tag}oi")
                upb = pb[2][pbs:pbs + 64, ((h // 2) % 4) * 128:((h // 2) % 4) * 128 + 128]
                mm(upb, arE[:, kht0 + h * 64:kht0 + (h + 1) * 64],
                   arC[:, tc * D + h * HD: tc * D + (h + 1) * HD],
                   True, True, [khtn, f"V{tc}"], ["pb2"], f"{tag}U")
                if stage == "U1":
                    continue
                Ssl = S_sb[pbs:pbs + 64, (h // 2) * 128:(h // 2 + 1) * 128]
                ew("dve", "scalar_tensor_tensor",
                   (Ssl, Ssl, dcol[pbs:pbs + 64, p8:p8 + 1], upb, ALU.mult, ALU.add),
                   [f"S{h}", "dcol", "pb2"], [f"S{h}"], f"{tag}Su")
                if stage != "Sup1":
                    ew("dve", "tensor_copy",
                       (S_bf[pbs:pbs + 64, (h // 2) * 128:(h // 2 + 1) * 128], Ssl),
                       [f"S{h}"], [f"Sb{h}"], f"{tag}Sb")
            if full:
                ew("dve", "tensor_copy", (o_sb[:, g * 512:(g + 1) * 512], pb[ob][:, 0:512]),
                   [f"pb{ob}"], [f"ob{g}"], f"{tag}oe")
        if full:
            head_ln(tc, tag)

    def head_ln(tc, tag):
        for j in range(2):
            act(AF.Square, P_sb[:, j * 1024:(j + 1) * 1024], o_sb[:, j * 1024:(j + 1) * 1024],
                [f"ob{2 * j}", f"ob{2 * j + 1}"],
                [f"Q{j}"] + [f"P{h}" for h in range(8 * j, 8 * j + 8)],
                tag=f"{tag}q2")
        for j4 in range(4):
            bk, pp = 3 + j4 // 2, (j4 % 2) * 32
            mm(pb[bk][pp:pp + 1, 0:512], colb[:, 1:2], o_sb[:, j4 * 512:(j4 + 1) * 512],
               True, True, [f"ob{j4}", "colb"], [f"pb{bk}"], f"{tag}mu")
            bk2 = 5 + j4 // 2
            mm(pb[bk2][pp:pp + 1, 0:512], colb[:, 1:2], P_sb[:, j4 * 512:(j4 + 1) * 512],
               True, True, [f"Q{j4 // 2}", "colb"], [f"pb{bk2}"], f"{tag}m2")
        for i, bk in enumerate((3, 4)):
            ew("dve", "tensor_copy", (scrb[0:64, i * 512:(i + 1) * 512], pb[bk][0:64, 0:512]),
               [f"pb{bk}"], ["scrb"], f"{tag}e1")
        for i, bk in enumerate((5, 6)):
            ew("dve", "tensor_copy", (scrb[0:64, 1024 + i * 512:1024 + (i + 1) * 512],
                                      pb[bk][0:64, 0:512]),
               [f"pb{bk}"], ["scrb"], f"{tag}e2")
        for j4 in range(4):
            i, pp = j4 // 2, (j4 % 2) * 32
            lhs = r4b[pp:pp + 1, :]
            mm(pb[3][:, 0:512], lhs, scrb[pp:pp + 1, i * 512:(i + 1) * 512], True, True,
               ["scrb", "r4b"], ["pb3"], f"{tag}bm")
            mm(pb[4][:, 0:512], lhs, scrb[pp:pp + 1, 1024 + i * 512:1024 + (i + 1) * 512],
               True, True, ["scrb", "r4b"], ["pb4"], f"{tag}b2")
            act(AF.Square, sgt[0][:, :], pb[3][:, 0:512], ["pb3"], ["sgt0"], tag=f"{tag}s")
            ew("dve", "tensor_sub", (scrf[:, 0:512], pb[4][:, 0:512], sgt[0][:, :]),
               ["pb4", "sgt0"], ["scrf"], f"{tag}v")
            act(AF.Ln, scrf[:, 0:512], scrf[:, 0:512], ["scrf", "actc"], ["scrf"], bias=acc_sb[:, 0:1], tag=f"{tag}vl")
            act(AF.Exp, sgt[1][:, :], scrf[:, 0:512], ["scrf"], ["sgt1"], scale=-0.5, tag=f"{tag}ve")
            ew("dve", "tensor_sub", (o_sb[:, j4 * 512:(j4 + 1) * 512],
                                     o_sb[:, j4 * 512:(j4 + 1) * 512], pb[3][:, 0:512]),
               [f"ob{j4}", "pb3"], [f"ob{j4}"], f"{tag}t1")
            dsl = ap3(arD, 4 * j4 * HYP + tc * 128, [(HYP, 4), (1, 128)])
            i0 = ap3(o_sb, j4 * 512, [(128, 4), (1, 128)])
            i1 = ap3(sgt[1], 0, [(128, 4), (1, 128)])
            ew("dve", "tensor_mul", (dsl, i0, i1),
               [f"ob{j4}", "sgt1"],
               [f"on{4 * j4}", f"on{4 * j4 + 1}", f"on{4 * j4 + 2}", f"on{4 * j4 + 3}"],
               f"{tag}on")

    def p1_hyper(xdram, hyp, full, first, tag):
        rms_xn(xdram, hyp, f"{tag}r")
        if stage == "rms":
            return
        tap(f"xn_{tag}", arA[:, 0:2048], (128, 2048), ["A0", "A1", "A2", "A3"])
        if full:
            def evq(ps, pn, oc):
                if oc % 2:
                    ew("act", "copy", (arB[:, oc * HYP:(oc + 1) * HYP], ps),
                       [pn], [f"Bq{oc}"], f"{tag}qe")
                else:
                    ew("dve", "tensor_copy", (arB[:, oc * HYP:(oc + 1) * HYP], ps),
                       [pn], [f"Bq{oc}"], f"{tag}qe")
            proj_fm(Wq, wqc, 1024, first, evq, f"{tag}q")
        proj_vtm(first, f"{tag}v")

        def evk(ps, pn, oc):
            if oc % 2:
                ew("act", "mul", (arB[:, 4096 + oc * HYP:4096 + (oc + 1) * HYP], ps, KSCALE),
                   [pn], [f"Bk{oc}"], f"{tag}ke")
            else:
                ew("dve", "tensor_scalar",
                   (arB[:, 4096 + oc * HYP:4096 + (oc + 1) * HYP], ps, KSCALE, None, ALU.mult),
                   [pn], [f"Bk{oc}"], f"{tag}ke")
        proj_fm(Wk, wkc, 1024, first, evk, f"{tag}k")
        proj_z1(f"{tag}z")
        if stage == "proj":
            return
        ntc = 1 if stage in ("prep1", "chunk1") else TC
        for tc in range(ntc):
            scan_prep(tc, full, f"{tag}p{tc}_")
            if stage == "prep1":
                continue
            scan_chunk(tc, full, f"{tag}s{tc}_")
        if full:
            def evg(ps, pn, oc):
                slg = oc % 2
                act(AF.Silu, sgt[slg][:, :], ps, [pn, "bg_sb"], [f"sgt{slg}"],
                    bias=bg_sb[:, oc:oc + 1], tag=f"{tag}gs")
                dsl = arD[:, oc * HYP:(oc + 1) * HYP]
                ew("dve", "tensor_mul", (dsl, dsl, sgt[slg][:, :]),
                   [f"on{oc}", f"sgt{slg}"], [f"on{oc}"], f"{tag}gm")
            proj_fm(Wg, wgc, 2048, first, evg, f"{tag}g")
            tap(f"og_{tag}", arD[:, 0:PC * HYP], (128, PC * HYP), [f"on{i}" for i in range(PC)])

    def layer_norm(srcfn, dstfn, tag, store=None):
        for oc in range(PC):
            sap, sname = srcfn(oc)
            act(AF.Square, xsq[oc % 2][:, :], sap, [sname], [f"xsq{oc % 2}"], tag=f"{tag}q")
            mm(pb[4][0:1, 0:512], colb[:, 2:3], sap, oc == 0, oc == PC - 1,
               [sname, "colb"], ["pb4"], f"{tag}m")
            mm(pb[5][0:1, 0:512], colb[:, 2:3], xsq[oc % 2][:, :], oc == 0, oc == PC - 1,
               [f"xsq{oc % 2}", "colb"], ["pb5"], f"{tag}2")
        ew("dve", "tensor_copy", (scrf[0:1, 0:512], pb[4][0:1, 0:512]), ["pb4"], ["scrf"], f"{tag}c")
        ew("dve", "tensor_copy", (scrf[0:1, 512:1024], pb[5][0:1, 0:512]), ["pb5"], ["scrf"], f"{tag}c2")
        ew("dve", "tensor_mul", (scrb[0:1, 0:512], scrf[0:1, 0:512], scrf[0:1, 0:512]),
           ["scrf"], ["scrb"], f"{tag}s")
        ew("dve", "tensor_sub", (scrf[0:1, 512:1024], scrf[0:1, 512:1024], scrb[0:1, 0:512]),
           ["scrf", "scrb"], ["scrf"], f"{tag}v")
        act(AF.Ln, scrf[0:1, 512:1024], scrf[0:1, 512:1024], ["scrf", "actc"], ["scrf"], bias=acc_sb[0:1, 0:1], tag=f"{tag}l")
        act(AF.Exp, scrf[0:1, 512:1024], scrf[0:1, 512:1024], ["scrf"], ["scrf"], scale=-0.5, tag=f"{tag}e")
        mm(pb[6][:, 0:512], r4f[0:1, :], scrf[0:1, 0:512], True, True,
           ["scrf", "r4f"], ["pb6"], f"{tag}bm")
        ew("dve", "tensor_copy", (mubf[:, :], pb[6][:, 0:512]), ["pb6"], ["mubf"], f"{tag}bmc")
        mm(pb[6][:, 0:512], r4f[0:1, :], scrf[0:1, 512:1024], True, True,
           ["scrf", "r4f"], ["pb6"], f"{tag}br")
        ew("dve", "tensor_copy", (rstdb[:, :], pb[6][:, 0:512]), ["pb6"], ["rstdb"], f"{tag}brc")
        for oc in range(PC):
            sap, sname = srcfn(oc)
            dap, dname, xw = dstfn(oc)
            ew("dve", "tensor_sub", (dap, sap, mubf[:, :]), [sname, "mubf"], [dname, *xw], f"{tag}a1")
            ew("dve", "tensor_mul", (dap, dap, rstdb[:, :]), [dname, "rstdb"], [dname], f"{tag}a2")
            if store is not None:
                (oap,) = store(oc)
                dma("qs", oap, dap, [dname, *xw], [], f"{tag}st")

    def hh_ap(hc, name):
        g0 = hc * HYP
        other = "hh%d" % hc if name == "fh" else "fh%d" % hc
        if hc < 16:
            return arC[:, g0:g0 + HYP], f"{name}{hc}", [f"V{hc // 4}", other]
        if hc < 32:
            return (arD[:, g0 - 16 * HYP:g0 - 16 * HYP + HYP], f"{name}{hc}",
                    [f"on{hc - 16}", other])
        return (arE[:, g0 - 32 * HYP:g0 - 32 * HYP + HYP], f"{name}{hc}",
                ETRANS[:8] + [other])

    def p2_hyper(hyp, first, tag):
        t0 = hyp * HYP
        for oc in range(PC):
            ws = wsm[oc % 4]
            wn = f"wsm{oc % 4}"
            stream_w(first, Wo[oc, :, :], woc[oc, :, :], ws[:, :], wn, f"{tag}o{oc}")
            xfo = XFS + (oc % 2) * HYP
            xfn = f"xf{oc % 2}"
            dma("qs", arF[:, xfo:xfo + HYP], xm[oc * 128:(oc + 1) * 128, t0:t0 + HYP],
                [], [xfn], f"{tag}ox")
            for dk in range(PC):
                mm(pb[oc % 4][:, 0:512], ws[:, dk * 128:(dk + 1) * 128],
                   arD[:, dk * HYP:(dk + 1) * HYP],
                   dk == 0, dk == PC - 1, [wn, f"on{dk}"], [f"pb{oc % 4}"], f"{tag}om")
            x1o = XFS + 1024 + (oc % 2) * HYP
            x1n = f"x1f{oc % 2}"
            ew("dve", "tensor_add", (arF[:, x1o:x1o + HYP], arF[:, xfo:xfo + HYP],
                                     pb[oc % 4][:, 0:512]),
               [xfn, f"pb{oc % 4}"], [x1n], f"{tag}x1")
            bxw = [f"Bq{oc}"] if oc < 8 else [f"Bk{oc - 8}"]
            ew("dve", "tensor_add", (arB[:, oc * HYP:(oc + 1) * HYP],
                                     arF[:, x1o:x1o + HYP], arF[:, xfo:xfo + HYP]),
               [x1n, xfn], [f"Bx{oc}", *bxw], f"{tag}xx")
            ew("act", "copy", (arA[:, oc * HYP:(oc + 1) * HYP], arF[:, x1o:x1o + HYP]),
               [x1n], [f"A{oc}"], f"{tag}xb")
            act(AF.Square, xsq[oc % 2][:, :], arA[:, oc * HYP:(oc + 1) * HYP],
                [f"A{oc}"], [f"xsq{oc % 2}"], tag=f"{tag}rq")
            mm(pb[4][0:1, 0:512], colb[:, 2:3], xsq[oc % 2][:, :], oc == 0, oc == PC - 1,
               [f"xsq{oc % 2}", "colb"], ["pb4"], f"{tag}rs")
        ew("dve", "tensor_copy", (rrow[0:1, :], pb[4][0:1, 0:512]), ["pb4"], ["rrow"], f"{tag}rc")
        act(AF.Ln, rrow[0:1, :], rrow[0:1, :], ["rrow", "actc"], ["rrow"], bias=acc_sb[0:1, 0:1], tag=f"{tag}rl")
        act(AF.Exp, rrow[0:1, :], rrow[0:1, :], ["rrow"], ["rrow"], scale=-0.5, tag=f"{tag}re")
        mm(pb[6][:, 0:512], r4f[0:1, :], rrow[0:1, :], True, True, ["r4f", "rrow"], ["pb6"], f"{tag}rb")
        ew("dve", "tensor_copy", (rstdb[:, :], pb[6][:, 0:512]), ["pb6"], ["rstdb"], f"{tag}rbc")
        for oc in range(PC):
            sl = arA[:, oc * HYP:(oc + 1) * HYP]
            ew("dve", "scalar_tensor_tensor",
               (sl, sl, ln2_sb[:, oc:oc + 1], rstdb[:, :], ALU.mult, ALU.mult),
               [f"A{oc}", "ln2_sb", "rstdb"], [f"A{oc}"], f"{tag}xn2")
        tap(f"xn2_{tag}", arA[:, 0:PC * HYP], (128, PC * HYP), [f"A{i}" for i in range(PC)])

        for hc in range(FC):
            ws = wsm[hc % 2]
            wn = f"wsm{hc % 2}"
            stream_w(first, W1[hc, :, :], w1c[hc, :, :], ws[:, :], wn, f"{tag}w1_{hc}")
            for dk in range(PC):
                mm(pb[hc % 2][:, 0:512], ws[:, dk * 128:(dk + 1) * 128],
                   arA[:, dk * HYP:(dk + 1) * HYP],
                   dk == 0, dk == PC - 1, [wn, f"A{dk}"], [f"pb{hc % 2}"], f"{tag}1m")
            act(AF.Silu, s1c[hc % 2][:, :], pb[hc % 2][:, 0:512], [f"pb{hc % 2}"],
                [f"s1c{hc % 2}"], tag=f"{tag}si")
            w3s = 2 + hc % 2
            stream_w(first, W3[hc, :, :], w3c[hc, :, :], wsm[w3s][:, :], f"wsm{w3s}", f"{tag}w3_{hc}")
            for dk in range(PC):
                mm(pb[2 + hc % 2][:, 0:512], wsm[w3s][:, dk * 128:(dk + 1) * 128],
                   arA[:, dk * HYP:(dk + 1) * HYP],
                   dk == 0, dk == PC - 1, [f"wsm{w3s}", f"A{dk}"], [f"pb{2 + hc % 2}"], f"{tag}3m")
            hap, hname, xw = hh_ap(hc, "hh")
            ew("dve", "tensor_mul", (hap, pb[2 + hc % 2][:, 0:512], s1c[hc % 2][:, :]),
               [f"pb{2 + hc % 2}", f"s1c{hc % 2}"], [hname, *xw], f"{tag}hh")

        for oc in range(PC):
            for half in range(2):
                hsl = (oc * 2 + half) % 2
                w2b = arA[:, hsl * FH:(hsl + 1) * FH]
                w2n = f"Aw2{hsl}"
                affected = [f"A{k}" for k in range(hsl * 5, hsl * 5 + 6)]
                stream_w(first, W2[oc, :, half * FH:(half + 1) * FH],
                         w2c[oc, :, half * FH:(half + 1) * FH],
                         w2b, w2n, f"{tag}w2_{oc}_{half}", extra_writes=affected)
                for dkh in range(22):
                    hap, hname, _ = hh_ap(half * 22 + dkh, "hh")
                    mm(pb[oc % 4][:, 0:512], w2b[:, dkh * 128:(dkh + 1) * 128],
                       hap, half == 0 and dkh == 0, half == 1 and dkh == 21,
                       [w2n, hname], [f"pb{oc % 4}"], f"{tag}2m")
            sl = arB[:, oc * HYP:(oc + 1) * HYP]
            ew("dve", "tensor_add", (sl, sl, pb[oc % 4][:, 0:512]),
               [f"Bx{oc}", f"pb{oc % 4}"], [f"Bx{oc}"], f"{tag}s")

        layer_norm(lambda oc: (arB[:, oc * HYP:(oc + 1) * HYP], f"Bx{oc}"),
                   lambda oc: (arA[:, oc * HYP:(oc + 1) * HYP], f"A{oc}",
                               ["Aw20", "Aw21"] if oc < 12 else []),
                   f"{tag}L1")
        tap(f"out1_{tag}", arA[:, 0:PC * HYP], (128, PC * HYP), [f"A{i}" for i in range(PC)])

        def quant(dst, dstn, srcap, srcn, wi, tmp, tmpn, tg, xw=()):
            ew("dve", "tensor_scalar", (tmp, srcap, thr_sb[:, 4 + wi:5 + wi], None, ALU.is_le),
               [srcn, "thr_sb"], [tmpn if isinstance(tmpn, str) else tmpn[0], *(() if isinstance(tmpn, str) else tmpn[1:])], f"{tg}l")
            ew("dve", "scalar_tensor_tensor", (dst, srcap, thr_sb[:, wi:wi + 1], tmp,
                                               ALU.is_ge, ALU.subtract),
               [srcn, "thr_sb", tmpn if isinstance(tmpn, str) else tmpn[0]], [dstn, *xw], f"{tg}g")

        for hc in range(FC):
            hs = hc % 2
            qsl = arB[:, hs * 2048:hs * 2048 + 2048]
            qn = f"Bqw{hs}"
            bxq = [f"Bx{4 * hs}", f"Bx{4 * hs + 1}", f"Bx{4 * hs + 2}", f"Bx{4 * hs + 3}"]
            tmp = arB[:, 4096:6144]
            if first:
                fo = hs * 2048
                fn_ = f"gbA{hs}"
                dma("qs", gbA[:, fo:fo + 2048], fw1T[hc, :, :], [], [fn_, "gbA0", "gbA1"], f"{tag}f1l")
                quant(qsl, qn, gbA[:, fo:fo + 2048], fn_, 0,
                      tmp, ["Btmp", "Bx8", "Bx9", "Bx10", "Bx11"], f"{tag}q1_{hc}", xw=bxq)
                dma("qs", qw1s[hc, :, :], qsl, [qn], [f"c1_{hc}"], f"{tag}c1w")
            else:
                dma("qs", qsl, qw1s[hc, :, :], [f"c1_{hc}"], [qn, *bxq], f"{tag}c1r")
            if hc == 0:
                tap(f"qw0_{tag}", arB[:, 0:2048], (128, 2048), [qn])
            for dk in range(PC):
                mm(pb[hs][:, 0:512], qsl[:, dk * 128:(dk + 1) * 128],
                   arA[:, dk * HYP:(dk + 1) * HYP],
                   dk == 0, dk == PC - 1, [qn, f"A{dk}"], [f"pb{hs}"], f"{tag}f1m")
            act(AF.Sigmoid, s1c[hs][:, :], pb[hs][:, 0:512], [f"pb{hs}"], [f"s1c{hs}"], tag=f"{tag}sg")
            if first:
                fo = hs * 2048
                fn_ = f"gbA{hs}"
                dma("qs", gbA[:, fo:fo + 2048], fw2T[hc, :, :], [], [fn_, "gbA0", "gbA1"], f"{tag}f2l")
                quant(qsl, qn, gbA[:, fo:fo + 2048], fn_, 1, tmp, "Btmp", f"{tag}q2_{hc}")
                dma("qs", qw2s[hc, :, :], qsl, [qn], [f"c2_{hc}"], f"{tag}c2w")
            else:
                dma("qs", qsl, qw2s[hc, :, :], [f"c2_{hc}"], [qn], f"{tag}c2r")
            for dk in range(PC):
                mm(pb[2 + hs][:, 0:512], qsl[:, dk * 128:(dk + 1) * 128],
                   arA[:, dk * HYP:(dk + 1) * HYP],
                   dk == 0, dk == PC - 1, [qn, f"A{dk}"], [f"pb{2 + hs}"], f"{tag}f2m")
            hap, hname, xw = hh_ap(hc, "fh")
            ew("dve", "tensor_mul", (hap, pb[2 + hs][:, 0:512], s1c[hs][:, :]),
               [f"pb{2 + hs}", f"s1c{hs}"], [hname, *xw], f"{tag}fh")
        tap(f"fh_{tag}", arC[:, 0:2048], (128, 2048), ["fh0", "fh1", "fh2", "fh3"])
        tap(f"qw1_{tag}", arB[:, 0:2048], (128, 2048), ["Bqw0", "Bqw1"])
        for oc in range(PC):
            for half in range(2):
                hsl = (oc * 2 + half) % 2
                q3 = arB[:, hsl * FH:(hsl + 1) * FH]
                qn3 = f"Bq3{hsl}"
                bx3 = [f"Bx{k}" for k in range(hsl * 5, hsl * 5 + 6)] + \
                      [f"Bqw{k}" for k in range(2)] + ["Btmp"]
                if first:
                    gsl = gbA[:, half * FH:(half + 1) * FH]
                    gn = f"gbA{half}"
                    dma("qs", gsl, fw3T[oc, :, half * FH:(half + 1) * FH],
                        [], [gn], f"{tag}f3l")
                    for piece in range(2):
                        quant(q3[:, piece * 1408:(piece + 1) * 1408], qn3,
                              gsl[:, piece * 1408:(piece + 1) * 1408], gn, 2,
                              scrb[:, 0:1408], "scrb", f"{tag}q3_{oc}_{half}_{piece}",
                              xw=bx3 if piece == 0 else ())
                    dma("qs", qw3s[oc, :, half * FH:(half + 1) * FH], q3, [qn3],
                        [f"c3_{oc}_{half}"], f"{tag}c3w")
                else:
                    dma("qs", q3, qw3s[oc, :, half * FH:(half + 1) * FH],
                        [f"c3_{oc}_{half}"], [qn3, *bx3], f"{tag}c3r")
                for dkh in range(22):
                    hap, hname, _ = hh_ap(half * 22 + dkh, "fh")
                    mm(pb[oc % 4][:, 0:512], q3[:, dkh * 128:(dkh + 1) * 128], hap,
                       half == 0 and dkh == 0, half == 1 and dkh == 21,
                       [qn3, hname], [f"pb{oc % 4}"], f"{tag}f3m")
            ew("dve", "tensor_add", (s2b[:, oc * HYP:(oc + 1) * HYP],
                                     arA[:, oc * HYP:(oc + 1) * HYP], pb[oc % 4][:, 0:512]),
               [f"A{oc}", f"pb{oc % 4}"], [f"s2_{oc}"], f"{tag}s2")
            if oc == 0:
                tap(f"q3a_{tag}", arB[:, 0:FH], (128, FH), ["Bq30"])
                tap(f"q3b_{tag}", arB[:, FH:2 * FH], (128, FH), ["Bq31"])
        tap(f"s2_{tag}", s2b[:, 0:2048], (128, 2048), ["s2_0", "s2_1", "s2_2", "s2_3"])
        tap(f"fhC_{tag}", arC[:, 0:8192], (128, 8192), [f"fh{i}" for i in range(16)])
        tap(f"fhD_{tag}", arD[:, 0:8192], (128, 8192), [f"fh{i}" for i in range(16, 32)])
        tap(f"fhE_{tag}", arE[:, 0:6144], (128, 6144), [f"fh{i}" for i in range(32, 44)])
        layer_norm(lambda oc: (s2b[:, oc * HYP:(oc + 1) * HYP], f"s2_{oc}"),
                   lambda oc: (arF[:, (oc % 2) * HYP:(oc % 2) * HYP + HYP],
                               f"os{oc % 2}", ["Fzt", "Flg"]),
                   f"{tag}L2",
                   store=lambda oc: (out[oc * 128:(oc + 1) * 128, t0:t0 + HYP],))

    if not do_gamma:
        gamma_units.clear()
    for hyp in range(n_pre):
        p1_hyper(xp, hyp, False, hyp == 0, f"P{hyp}")
    if do_gamma:
        gamma_finish()
    else:
        gamma_units.clear()
    tap("Spre", S_sb[:, 0:1024], (128, 1024), [f"S{h}" for h in range(16)], F32)
    for hyp in range(n_main):
        p1_hyper(xm, hyp, True, hyp == 0, f"M{hyp}")
        if do_p2:
            p2_hyper(hyp, hyp == 0, f"W{hyp}")
    if n_main == 0:
        dma("qs", out[0:128, 0:512], arF[:, 0:512], ["Fzt"], [], "dummyout")

    P.resolve(serial=serial)
    with nc.Block() as block:
        block.tensor(P.stream("pe", sems))
        block.scalar(P.stream("act", sems))
        block.vector(P.stream("dve", sems))
        block.gpsimd(P.stream("gp", sems))
        block.sync(P.stream("sync", sems))
    ctx.close()
    return nc


def _chunkW(W, nb):
    K, N = W.shape
    H = W.reshape(K // 128, 128, N // 128, 128).transpose(2, 1, 0, 3).reshape(N // 128, 128, K)
    assert H.shape[0] == nb
    return np.ascontiguousarray(H)


def _chunkV(W):
    H = W.reshape(16, 128, 4, 512).transpose(2, 1, 0, 3).reshape(4, 128, 8192)
    return np.ascontiguousarray(H)


def _prep_inputs(inputs):
    i = inputs
    f32 = np.float32
    common = {
        "Wq": _chunkW(np.asarray(i["L0_attn_Wq"], f32), 8),
        "Wk": _chunkW(np.asarray(i["L0_attn_Wk"], f32), 8),
        "Wkg1": np.ascontiguousarray(np.asarray(i["L0_attn_Wkg1"], f32)),
        "bkg2b": np.ascontiguousarray(np.repeat(np.asarray(i["L0_attn_bkg2"], f32).reshape(1, 1024), 128, 0)),
        "Wkg2p": np.ascontiguousarray(np.pad(np.asarray(i["L0_attn_Wkg2"], f32), ((0, 112), (0, 0)))),
        "Wv": _chunkV(np.asarray(i["L0_attn_Wv"], f32)),
        "Wg": _chunkW(np.asarray(i["L0_attn_Wg"], f32), 16),
        "bgc": np.ascontiguousarray(np.asarray(i["L0_attn_bg"], f32).reshape(PC, 128).T),
        "Wo": _chunkW(np.asarray(i["L0_attn_Wo"], f32), 16),
        "ln1c": np.ascontiguousarray(np.asarray(i["L0_attn_ln1_w"], f32).reshape(PC, 128).T),
        "ln2c": np.ascontiguousarray(np.asarray(i["L0_attn_ln2_w"], f32).reshape(PC, 128).T),
        "W1": _chunkW(np.asarray(i["L0_attn_W1"], f32), 44),
        "W3": _chunkW(np.asarray(i["L0_attn_W3"], f32), 44),
        "W2": _chunkW(np.asarray(i["L0_attn_W2"], f32), 16),
        "fw1T": _chunkW(np.ascontiguousarray(np.asarray(i["L0_ffn_w1"], f32).T), 44),
        "fw2T": _chunkW(np.ascontiguousarray(np.asarray(i["L0_ffn_w2"], f32).T), 44),
        "fw3T": _chunkW(np.ascontiguousarray(np.asarray(i["L0_ffn_w3"], f32).T), 16),
    }
    x = np.asarray(i["L0_attn_x"], f32)
    zeros = np.zeros((D, T), f32)
    in_maps = []
    for c in range(8):
        b, h = c // 2, c % 2
        m = dict(common)
        m["xm"] = np.ascontiguousarray(x[b, h * T:(h + 1) * T, :].T)
        m["xp"] = np.ascontiguousarray(x[b, 0:T, :].T) if h == 1 else zeros
        in_maps.append(m)
    return in_maps


def kernel(**inputs):
    in_maps = _prep_inputs(inputs)
    nc = build()
    res = run_bass_kernel_spmd(nc, in_maps, core_ids=list(range(8)))
    outp = np.empty((4, 4096, D), np.float32)
    for c in range(8):
        b, h = c // 2, c % 2
        outp[b, h * T:(h + 1) * T, :] = res.results[c]["out"].T
    return outp


# version 19 (final)
# speedup vs baseline: 1.1325x; 1.0159x over previous
import contextlib

import numpy as np
import ml_dtypes

import concourse.bass as bass
import concourse.mybir as mybir
from concourse.bass_utils import run_bass_kernel_spmd

F32 = mybir.dt.float32
BF16 = mybir.dt.bfloat16
AF = mybir.ActivationFunctionType
ALU = mybir.AluOpType
AX = mybir.AxisListType
BFNP = ml_dtypes.bfloat16

D = 2048
T = 2048
HYP = 512
NHYP = T // HYP
H = 16
KD = 64
HD = 128
C = 128
TC = HYP // C
PC = D // 128
QPC = (H * KD) // 128
FF = 5632
FC = FF // 128
FH = FF // 2
KSCALE = float(HD) ** -0.5
NFF = FF * D
EPS = 1e-5
ENGS = ("pe", "act", "dve", "gp", "sync")


class Prog:

    QR = {"qs": 10, "qg": 6, "qt": 8}

    def __init__(self):
        self.steps = []
        self.seq = dict.fromkeys(ENGS, 0)
        self.dseq = {"qs": 0, "qg": 0, "qt": 0}
        self.res = {}

    def _r(self, r):
        return self.res.setdefault(r, {"w": None, "rs": {}})

    def step(self, eng, emit, reads=(), writes=(), dma=None, tag=""):
        deps = []
        for r in reads:
            e = self._r(r)["w"]
            if e is not None:
                deps.append(e)
        for r in writes:
            d = self._r(r)
            if d["w"] is not None:
                deps.append(d["w"])
            deps.extend(d["rs"].items())
        if dma is not None:
            i = self.dseq[dma]
            self.dseq[dma] += 1
            R = self.QR[dma]
            sem = f"{dma}{i % R}"
            ev = (sem, 16 * (i // R + 1))
            if i // R > 0:
                deps.append((sem, 16 * (i // R)))
        else:
            self.seq[eng] += 1
            ev = (eng, self.seq[eng])
        self.steps.append(dict(eng=eng, emit=emit, deps=deps, dma=dma, ev=ev, tag=tag,
                               preads=tuple(reads)))
        for r in reads:
            rs = self._r(r)["rs"]
            if rs.get(ev[0], 0) < ev[1]:
                rs[ev[0]] = ev[1]
        for r in writes:
            d = self._r(r)
            d["w"] = ev
            d["rs"] = {ev[0]: ev[1]}
        return ev

    STRICT = ("dve", "act", "gp")

    def resolve(self, serial=False):
        if serial:
            prev = None
            for st in self.steps:
                if prev is not None:
                    st["deps"] = list(st["deps"]) + [prev]
                prev = st["ev"]
        loose = {}
        for st in self.steps:
            if st["dma"] is None and st["eng"] in self.STRICT:
                loose[st["ev"]] = True
        post_vc = {}
        vc = {e: {} for e in ENGS}
        for st in self.steps:
            eng = st["eng"]
            strict = eng in self.STRICT
            mv = vc[eng]
            waits = []
            for (s, v) in st["deps"]:
                if s == eng and st["dma"] is None and not loose.get((s, v), False):
                    continue
                if mv.get(s, 0) >= v:
                    continue
                waits.append((s, v))
                src = post_vc.get((s, v))
                if src:
                    for k2, v2 in src.items():
                        if mv.get(k2, 0) < v2:
                            mv[k2] = v2
                if mv.get(s, 0) < v:
                    mv[s] = v
            st["waits"] = waits
            s0, v0 = st["ev"]
            if st["dma"] is not None:
                pv = dict(mv)
                pv[s0] = max(pv.get(s0, 0), v0)
                post_vc[st["ev"]] = pv
            else:
                if mv.get(s0, 0) < v0 and not loose.get(st["ev"], False):
                    mv[s0] = v0
                pv = dict(mv)
                pv[s0] = max(pv.get(s0, 0), v0)
                post_vc[st["ev"]] = pv

    def stream(self, name, sems):
        mysteps = [st for st in self.steps if st["eng"] == name]
        dq = dict(self.dseq)

        def f(eng):
            for st in mysteps:
                for (s, v) in st["waits"]:
                    eng.wait_ge(sems[s], v)
                try:
                    inst = st["emit"](eng)
                except Exception as ex:
                    raise RuntimeError(f"step {st['tag']} failed: {ex}") from ex
                s0, _ = st["ev"]
                inst.then_inc(sems[s0], 16 if st["dma"] is not None else 1)
            if name == "sync":
                for q in ("qs", "qt"):
                    R = self.QR[q]
                    for j in range(min(R, dq[q])):
                        n = dq[q] // R + (1 if j < dq[q] % R else 0)
                        eng.wait_ge(sems[f"{q}{j}"], 16 * n)
            if name == "gp":
                q, R = "qg", self.QR["qg"]
                for j in range(min(R, dq[q])):
                    n = dq[q] // R + (1 if j < dq[q] % R else 0)
                    eng.wait_ge(sems[f"{q}{j}"], 16 * n)
        return f


def colap(t, start, step, count, pcount=128, pbase=0):
    fsz = int(np.prod(t.shape[1:]))
    return bass.AP(t, pbase * fsz + start, [[fsz, pcount], [step, count]])


def ap3(t, f0, dims, pcount=128, pbase=0):
    fsz = int(np.prod(t.shape[1:]))
    return bass.AP(t, pbase * fsz + f0, [[fsz, pcount]] + [list(d) for d in dims])


def build(debug_taps=(), n_pre=NHYP, n_main=NHYP, do_p2=True, do_gamma=True, stage="all", serial=False):
    taps = set(debug_taps)
    nc = bass.Bass("TRN2", target_bir_lowering=False, debug=False)

    xm = nc.dram_tensor("xm", [D, T], F32, kind="ExternalInput")
    xp = nc.dram_tensor("xp", [D, T], F32, kind="ExternalInput")
    Wq = nc.dram_tensor("Wq", [8, 128, D], F32, kind="ExternalInput")
    Wk = nc.dram_tensor("Wk", [8, 128, D], F32, kind="ExternalInput")
    Wkg1 = nc.dram_tensor("Wkg1", [D, 16], F32, kind="ExternalInput")
    bkg2b_d = nc.dram_tensor("bkg2b", [128, 1024], F32, kind="ExternalInput")
    wkg2p_d = nc.dram_tensor("Wkg2p", [128, 1024], F32, kind="ExternalInput")
    Wv = nc.dram_tensor("Wv", [4, 128, 8192], F32, kind="ExternalInput")
    Wg = nc.dram_tensor("Wg", [16, 128, D], F32, kind="ExternalInput")
    bgc_d = nc.dram_tensor("bgc", [128, PC], F32, kind="ExternalInput")
    Wo = nc.dram_tensor("Wo", [16, 128, D], F32, kind="ExternalInput")
    ln1_d = nc.dram_tensor("ln1c", [128, PC], F32, kind="ExternalInput")
    ln2_d = nc.dram_tensor("ln2c", [128, PC], F32, kind="ExternalInput")
    W1 = nc.dram_tensor("W1", [FC, 128, D], F32, kind="ExternalInput")
    W3 = nc.dram_tensor("W3", [FC, 128, D], F32, kind="ExternalInput")
    W2 = nc.dram_tensor("W2", [16, 128, FF], F32, kind="ExternalInput")
    fw1T = nc.dram_tensor("fw1T", [FC, 128, D], F32, kind="ExternalInput")
    fw2T = nc.dram_tensor("fw2T", [FC, 128, D], F32, kind="ExternalInput")
    fw3T = nc.dram_tensor("fw3T", [16, 128, FF], F32, kind="ExternalInput")
    out = nc.dram_tensor("out", [D, T], F32, kind="ExternalOutput")

    wqc = nc.dram_tensor("wqc", [8, 128, 2048], BF16, kind="Internal")
    wkc = nc.dram_tensor("wkc", [8, 128, 2048], BF16, kind="Internal")
    wvc = nc.dram_tensor("wvc", [4, 128, 8192], BF16, kind="Internal")
    wgc = nc.dram_tensor("wgc", [16, 128, 2048], BF16, kind="Internal")
    woc = nc.dram_tensor("woc", [16, 128, 2048], BF16, kind="Internal")
    w1c = nc.dram_tensor("w1c", [FC, 128, 2048], BF16, kind="Internal")
    w3c = nc.dram_tensor("w3c", [FC, 128, 2048], BF16, kind="Internal")
    w2c = nc.dram_tensor("w2c", [PC, 128, FF], BF16, kind="Internal")
    qw1s = nc.dram_tensor("qw1s", [FC, 128, 2048], BF16, kind="Internal")
    qw2s = nc.dram_tensor("qw2s", [FC, 128, 2048], BF16, kind="Internal")
    qw3s = nc.dram_tensor("qw3s", [PC, 128, FF], BF16, kind="Internal")

    rows4 = np.zeros((128, 128), np.float32)
    for p in (0, 32, 64, 96):
        rows4[p, :] = 1.0
    colsc = np.concatenate([np.ones((128, 1), np.float32),
                            np.full((128, 1), 1 / 128, np.float32),
                            np.full((128, 1), 1 / 2048, np.float32)], 1)
    identc = nc.inline_tensor(np.eye(128, dtype=BFNP), "identc")
    maskutc = nc.inline_tensor(np.triu(np.ones((128, 128), np.float32)).astype(BFNP), "maskutc")
    triu16c = nc.inline_tensor((np.triu(np.ones((128, 128), np.float32)) / 16.0).astype(np.float32), "triu16c")
    colsfc = nc.inline_tensor(colsc, "colsfc")
    colsbc = nc.inline_tensor(colsc.astype(BFNP), "colsbc")
    rows4fc = nc.inline_tensor(rows4, "rows4fc")
    actcc = nc.inline_tensor(np.concatenate([np.full((128, 1), EPS, np.float32),
                                             np.ones((128, 1), np.float32)], 1), "actcc")
    rows4bc = nc.inline_tensor(rows4.astype(BFNP), "rows4bc")

    AT = nc.alloc_sbuf_tensor
    arA = AT("arA", [128, PC * HYP], BF16)
    arB = AT("arB", [128, 2 * QPC * HYP], BF16)
    arC = AT("arC", [128, TC * D], BF16)
    arD = AT("arD", [128, PC * HYP], BF16)
    arE = AT("arE", [128, 10240], BF16)
    arF = AT("arF", [128, 6144], F32)
    gbA = AT("gbA", [128, FF], F32)
    s2b = AT("s2b", [128, PC * HYP], BF16)
    wsm = [AT(f"wsm{i}", [128, 2048], BF16) for i in range(4)]

    id_sb = AT("id_sb", [128, 128], BF16)
    mut_sb = AT("mut_sb", [128, 128], BF16)
    tri_sb = AT("tri_sb", [128, 128], F32)
    colb = AT("colb", [128, 3], BF16)
    colf = AT("colf", [128, 3], F32)
    r4f = AT("r4f", [128, 128], F32)
    r4b = AT("r4b", [128, 128], BF16)
    ln1_sb = AT("ln1_sb", [128, PC], F32)
    ln2_sb = AT("ln2_sb", [128, PC], F32)
    bg_sb = AT("bg_sb", [128, PC], F32)
    bk2b = AT("bk2b", [128, 1024], BF16)
    wkg2p_sb = AT("wkg2p_sb", [128, 1024], BF16)
    wkg1_sb = AT("wkg1_sb", [128, 256], BF16)
    thr_sb = AT("thr_sb", [128, 8], F32)
    gacc = AT("gacc", [128, 4], F32)
    gtmp = AT("gtmp", [128, 1], F32)
    acc_sb = AT("acc_sb", [128, 2], F32)
    z1b = AT("z1b", [16, HYP], BF16)
    S_sb = AT("S_sb", [128, 1024], F32)
    S_bf = AT("S_bf", [128, 1024], BF16)
    xsq = [AT(f"xsq{i}", [128, HYP], BF16) for i in range(2)]
    rrow = AT("rrow", [1, 512], F32)
    rstdb = AT("rstdb", [128, HYP], BF16)
    mubf = AT("mubf", [128, HYP], BF16)
    P_sb = AT("P_sb", [128, 2048], BF16)
    o_sb = AT("o_sb", [128, 2048], BF16)
    amid = AT("amid", [128, QPC], F32)
    namid = AT("namid", [128, QPC], F32)
    eacm = AT("eacm", [128, QPC], F32)
    eamid = AT("eamid", [128, QPC], F32)
    dcol = AT("dcol", [128, QPC], F32)
    scrb = AT("scrb", [128, 2048], BF16)
    scrf = AT("scrf", [128, 1024], F32)
    s1c = [AT(f"s1c{i}", [128, HYP], BF16) for i in range(2)]
    sgt = [AT(f"sgt{i}", [128, HYP], BF16) for i in range(2)]

    PS = nc.alloc_psum_tensor
    pb = [PS(f"pb{i}", [128, 512], F32) for i in range(7)]
    pT = PS("pT", [128, 1024], BF16)

    ctx = contextlib.ExitStack()
    semnames = ["pe", "act", "dve", "gp"]
    for q, R in Prog.QR.items():
        semnames += [f"{q}{j}" for j in range(R)]
    sems = {s: ctx.enter_context(nc.semaphore(f"sem_{s}")) for s in semnames}
    P = Prog()

    def dma(q, outap, inap, reads, writes, tag=""):
        eng = "sync" if q == "qs" else "gp"
        P.step(eng, lambda e, o=outap, i=inap: e.dma_start(out=o, in_=i),
               reads=reads, writes=writes, dma=q, tag=tag)

    def dmaT(outap, inap, reads, writes, tag=""):
        P.step("sync", lambda e, o=outap, i=inap: e.dma_start_transpose(o, i),
               reads=reads, writes=writes, dma="qt", tag=tag)

    def mm(outap, lhs, rhs, start, stop, reads, writes, tag=""):
        P.step("pe", lambda e, o=outap, a=lhs, b=rhs, s1=start, s2=stop:
               e.matmul(o, a, b, start=s1, stop=s2),
               reads=reads, writes=writes, tag=tag)

    def tr(outap, inap, idap, reads, writes, tag=""):
        P.step("pe", lambda e, o=outap, i=inap, d=idap: e.transpose(o, i, d),
               reads=reads, writes=writes, tag=tag)

    def act(fn, outap, inap, reads, writes, bias=0.0, scale=1.0, tag=""):
        P.step("act", lambda e, o=outap, i=inap, b=bias, s=scale, f=fn:
               e.activation(o, i, f, bias=b, scale=s),
               reads=reads, writes=writes, tag=tag)

    def ew(eng, name, args, reads, writes, tag=""):
        P.step(eng, lambda e, n=name, a=args: getattr(e, n)(*a),
               reads=reads, writes=writes, tag=tag)

    def tap(name, srcap, shape, reads, dtype=BF16):
        if name not in taps:
            return
        dt_ = nc.dram_tensor(f"tap_{name}", list(shape), dtype, kind="ExternalOutput")
        dma("qs", dt_[tuple(slice(0, s) for s in shape)], srcap, reads=reads,
            writes=[f"tap_{name}"], tag=f"tap_{name}")

    def d3(ap, n):
        return ap.rearrange("p (dk n) -> p dk n", n=n)

    def stream_w(first, srcf32, cache_ap, dest, destres, tag, extra_writes=(), n3=None):
        if first:
            dma("qg", dest, srcf32, [], [destres, *extra_writes], f"{tag}ld")
            dma("qs", cache_ap, dest, [destres], [f"c{tag}"], f"{tag}cw")
        else:
            dma("qs", dest, cache_ap, [f"c{tag}"], [destres, *extra_writes], f"{tag}cr")

    gamma_units = []

    def make_gamma_units():
        ui = [0]
        for (wt, nblk, cw, wi, nm) in ((fw1T, FC, D, 0, "g1"), (fw2T, FC, D, 1, "g2"),
                                       (fw3T, 16, FF, 2, "g3")):
            for r in range(nblk):
                def unit(r=r, wt=wt, cw=cw, wi=wi, nm=nm):
                    sl = ui[0] % 2
                    ui[0] += 1
                    b = gbA[:, sl * FH:sl * FH + cw] if cw <= FH else gbA[:, 0:cw]
                    bn = f"gbA{sl}" if cw <= FH else "gbA01"
                    xw = [] if cw <= FH else ["gbA0", "gbA1"]
                    dma("qs", b, wt[r, :, 0:cw], [], [bn, *xw], f"{nm}ld")
                    ew("dve", "tensor_reduce", (gtmp[:, 0:1], b, AX.X, ALU.add, True),
                       [bn], ["gtmp"], f"{nm}red")
                    ew("dve", "tensor_add", (gacc[:, wi:wi + 1], gacc[:, wi:wi + 1], gtmp[:, 0:1]),
                       ["gtmp", "gacc"], ["gacc"], f"{nm}acc")
                gamma_units.append(unit)

    def drip_gamma(n=1):
        for _ in range(n):
            if gamma_units:
                gamma_units.pop(0)()

    def gamma_finish():
        while gamma_units:
            drip_gamma()
        tap("gacc", gacc[:, 0:4], (128, 4), ["gacc"], F32)
        mm(pb[6][0:1, 0:3], colf[:, 0:1], gacc[:, 0:3], True, True,
           ["colf", "gacc"], ["pb6"], "gsum")
        ew("dve", "tensor_copy", (scrf[0:1, 0:3], pb[6][0:1, 0:3]), ["pb6"], ["scrf"])
        ew("dve", "tensor_scalar",
           (scrf[0:1, 4:7], scrf[0:1, 0:3], 0.5 / NFF, 0.5e-8, ALU.mult, ALU.add),
           ["scrf"], ["scrf"], "gthr")
        tap("scrf", scrf[0:1, 0:8], (1, 8), ["scrf"], F32)
        for wi in range(3):
            mm(pb[6][:, 0:1], r4f[0:1, :], scrf[0:1, 4 + wi:5 + wi], True, True,
               ["r4f", "scrf"], ["pb6"])
            ew("dve", "tensor_copy", (thr_sb[:, wi:wi + 1], pb[6][:, 0:1]),
               ["pb6"], ["thr_sb"])
        ew("dve", "tensor_scalar", (thr_sb[:, 4:7], thr_sb[:, 0:3], -1.0, None, ALU.mult),
           ["thr_sb"], ["thr_sb"])
        tap("thr", thr_sb[:, 0:8], (128, 8), ["thr_sb"], F32)

    dma("qs", id_sb[:, :], identc[:, :], [], ["id_sb"])
    dma("qs", mut_sb[:, :], maskutc[:, :], [], ["mut_sb"])
    dma("qs", tri_sb[:, :], triu16c[:, :], [], ["tri_sb"])
    dma("qs", colf[:, :], colsfc[:, :], [], ["colf"])
    dma("qs", colb[:, :], colsbc[:, :], [], ["colb"])
    dma("qs", r4f[:, :], rows4fc[:, :], [], ["r4f"])
    dma("qs", acc_sb[:, :], actcc[:, :], [], ["actc"])
    dma("qs", r4b[:, :], rows4bc[:, :], [], ["r4b"])
    dma("qs", ln1_sb[:, :], ln1_d[:, :], [], ["ln1_sb"])
    dma("qs", ln2_sb[:, :], ln2_d[:, :], [], ["ln2_sb"])
    dma("qs", bg_sb[:, :], bgc_d[:, :], [], ["bg_sb"])
    dma("qg", bk2b[:, :], bkg2b_d[:, :], [], ["bk2b"])
    dma("qg", wkg2p_sb[:, :], wkg2p_d[:, :], [], ["wkg2_sb"])
    dma("qg", wkg1_sb[:, :].rearrange("p (dk n) -> p dk n", n=16),
        Wkg1.ap().rearrange("(dk p) n -> p dk n", p=128), [], ["wkg1_sb"])
    ew("dve", "memset", (S_sb[:, :], 0.0), [], ["S_sb"] + [f"S{h}" for h in range(16)])
    ew("dve", "memset", (S_bf[:, :], 0.0), [], [f"Sb{h}" for h in range(16)])
    ew("dve", "memset", (gacc[:, :], 0.0), [], ["gacc"])
    make_gamma_units()

    ZT0, LG0, GK0, A20 = 0, 1024, 2048, 3072
    XFS = 4096
    EV0 = 0
    SQT, SQG, SKT, SKH = 0, 1024, 2048, 3072
    SKHT = 8192
    ETRANS = ["Ekt0", "Ekh0", "Eqt0", "Eqg0", "Ekt1", "Ekh1", "Eqt1", "Eqg1",
              "EkhT0", "EkhT1", "Evw"]

    def rms_xn(xdram, hyp, tag):
        t0 = hyp * HYP
        for pc in range(PC):
            xfo = XFS + (pc % 4) * HYP
            xfn = f"xf{pc % 4}"
            dma("qs", arF[:, xfo:xfo + HYP], xdram[pc * 128:(pc + 1) * 128, t0:t0 + HYP],
                [], [xfn], f"{tag}xld")
            if pc % 2:
                ew("act", "copy", (arA[:, pc * HYP:(pc + 1) * HYP], arF[:, xfo:xfo + HYP]),
                   [xfn, "x1f0", "x1f1"], [f"A{pc}"], f"{tag}xc")
            else:
                ew("dve", "tensor_copy", (arA[:, pc * HYP:(pc + 1) * HYP], arF[:, xfo:xfo + HYP]),
                   [xfn, "x1f0", "x1f1"], [f"A{pc}"], f"{tag}xc")
            act(AF.Square, xsq[pc % 2][:, :], arA[:, pc * HYP:(pc + 1) * HYP],
                [f"A{pc}"], [f"xsq{pc % 2}"], tag=f"{tag}sq")
            mm(pb[4][0:1, 0:512], colb[:, 2:3], xsq[pc % 2][:, :],
               pc == 0, pc == PC - 1, [f"xsq{pc % 2}", "colb"], ["pb4"], f"{tag}s2")
            drip_gamma(1)
        ew("act", "copy", (rrow[0:1, :], pb[4][0:1, 0:512]), ["pb4"], ["rrow"], f"{tag}rc")
        act(AF.Ln, rrow[0:1, :], rrow[0:1, :], ["rrow", "actc"], ["rrow"], bias=acc_sb[0:1, 0:1], tag=f"{tag}ln")
        act(AF.Exp, rrow[0:1, :], rrow[0:1, :], ["rrow"], ["rrow"], scale=-0.5, tag=f"{tag}ex")
        mm(pb[6][:, 0:512], r4f[0:1, :], rrow[0:1, :], True, True,
           ["r4f", "rrow"], ["pb6"], f"{tag}bc")
        ew("dve", "tensor_copy", (rstdb[:, :], pb[6][:, 0:512]), ["pb6"], ["rstdb"], f"{tag}bcc")
        for pc in range(PC):
            sl = arA[:, pc * HYP:(pc + 1) * HYP]
            ew("dve", "scalar_tensor_tensor",
               (sl, sl, ln1_sb[:, pc:pc + 1], rstdb[:, :], ALU.mult, ALU.mult),
               [f"A{pc}", "ln1_sb", "rstdb"], [f"A{pc}"], f"{tag}xn")

    def proj_fm(wdram, cache, dout, first, evac, tag):
        for oc in range(dout // 128):
            ws = wsm[oc % 4]
            wn = f"wsm{oc % 4}"
            stream_w(first, wdram[oc, :, :], cache[oc, :, :], ws[:, :], wn, f"{tag}{oc}")
            for dk in range(PC):
                mm(pb[oc % 4][:, 0:512], ws[:, dk * 128:(dk + 1) * 128],
                   arA[:, dk * HYP:(dk + 1) * HYP],
                   dk == 0, dk == PC - 1, [wn, f"A{dk}"], [f"pb{oc % 4}"], f"{tag}mm")
            evac(pb[oc % 4][:, 0:512], f"pb{oc % 4}", oc)
            drip_gamma(1)

    def proj_vtm(first, tag):
        for js in range(4):
            for hf in range(2):
                dk0 = hf * 8
                stream_w(first, Wv[js, :, hf * 4096:(hf + 1) * 4096],
                         wvc[js, :, hf * 4096:(hf + 1) * 4096],
                         arE[:, EV0 + hf * 4096:EV0 + (hf + 1) * 4096], f"Evw{hf}",
                         f"{tag}{js}_{hf}", extra_writes=ETRANS[:8])
            for tc in range(TC):
                bk = tc % 4
                for dk in range(PC):
                    mm(pb[bk][:, 0:512],
                       arA[:, dk * HYP + tc * 128: dk * HYP + (tc + 1) * 128],
                       arE[:, EV0 + dk * 512:EV0 + (dk + 1) * 512],
                       dk == 0, dk == PC - 1, [f"Evw{dk // 8}", f"A{dk}"], [f"pb{bk}"], f"{tag}mm")
                ew("act" if tc % 2 else "dve", "copy" if tc % 2 else "tensor_copy",
                   (arC[:, tc * D + js * 512: tc * D + (js + 1) * 512], pb[bk][:, 0:512]),
                   [f"pb{bk}"], [f"V{tc}"], f"{tag}ev")

    def proj_z1(tag):
        for dk in range(PC):
            mm(pb[2][0:16, 0:512], wkg1_sb[:, dk * 16:(dk + 1) * 16],
               arA[:, dk * HYP:(dk + 1) * HYP],
               dk == 0, dk == PC - 1, ["wkg1_sb", f"A{dk}"], ["pb2"], f"{tag}mm")
        ew("dve", "tensor_copy", (z1b[0:16, :], pb[2][0:16, 0:512]), ["pb2"], ["z1b"], f"{tag}ev")

    def scan_prep(tc, full, tag):
        sl = tc % 2
        zt = arF[:, ZT0:ZT0 + 1024]
        lg = arF[:, LG0:LG0 + 1024]
        gk = arF[:, GK0:GK0 + 1024]
        for j in range(2):
            mm(pb[3 + j][:, 0:512], z1b[0:16, tc * 128:(tc + 1) * 128],
               wkg2p_sb[0:16, j * 512:(j + 1) * 512], True, True,
               ["z1b", "wkg2_sb"], [f"pb{3 + j}"], f"{tag}zm")
            ew("dve", "tensor_add", (arF[:, ZT0 + j * 512:ZT0 + (j + 1) * 512],
                                     pb[3 + j][:, 0:512], bk2b[:, j * 512:(j + 1) * 512]),
               [f"pb{3 + j}", "bk2b"], ["Fzt"], f"{tag}ze")
        act(AF.Abs, lg, zt, ["Fzt"], ["Flg"], tag=f"{tag}la")
        act(AF.Exp, lg, lg, ["Flg"], ["Flg"], scale=-1.0, tag=f"{tag}le")
        act(AF.Ln, lg, lg, ["Flg", "actc"], ["Flg"], bias=acc_sb[:, 1:2], tag=f"{tag}ll")
        ew("dve", "tensor_scalar", (gk, zt, 0.0, None, ALU.min), ["Fzt"], ["Fgk"], f"{tag}lm")
        ew("dve", "tensor_sub", (gk, gk, lg), ["Fgk", "Flg"], ["Fgk"], f"{tag}ls")
        for p8 in range(QPC):
            bk = 3 + (p8 % 2)
            mm(pb[bk][:, 0:128], arF[:, GK0 + p8 * 128:GK0 + (p8 + 1) * 128], tri_sb[:, :],
               True, True, ["Fgk", "tri_sb"], [f"pb{bk}"], f"{tag}Am")
            ew("dve", "tensor_copy", (arF[:, A20 + p8 * 128:A20 + (p8 + 1) * 128],
                                      pb[bk][:, 0:128]),
               [f"pb{bk}"], ["FA"], f"{tag}Ae")
        a63 = colap(arF, A20 + 63, 128, QPC)
        a127 = colap(arF, A20 + 127, 128, QPC)
        ew("dve", "tensor_copy", (amid[:, :], a63), ["FA"], ["amid"], f"{tag}am")
        act(AF.Exp, dcol[:, :], a127, ["FA"], ["dcol"], tag=f"{tag}de")
        if full:
            ew("dve", "tensor_scalar", (namid[:, :], amid[:, :], -1.0, None, ALU.mult),
               ["amid"], ["namid"], f"{tag}nm")
        kt0, kh0 = SKT + sl * 4096, SKH + sl * 4096
        qt0, qg0 = SQT + sl * 4096, SQG + sl * 4096
        ktn, khn, qtn, qgn = f"Ekt{sl}", f"Ekh{sl}", f"Eqt{sl}", f"Eqg{sl}"
        for p8 in range(QPC):
            asl = arF[:, A20 + p8 * 128:A20 + (p8 + 1) * 128]
            act(AF.Exp, scrb[:, p8 * 128:(p8 + 1) * 128], asl, ["FA", "amid"], ["scrb"],
                bias=amid[:, p8:p8 + 1], scale=-1.0, tag=f"{tag}Dk")
            act(AF.Exp, P_sb[:, p8 * 128:(p8 + 1) * 128], asl,
                ["FA"], ["Dk2", f"P{p8}"],
                bias=colap(arF, A20 + p8 * 128 + 127, 128, 1), scale=-1.0, tag=f"{tag}D2")
        kap = ap3(arB, 4096 + tc * 128, [(HYP, QPC), (1, 128)])
        dkap = ap3(scrb, 0, [(128, QPC), (1, 128)])
        dk2ap = ap3(P_sb, 0, [(128, QPC), (1, 128)])
        ktap = ap3(arE, kt0, [(128, QPC), (1, 128)])
        khap = ap3(arE, kh0, [(128, QPC), (1, 128)])
        ew("dve", "tensor_mul", (ktap, kap, dkap),
           [f"Bk{p8}" for p8 in range(QPC)] + ["scrb"], [ktn, "Evw0", "Evw1"], f"{tag}kt")
        ew("dve", "tensor_mul", (khap, kap, dk2ap),
           [f"Bk{p8}" for p8 in range(QPC)] + ["Dk2"], [khn, "Evw0", "Evw1"], f"{tag}kh")
        if full:
            for p8 in range(QPC):
                asl = arF[:, A20 + p8 * 128:A20 + (p8 + 1) * 128]
                act(AF.Exp, scrb[:, 1024 + p8 * 128:1024 + (p8 + 1) * 128], asl,
                    ["FA", "namid"], ["sDq"],
                    bias=namid[:, p8:p8 + 1], scale=1.0, tag=f"{tag}Dq")
                act(AF.Exp, o_sb[:, p8 * 128:(p8 + 1) * 128], asl,
                    ["FA"], ["Dg", f"ob{p8 // 4}"], scale=1.0, tag=f"{tag}Dg")
            qap = ap3(arB, tc * 128, [(HYP, QPC), (1, 128)])
            dqap = ap3(scrb, 1024, [(128, QPC), (1, 128)])
            dgap = ap3(o_sb, 0, [(128, QPC), (1, 128)])
            qtap = ap3(arE, qt0, [(128, QPC), (1, 128)])
            qgap = ap3(arE, qg0, [(128, QPC), (1, 128)])
            ew("dve", "tensor_mul", (qtap, qap, dqap),
               [f"Bq{p8}" for p8 in range(QPC)] + ["sDq"], [qtn, "Evw0", "Evw1"], f"{tag}qt")
            ew("dve", "tensor_mul", (qgap, qap, dgap),
               [f"Bq{p8}" for p8 in range(QPC)] + ["Dg"], [qgn, "Evw0", "Evw1"], f"{tag}qg")

    def scan_chunk(tc, full, tag):
        sl = tc % 2
        kt0, kh0 = SKT + sl * 4096, SKH + sl * 4096
        qt0, qg0 = SQT + sl * 4096, SQG + sl * 4096
        kht0 = SKHT + sl * 1024
        ktn, khn, qtn, qgn = f"Ekt{sl}", f"Ekh{sl}", f"Eqt{sl}", f"Eqg{sl}"
        khtn = f"EkhT{sl}"
        for p8 in range(QPC):
            dmaT(arE[:, kht0 + p8 * 128:kht0 + (p8 + 1) * 128],
                 arE[:, kh0 + p8 * 128:kh0 + (p8 + 1) * 128],
                 [khn], [khtn], f"{tag}T")
        if stage == "T1":
            return
        for g in range(4):
            ob = g % 2
            for h in range(4 * g, 4 * g + 4):
                p8, pbs = h // 2, (h % 2) * 64
                if full:
                    mm(pb[5][:, (h % 4) * 128:(h % 4) * 128 + 128],
                       arE[pbs:pbs + 64, kt0 + p8 * 128:kt0 + (p8 + 1) * 128],
                       arE[pbs:pbs + 64, qt0 + p8 * 128:qt0 + (p8 + 1) * 128],
                       True, True, [ktn, qtn], ["pb5"], f"{tag}sc")
                    ew("dve", "tensor_mul", (P_sb[:, h * 128:(h + 1) * 128],
                                             pb[5][:, (h % 4) * 128:(h % 4) * 128 + 128],
                                             mut_sb[:, :]),
                       ["pb5", "mut_sb"], [f"P{h}"], f"{tag}mk")
                    mm(pb[ob][:, (h % 4) * 128:(h % 4) * 128 + 128],
                       arC[:, tc * D + h * HD: tc * D + (h + 1) * HD],
                       P_sb[:, h * 128:(h + 1) * 128],
                       True, False, [f"V{tc}", f"P{h}"], [f"pb{ob}"], f"{tag}ov")
                    mm(pb[ob][:, (h % 4) * 128:(h % 4) * 128 + 128],
                       S_bf[pbs:pbs + 64, (h // 2) * 128:(h // 2 + 1) * 128],
                       arE[pbs:pbs + 64, qg0 + p8 * 128:qg0 + (p8 + 1) * 128],
                       False, True, [f"Sb{h}", qgn], [f"pb{ob}"], f"{tag}oi")
                upb = pb[2][pbs:pbs + 64, ((h // 2) % 4) * 128:((h // 2) % 4) * 128 + 128]
                mm(upb, arE[:, kht0 + h * 64:kht0 + (h + 1) * 64],
                   arC[:, tc * D + h * HD: tc * D + (h + 1) * HD],
                   True, True, [khtn, f"V{tc}"], ["pb2"], f"{tag}U")
                if stage == "U1":
                    continue
                Ssl = S_sb[pbs:pbs + 64, (h // 2) * 128:(h // 2 + 1) * 128]
                ew("dve", "scalar_tensor_tensor",
                   (Ssl, Ssl, dcol[pbs:pbs + 64, p8:p8 + 1], upb, ALU.mult, ALU.add),
                   [f"S{h}", "dcol", "pb2"], [f"S{h}"], f"{tag}Su")
                if stage != "Sup1":
                    ew("dve", "tensor_copy",
                       (S_bf[pbs:pbs + 64, (h // 2) * 128:(h // 2 + 1) * 128], Ssl),
                       [f"S{h}"], [f"Sb{h}"], f"{tag}Sb")
            if full:
                ew("dve", "tensor_copy", (o_sb[:, g * 512:(g + 1) * 512], pb[ob][:, 0:512]),
                   [f"pb{ob}"], [f"ob{g}"], f"{tag}oe")
        if full:
            head_ln(tc, tag)

    def head_ln(tc, tag):
        for j in range(2):
            act(AF.Square, P_sb[:, j * 1024:(j + 1) * 1024], o_sb[:, j * 1024:(j + 1) * 1024],
                [f"ob{2 * j}", f"ob{2 * j + 1}"],
                [f"Q{j}"] + [f"P{h}" for h in range(8 * j, 8 * j + 8)],
                tag=f"{tag}q2")
        for j4 in range(4):
            bk, pp = 3 + j4 // 2, (j4 % 2) * 32
            mm(pb[bk][pp:pp + 1, 0:512], colb[:, 1:2], o_sb[:, j4 * 512:(j4 + 1) * 512],
               True, True, [f"ob{j4}", "colb"], [f"pb{bk}"], f"{tag}mu")
            bk2 = 5 + j4 // 2
            mm(pb[bk2][pp:pp + 1, 0:512], colb[:, 1:2], P_sb[:, j4 * 512:(j4 + 1) * 512],
               True, True, [f"Q{j4 // 2}", "colb"], [f"pb{bk2}"], f"{tag}m2")
        for i, bk in enumerate((3, 4)):
            ew("dve", "tensor_copy", (scrb[0:64, i * 512:(i + 1) * 512], pb[bk][0:64, 0:512]),
               [f"pb{bk}"], ["scrb"], f"{tag}e1")
        for i, bk in enumerate((5, 6)):
            ew("dve", "tensor_copy", (scrb[0:64, 1024 + i * 512:1024 + (i + 1) * 512],
                                      pb[bk][0:64, 0:512]),
               [f"pb{bk}"], ["scrb"], f"{tag}e2")
        for j4 in range(4):
            i, pp = j4 // 2, (j4 % 2) * 32
            lhs = r4b[pp:pp + 1, :]
            mm(pb[3][:, 0:512], lhs, scrb[pp:pp + 1, i * 512:(i + 1) * 512], True, True,
               ["scrb", "r4b"], ["pb3"], f"{tag}bm")
            mm(pb[4][:, 0:512], lhs, scrb[pp:pp + 1, 1024 + i * 512:1024 + (i + 1) * 512],
               True, True, ["scrb", "r4b"], ["pb4"], f"{tag}b2")
            act(AF.Square, sgt[0][:, :], pb[3][:, 0:512], ["pb3"], ["sgt0"], tag=f"{tag}s")
            ew("dve", "tensor_sub", (scrf[:, 0:512], pb[4][:, 0:512], sgt[0][:, :]),
               ["pb4", "sgt0"], ["scrf"], f"{tag}v")
            act(AF.Ln, scrf[:, 0:512], scrf[:, 0:512], ["scrf", "actc"], ["scrf"], bias=acc_sb[:, 0:1], tag=f"{tag}vl")
            act(AF.Exp, sgt[1][:, :], scrf[:, 0:512], ["scrf"], ["sgt1"], scale=-0.5, tag=f"{tag}ve")
            ew("dve", "tensor_sub", (o_sb[:, j4 * 512:(j4 + 1) * 512],
                                     o_sb[:, j4 * 512:(j4 + 1) * 512], pb[3][:, 0:512]),
               [f"ob{j4}", "pb3"], [f"ob{j4}"], f"{tag}t1")
            dsl = ap3(arD, 4 * j4 * HYP + tc * 128, [(HYP, 4), (1, 128)])
            i0 = ap3(o_sb, j4 * 512, [(128, 4), (1, 128)])
            i1 = ap3(sgt[1], 0, [(128, 4), (1, 128)])
            ew("dve", "tensor_mul", (dsl, i0, i1),
               [f"ob{j4}", "sgt1"],
               [f"on{4 * j4}", f"on{4 * j4 + 1}", f"on{4 * j4 + 2}", f"on{4 * j4 + 3}"],
               f"{tag}on")

    def p1_hyper(xdram, hyp, full, first, tag):
        rms_xn(xdram, hyp, f"{tag}r")
        if stage == "rms":
            return
        tap(f"xn_{tag}", arA[:, 0:2048], (128, 2048), ["A0", "A1", "A2", "A3"])
        if full:
            def evq(ps, pn, oc):
                if oc % 2:
                    ew("act", "copy", (arB[:, oc * HYP:(oc + 1) * HYP], ps),
                       [pn], [f"Bq{oc}"], f"{tag}qe")
                else:
                    ew("dve", "tensor_copy", (arB[:, oc * HYP:(oc + 1) * HYP], ps),
                       [pn], [f"Bq{oc}"], f"{tag}qe")
            proj_fm(Wq, wqc, 1024, first, evq, f"{tag}q")
        proj_vtm(first, f"{tag}v")

        def evk(ps, pn, oc):
            if oc % 2:
                ew("act", "mul", (arB[:, 4096 + oc * HYP:4096 + (oc + 1) * HYP], ps, KSCALE),
                   [pn], [f"Bk{oc}"], f"{tag}ke")
            else:
                ew("dve", "tensor_scalar",
                   (arB[:, 4096 + oc * HYP:4096 + (oc + 1) * HYP], ps, KSCALE, None, ALU.mult),
                   [pn], [f"Bk{oc}"], f"{tag}ke")
        proj_fm(Wk, wkc, 1024, first, evk, f"{tag}k")
        proj_z1(f"{tag}z")
        if stage == "proj":
            return
        ntc = 1 if stage in ("prep1", "chunk1") else TC
        for tc in range(ntc):
            scan_prep(tc, full, f"{tag}p{tc}_")
            if stage == "prep1":
                continue
            scan_chunk(tc, full, f"{tag}s{tc}_")
        if full:
            def evg(ps, pn, oc):
                slg = oc % 2
                act(AF.Silu, sgt[slg][:, :], ps, [pn, "bg_sb"], [f"sgt{slg}"],
                    bias=bg_sb[:, oc:oc + 1], tag=f"{tag}gs")
                dsl = arD[:, oc * HYP:(oc + 1) * HYP]
                ew("dve", "tensor_mul", (dsl, dsl, sgt[slg][:, :]),
                   [f"on{oc}", f"sgt{slg}"], [f"on{oc}"], f"{tag}gm")
            proj_fm(Wg, wgc, 2048, first, evg, f"{tag}g")
            tap(f"og_{tag}", arD[:, 0:PC * HYP], (128, PC * HYP), [f"on{i}" for i in range(PC)])

    def layer_norm(srcfn, dstfn, tag, store=None):
        for oc in range(PC):
            sap, sname = srcfn(oc)
            act(AF.Square, xsq[oc % 2][:, :], sap, [sname], [f"xsq{oc % 2}"], tag=f"{tag}q")
            mm(pb[4][0:1, 0:512], colb[:, 2:3], sap, oc == 0, oc == PC - 1,
               [sname, "colb"], ["pb4"], f"{tag}m")
            mm(pb[5][0:1, 0:512], colb[:, 2:3], xsq[oc % 2][:, :], oc == 0, oc == PC - 1,
               [f"xsq{oc % 2}", "colb"], ["pb5"], f"{tag}2")
        ew("dve", "tensor_copy", (scrf[0:1, 0:512], pb[4][0:1, 0:512]), ["pb4"], ["scrf"], f"{tag}c")
        ew("dve", "tensor_copy", (scrf[0:1, 512:1024], pb[5][0:1, 0:512]), ["pb5"], ["scrf"], f"{tag}c2")
        ew("dve", "tensor_mul", (scrb[0:1, 0:512], scrf[0:1, 0:512], scrf[0:1, 0:512]),
           ["scrf"], ["scrb"], f"{tag}s")
        ew("dve", "tensor_sub", (scrf[0:1, 512:1024], scrf[0:1, 512:1024], scrb[0:1, 0:512]),
           ["scrf", "scrb"], ["scrf"], f"{tag}v")
        act(AF.Ln, scrf[0:1, 512:1024], scrf[0:1, 512:1024], ["scrf", "actc"], ["scrf"], bias=acc_sb[0:1, 0:1], tag=f"{tag}l")
        act(AF.Exp, scrf[0:1, 512:1024], scrf[0:1, 512:1024], ["scrf"], ["scrf"], scale=-0.5, tag=f"{tag}e")
        mm(pb[6][:, 0:512], r4f[0:1, :], scrf[0:1, 0:512], True, True,
           ["scrf", "r4f"], ["pb6"], f"{tag}bm")
        ew("dve", "tensor_copy", (mubf[:, :], pb[6][:, 0:512]), ["pb6"], ["mubf"], f"{tag}bmc")
        mm(pb[6][:, 0:512], r4f[0:1, :], scrf[0:1, 512:1024], True, True,
           ["scrf", "r4f"], ["pb6"], f"{tag}br")
        ew("dve", "tensor_copy", (rstdb[:, :], pb[6][:, 0:512]), ["pb6"], ["rstdb"], f"{tag}brc")
        for oc in range(PC):
            sap, sname = srcfn(oc)
            dap, dname, xw = dstfn(oc)
            ew("dve", "tensor_sub", (dap, sap, mubf[:, :]), [sname, "mubf"], [dname, *xw], f"{tag}a1")
            ew("dve", "tensor_mul", (dap, dap, rstdb[:, :]), [dname, "rstdb"], [dname], f"{tag}a2")
            if store is not None:
                (oap,) = store(oc)
                dma("qs", oap, dap, [dname, *xw], [], f"{tag}st")

    def hh_ap(hc, name):
        g0 = hc * HYP
        other = "hh%d" % hc if name == "fh" else "fh%d" % hc
        if hc < 16:
            return arC[:, g0:g0 + HYP], f"{name}{hc}", [f"V{hc // 4}", other]
        if hc < 32:
            return (arD[:, g0 - 16 * HYP:g0 - 16 * HYP + HYP], f"{name}{hc}",
                    [f"on{hc - 16}", other])
        return (arE[:, g0 - 32 * HYP:g0 - 32 * HYP + HYP], f"{name}{hc}",
                ETRANS[:8] + [other])

    def p2_hyper(hyp, first, tag):
        t0 = hyp * HYP
        for oc in range(PC):
            ws = wsm[oc % 4]
            wn = f"wsm{oc % 4}"
            stream_w(first, Wo[oc, :, :], woc[oc, :, :], ws[:, :], wn, f"{tag}o{oc}")
            xfo = XFS + (oc % 2) * HYP
            xfn = f"xf{oc % 2}"
            dma("qs", arF[:, xfo:xfo + HYP], xm[oc * 128:(oc + 1) * 128, t0:t0 + HYP],
                [], [xfn], f"{tag}ox")
            for dk in range(PC):
                mm(pb[oc % 4][:, 0:512], ws[:, dk * 128:(dk + 1) * 128],
                   arD[:, dk * HYP:(dk + 1) * HYP],
                   dk == 0, dk == PC - 1, [wn, f"on{dk}"], [f"pb{oc % 4}"], f"{tag}om")
            x1o = XFS + 1024 + (oc % 2) * HYP
            x1n = f"x1f{oc % 2}"
            ew("dve", "tensor_add", (arF[:, x1o:x1o + HYP], arF[:, xfo:xfo + HYP],
                                     pb[oc % 4][:, 0:512]),
               [xfn, f"pb{oc % 4}"], [x1n], f"{tag}x1")
            bxw = [f"Bq{oc}"] if oc < 8 else [f"Bk{oc - 8}"]
            ew("dve", "tensor_add", (arB[:, oc * HYP:(oc + 1) * HYP],
                                     arF[:, x1o:x1o + HYP], arF[:, xfo:xfo + HYP]),
               [x1n, xfn], [f"Bx{oc}", *bxw], f"{tag}xx")
            ew("act", "copy", (arA[:, oc * HYP:(oc + 1) * HYP], arF[:, x1o:x1o + HYP]),
               [x1n], [f"A{oc}"], f"{tag}xb")
            act(AF.Square, xsq[oc % 2][:, :], arA[:, oc * HYP:(oc + 1) * HYP],
                [f"A{oc}"], [f"xsq{oc % 2}"], tag=f"{tag}rq")
            mm(pb[4][0:1, 0:512], colb[:, 2:3], xsq[oc % 2][:, :], oc == 0, oc == PC - 1,
               [f"xsq{oc % 2}", "colb"], ["pb4"], f"{tag}rs")
        ew("act", "copy", (rrow[0:1, :], pb[4][0:1, 0:512]), ["pb4"], ["rrow"], f"{tag}rc")
        act(AF.Ln, rrow[0:1, :], rrow[0:1, :], ["rrow", "actc"], ["rrow"], bias=acc_sb[0:1, 0:1], tag=f"{tag}rl")
        act(AF.Exp, rrow[0:1, :], rrow[0:1, :], ["rrow"], ["rrow"], scale=-0.5, tag=f"{tag}re")
        mm(pb[6][:, 0:512], r4f[0:1, :], rrow[0:1, :], True, True, ["r4f", "rrow"], ["pb6"], f"{tag}rb")
        ew("dve", "tensor_copy", (rstdb[:, :], pb[6][:, 0:512]), ["pb6"], ["rstdb"], f"{tag}rbc")
        for oc in range(PC):
            sl = arA[:, oc * HYP:(oc + 1) * HYP]
            ew("dve", "scalar_tensor_tensor",
               (sl, sl, ln2_sb[:, oc:oc + 1], rstdb[:, :], ALU.mult, ALU.mult),
               [f"A{oc}", "ln2_sb", "rstdb"], [f"A{oc}"], f"{tag}xn2")
        tap(f"xn2_{tag}", arA[:, 0:PC * HYP], (128, PC * HYP), [f"A{i}" for i in range(PC)])

        for hc in range(FC):
            ws = wsm[hc % 2]
            wn = f"wsm{hc % 2}"
            stream_w(first, W1[hc, :, :], w1c[hc, :, :], ws[:, :], wn, f"{tag}w1_{hc}")
            for dk in range(PC):
                mm(pb[hc % 2][:, 0:512], ws[:, dk * 128:(dk + 1) * 128],
                   arA[:, dk * HYP:(dk + 1) * HYP],
                   dk == 0, dk == PC - 1, [wn, f"A{dk}"], [f"pb{hc % 2}"], f"{tag}1m")
            act(AF.Silu, s1c[hc % 2][:, :], pb[hc % 2][:, 0:512], [f"pb{hc % 2}"],
                [f"s1c{hc % 2}"], tag=f"{tag}si")
            w3s = 2 + hc % 2
            stream_w(first, W3[hc, :, :], w3c[hc, :, :], wsm[w3s][:, :], f"wsm{w3s}", f"{tag}w3_{hc}")
            for dk in range(PC):
                mm(pb[2 + hc % 2][:, 0:512], wsm[w3s][:, dk * 128:(dk + 1) * 128],
                   arA[:, dk * HYP:(dk + 1) * HYP],
                   dk == 0, dk == PC - 1, [f"wsm{w3s}", f"A{dk}"], [f"pb{2 + hc % 2}"], f"{tag}3m")
            hap, hname, xw = hh_ap(hc, "hh")
            ew("dve", "tensor_mul", (hap, pb[2 + hc % 2][:, 0:512], s1c[hc % 2][:, :]),
               [f"pb{2 + hc % 2}", f"s1c{hc % 2}"], [hname, *xw], f"{tag}hh")

        for oc in range(PC):
            for half in range(2):
                hsl = (oc * 2 + half) % 2
                w2b = arA[:, hsl * FH:(hsl + 1) * FH]
                w2n = f"Aw2{hsl}"
                affected = [f"A{k}" for k in range(hsl * 5, hsl * 5 + 6)]
                stream_w(first, W2[oc, :, half * FH:(half + 1) * FH],
                         w2c[oc, :, half * FH:(half + 1) * FH],
                         w2b, w2n, f"{tag}w2_{oc}_{half}", extra_writes=affected)
                for dkh in range(22):
                    hap, hname, _ = hh_ap(half * 22 + dkh, "hh")
                    mm(pb[oc % 4][:, 0:512], w2b[:, dkh * 128:(dkh + 1) * 128],
                       hap, half == 0 and dkh == 0, half == 1 and dkh == 21,
                       [w2n, hname], [f"pb{oc % 4}"], f"{tag}2m")
            sl = arB[:, oc * HYP:(oc + 1) * HYP]
            ew("dve", "tensor_add", (sl, sl, pb[oc % 4][:, 0:512]),
               [f"Bx{oc}", f"pb{oc % 4}"], [f"Bx{oc}"], f"{tag}s")

        layer_norm(lambda oc: (arB[:, oc * HYP:(oc + 1) * HYP], f"Bx{oc}"),
                   lambda oc: (arA[:, oc * HYP:(oc + 1) * HYP], f"A{oc}",
                               ["Aw20", "Aw21"] if oc < 12 else []),
                   f"{tag}L1")
        tap(f"out1_{tag}", arA[:, 0:PC * HYP], (128, PC * HYP), [f"A{i}" for i in range(PC)])

        def quant(dst, dstn, srcap, srcn, wi, tmp, tmpn, tg, xw=()):
            ew("dve", "tensor_scalar", (tmp, srcap, thr_sb[:, 4 + wi:5 + wi], None, ALU.is_le),
               [srcn, "thr_sb"], [tmpn if isinstance(tmpn, str) else tmpn[0], *(() if isinstance(tmpn, str) else tmpn[1:])], f"{tg}l")
            ew("dve", "scalar_tensor_tensor", (dst, srcap, thr_sb[:, wi:wi + 1], tmp,
                                               ALU.is_ge, ALU.subtract),
               [srcn, "thr_sb", tmpn if isinstance(tmpn, str) else tmpn[0]], [dstn, *xw], f"{tg}g")

        for hc in range(FC):
            hs = hc % 2
            qsl = arB[:, hs * 2048:hs * 2048 + 2048]
            qn = f"Bqw{hs}"
            bxq = [f"Bx{4 * hs}", f"Bx{4 * hs + 1}", f"Bx{4 * hs + 2}", f"Bx{4 * hs + 3}"]
            tmp = arB[:, 4096:6144]
            if first:
                fo = hs * 2048
                fn_ = f"gbA{hs}"
                dma("qs", gbA[:, fo:fo + 2048], fw1T[hc, :, :], [], [fn_, "gbA0", "gbA1"], f"{tag}f1l")
                quant(qsl, qn, gbA[:, fo:fo + 2048], fn_, 0,
                      tmp, ["Btmp", "Bx8", "Bx9", "Bx10", "Bx11"], f"{tag}q1_{hc}", xw=bxq)
                dma("qs", qw1s[hc, :, :], qsl, [qn], [f"c1_{hc}"], f"{tag}c1w")
            else:
                dma("qs", qsl, qw1s[hc, :, :], [f"c1_{hc}"], [qn, *bxq], f"{tag}c1r")
            if hc == 0:
                tap(f"qw0_{tag}", arB[:, 0:2048], (128, 2048), [qn])
            for dk in range(PC):
                mm(pb[hs][:, 0:512], qsl[:, dk * 128:(dk + 1) * 128],
                   arA[:, dk * HYP:(dk + 1) * HYP],
                   dk == 0, dk == PC - 1, [qn, f"A{dk}"], [f"pb{hs}"], f"{tag}f1m")
            act(AF.Sigmoid, s1c[hs][:, :], pb[hs][:, 0:512], [f"pb{hs}"], [f"s1c{hs}"], tag=f"{tag}sg")
            if first:
                fo = hs * 2048
                fn_ = f"gbA{hs}"
                dma("qs", gbA[:, fo:fo + 2048], fw2T[hc, :, :], [], [fn_, "gbA0", "gbA1"], f"{tag}f2l")
                quant(qsl, qn, gbA[:, fo:fo + 2048], fn_, 1, tmp, "Btmp", f"{tag}q2_{hc}")
                dma("qs", qw2s[hc, :, :], qsl, [qn], [f"c2_{hc}"], f"{tag}c2w")
            else:
                dma("qs", qsl, qw2s[hc, :, :], [f"c2_{hc}"], [qn], f"{tag}c2r")
            for dk in range(PC):
                mm(pb[2 + hs][:, 0:512], qsl[:, dk * 128:(dk + 1) * 128],
                   arA[:, dk * HYP:(dk + 1) * HYP],
                   dk == 0, dk == PC - 1, [qn, f"A{dk}"], [f"pb{2 + hs}"], f"{tag}f2m")
            hap, hname, xw = hh_ap(hc, "fh")
            ew("dve", "tensor_mul", (hap, pb[2 + hs][:, 0:512], s1c[hs][:, :]),
               [f"pb{2 + hs}", f"s1c{hs}"], [hname, *xw], f"{tag}fh")
        tap(f"fh_{tag}", arC[:, 0:2048], (128, 2048), ["fh0", "fh1", "fh2", "fh3"])
        tap(f"qw1_{tag}", arB[:, 0:2048], (128, 2048), ["Bqw0", "Bqw1"])
        for oc in range(PC):
            for half in range(2):
                hsl = (oc * 2 + half) % 2
                q3 = arB[:, hsl * FH:(hsl + 1) * FH]
                qn3 = f"Bq3{hsl}"
                bx3 = [f"Bx{k}" for k in range(hsl * 5, hsl * 5 + 6)] + \
                      [f"Bqw{k}" for k in range(2)] + ["Btmp"]
                if first:
                    gsl = gbA[:, half * FH:(half + 1) * FH]
                    gn = f"gbA{half}"
                    dma("qs", gsl, fw3T[oc, :, half * FH:(half + 1) * FH],
                        [], [gn], f"{tag}f3l")
                    for piece in range(2):
                        quant(q3[:, piece * 1408:(piece + 1) * 1408], qn3,
                              gsl[:, piece * 1408:(piece + 1) * 1408], gn, 2,
                              scrb[:, 0:1408], "scrb", f"{tag}q3_{oc}_{half}_{piece}",
                              xw=bx3 if piece == 0 else ())
                    dma("qs", qw3s[oc, :, half * FH:(half + 1) * FH], q3, [qn3],
                        [f"c3_{oc}_{half}"], f"{tag}c3w")
                else:
                    dma("qs", q3, qw3s[oc, :, half * FH:(half + 1) * FH],
                        [f"c3_{oc}_{half}"], [qn3, *bx3], f"{tag}c3r")
                for dkh in range(22):
                    hap, hname, _ = hh_ap(half * 22 + dkh, "fh")
                    mm(pb[oc % 4][:, 0:512], q3[:, dkh * 128:(dkh + 1) * 128], hap,
                       half == 0 and dkh == 0, half == 1 and dkh == 21,
                       [qn3, hname], [f"pb{oc % 4}"], f"{tag}f3m")
            ew("dve", "tensor_add", (s2b[:, oc * HYP:(oc + 1) * HYP],
                                     arA[:, oc * HYP:(oc + 1) * HYP], pb[oc % 4][:, 0:512]),
               [f"A{oc}", f"pb{oc % 4}"], [f"s2_{oc}"], f"{tag}s2")
            if oc == 0:
                tap(f"q3a_{tag}", arB[:, 0:FH], (128, FH), ["Bq30"])
                tap(f"q3b_{tag}", arB[:, FH:2 * FH], (128, FH), ["Bq31"])
        tap(f"s2_{tag}", s2b[:, 0:2048], (128, 2048), ["s2_0", "s2_1", "s2_2", "s2_3"])
        tap(f"fhC_{tag}", arC[:, 0:8192], (128, 8192), [f"fh{i}" for i in range(16)])
        tap(f"fhD_{tag}", arD[:, 0:8192], (128, 8192), [f"fh{i}" for i in range(16, 32)])
        tap(f"fhE_{tag}", arE[:, 0:6144], (128, 6144), [f"fh{i}" for i in range(32, 44)])
        layer_norm(lambda oc: (s2b[:, oc * HYP:(oc + 1) * HYP], f"s2_{oc}"),
                   lambda oc: (arF[:, (oc % 2) * HYP:(oc % 2) * HYP + HYP],
                               f"os{oc % 2}", ["Fzt", "Flg"]),
                   f"{tag}L2",
                   store=lambda oc: (out[oc * 128:(oc + 1) * 128, t0:t0 + HYP],))

    if not do_gamma:
        gamma_units.clear()
    for hyp in range(n_pre):
        p1_hyper(xp, hyp, False, hyp == 0, f"P{hyp}")
    if do_gamma:
        gamma_finish()
    else:
        gamma_units.clear()
    tap("Spre", S_sb[:, 0:1024], (128, 1024), [f"S{h}" for h in range(16)], F32)
    for hyp in range(n_main):
        p1_hyper(xm, hyp, True, hyp == 0, f"M{hyp}")
        if do_p2:
            p2_hyper(hyp, hyp == 0, f"W{hyp}")
    if n_main == 0:
        dma("qs", out[0:128, 0:512], arF[:, 0:512], ["Fzt"], [], "dummyout")

    P.resolve(serial=serial)
    with nc.Block() as block:
        block.tensor(P.stream("pe", sems))
        block.scalar(P.stream("act", sems))
        block.vector(P.stream("dve", sems))
        block.gpsimd(P.stream("gp", sems))
        block.sync(P.stream("sync", sems))
    ctx.close()
    return nc


def _chunkW(W, nb):
    K, N = W.shape
    H = W.reshape(K // 128, 128, N // 128, 128).transpose(2, 1, 0, 3).reshape(N // 128, 128, K)
    assert H.shape[0] == nb
    return np.ascontiguousarray(H)


def _chunkV(W):
    H = W.reshape(16, 128, 4, 512).transpose(2, 1, 0, 3).reshape(4, 128, 8192)
    return np.ascontiguousarray(H)


def _prep_inputs(inputs):
    i = inputs
    f32 = np.float32
    common = {
        "Wq": _chunkW(np.asarray(i["L0_attn_Wq"], f32), 8),
        "Wk": _chunkW(np.asarray(i["L0_attn_Wk"], f32), 8),
        "Wkg1": np.ascontiguousarray(np.asarray(i["L0_attn_Wkg1"], f32)),
        "bkg2b": np.ascontiguousarray(np.repeat(np.asarray(i["L0_attn_bkg2"], f32).reshape(1, 1024), 128, 0)),
        "Wkg2p": np.ascontiguousarray(np.pad(np.asarray(i["L0_attn_Wkg2"], f32), ((0, 112), (0, 0)))),
        "Wv": _chunkV(np.asarray(i["L0_attn_Wv"], f32)),
        "Wg": _chunkW(np.asarray(i["L0_attn_Wg"], f32), 16),
        "bgc": np.ascontiguousarray(np.asarray(i["L0_attn_bg"], f32).reshape(PC, 128).T),
        "Wo": _chunkW(np.asarray(i["L0_attn_Wo"], f32), 16),
        "ln1c": np.ascontiguousarray(np.asarray(i["L0_attn_ln1_w"], f32).reshape(PC, 128).T),
        "ln2c": np.ascontiguousarray(np.asarray(i["L0_attn_ln2_w"], f32).reshape(PC, 128).T),
        "W1": _chunkW(np.asarray(i["L0_attn_W1"], f32), 44),
        "W3": _chunkW(np.asarray(i["L0_attn_W3"], f32), 44),
        "W2": _chunkW(np.asarray(i["L0_attn_W2"], f32), 16),
        "fw1T": _chunkW(np.ascontiguousarray(np.asarray(i["L0_ffn_w1"], f32).T), 44),
        "fw2T": _chunkW(np.ascontiguousarray(np.asarray(i["L0_ffn_w2"], f32).T), 44),
        "fw3T": _chunkW(np.ascontiguousarray(np.asarray(i["L0_ffn_w3"], f32).T), 16),
    }
    x = np.asarray(i["L0_attn_x"], f32)
    zeros = np.zeros((D, T), f32)
    in_maps = []
    for c in range(8):
        b, h = c // 2, c % 2
        m = dict(common)
        m["xm"] = np.ascontiguousarray(x[b, h * T:(h + 1) * T, :].T)
        m["xp"] = np.ascontiguousarray(x[b, 0:T, :].T) if h == 1 else zeros
        in_maps.append(m)
    return in_maps


def kernel(**inputs):
    in_maps = _prep_inputs(inputs)
    nc = build()
    res = run_bass_kernel_spmd(nc, in_maps, core_ids=list(range(8)))
    outp = np.empty((4, 4096, D), np.float32)
    for c in range(8):
        b, h = c // 2, c % 2
        outp[b, h * T:(h + 1) * T, :] = res.results[c]["out"].T
    return outp
